# version 08
# speedup vs baseline: 1.8389x; 1.0410x over previous
import numpy as np
from contextlib import ExitStack

import ml_dtypes
import concourse.bass as bass
import concourse.mybir as mybir
from concourse.bass_utils import run_bass_kernel_spmd

F32 = mybir.dt.float32
BF16 = mybir.dt.bfloat16
I32 = mybir.dt.int32
AF = mybir.ActivationFunctionType
ALU = mybir.AluOpType
AX = mybir.AxisListType.X

NCORES = 8
T = 2048
D = 2048
NT = T // 128
ND = D // 128
WIN = 1024
RSD = 0.08838834764831845
NEG = -1.0e30


def perm_slice_params(kd):
    if kd < 8:
        return kd * 256, 2
    return (kd - 8) * 256 + 1, 2


_CONSTS = None

def make_consts():
    global _CONSTS
    if _CONSTS is not None:
        return _CONSTS
    ident = np.eye(128, dtype=ml_dtypes.bfloat16)
    Dh = 128
    inv_freq = 1.0 / (10000.0 ** (np.arange(0, Dh, 2, dtype=np.float64) / Dh))
    ang = np.arange(T, dtype=np.float64)[:, None] * inv_freq[None, :]
    cos = np.concatenate([np.cos(ang), np.cos(ang)], axis=1).astype(np.float32)
    sin = np.concatenate([np.sin(ang), np.sin(ang)], axis=1).astype(np.float32)
    sinm = sin.copy()
    sinm[:, :64] = -sinm[:, :64]
    i = np.arange(128)
    causal = np.where(i[:, None] >= i[None, :], 0.0, NEG).astype(np.float32)
    ti = i[:, None]
    tj = np.arange(1152)[None, :] - 1024
    para = np.where((ti - tj >= 0) & (ti - tj <= WIN), 0.0, NEG).astype(np.float32)
    s_ = np.arange(128)
    upS = (s_[:, None] < s_[None, :]).astype(np.float32)
    bdm = ((s_[:, None] // 16) == (s_[None, :] // 16)).astype(np.float32)
    cp2 = np.zeros((128, 896), dtype=np.float32)
    cp2[:, 0:128] = (s_[:, None] <= s_[None, :]).astype(np.float32)
    cp2[:, 128:256] = upS * bdm
    cp2[:, 256:384] = 1.0
    cp2[:, 384:512] = (upS * bdm).T
    cp2[:, 512:640] = upS.T * (1 - bdm)
    cp2[:, 640:768] = np.eye(128, dtype=np.float32)
    identf = np.eye(128, dtype=np.float32)
    _CONSTS = dict(ident=ident, ropec=cos, ropes=sinm, causal=causal, para=para,
                   cp2=cp2, identf=identf)
    return _CONSTS


_WRAP_OPS = {
    "tensor_scalar", "tensor_tensor", "scalar_tensor_tensor", "tensor_copy",
    "tensor_reduce", "reciprocal", "activation", "copy", "add", "mul",
    "memset", "select", "iota", "transpose", "reduce_max", "reduce_sum",
    "tensor_tensor_scan", "copy_predicated", "match_replace", "tensor_add",
    "tensor_mul", "tensor_sub", "tensor_max", "tensor_relu",
}


class EngProxy:
    def __init__(self, b, eng, name):
        self._b = b
        self._eng = eng
        self._name = name

    def __getattr__(self, attr):
        real = getattr(self._eng, attr)
        if attr not in _WRAP_OPS:
            return real
        b, name, eng = self._b, self._name, self._eng

        def f(*a, **k):
            sem = b.sem("ch_" + name)
            n = b.ch.get(name, 0)
            if n > 0:
                eng.wait_ge(sem, n)
            inst = real(*a, **k)
            inst.then_inc(sem, 1)
            b.ch[name] = n + 1
            return inst
        return f


class B:
    ENG = ("sync", "tensor", "vector", "scalar", "gpsimd")

    def __init__(self):
        self.nc = bass.Bass("TRN2", target_bir_lowering=False, debug=False)
        self.stk = ExitStack()
        self.scopes = []
        self.sems = {}
        self.cnt = {}
        self.ch = {}
        self.nbar = 0
        self.block = None

    def push_scope(self):
        self.scopes.append(ExitStack())
        self._scope_maps = getattr(self, "_scope_maps", [])
        self._scope_maps.append({})

    def pop_scope(self):
        self.barrier()
        m = self._scope_maps.pop()
        for n, (h, klass) in m.items():
            self._pools.setdefault(klass, []).append(h)
            for q in getattr(self, "dma_by_q", {}).values():
                q.discard(n)
        self.scopes.pop().close()

    def we(self, eng, name):
        n = self.ch.get(name, 0)
        if n > 0:
            eng.wait_ge(self.sem("ch_" + name), n)

    def chv(self, name):
        return self.ch.get(name, 0)

    def start(self):
        self.block = self.stk.enter_context(self.nc.Block())

    def close(self):
        self.stk.close()

    def _uname(self, name):
        if not hasattr(self, "_names"):
            self._names = {}
        n = self._names.get(name, 0)
        self._names[name] = n + 1
        return name if n == 0 else f"{name}_{n}"

    def sb(self, name, shape, dtype):
        stk = self.scopes[-1] if self.scopes else self.stk
        return stk.enter_context(self.nc.sbuf_tensor(self._uname(name), shape, dtype))

    def ps(self, name, shape, dtype):
        stk = self.scopes[-1] if self.scopes else self.stk
        return stk.enter_context(self.nc.psum_tensor(self._uname(name), shape, dtype))

    def sem(self, name, klass="c"):
        persistent = name == "bar" or name.startswith("ch_")
        maps = getattr(self, "_scope_maps", [])
        if not hasattr(self, "allsems"):
            self.allsems = {}
        if not hasattr(self, "_pools"):
            self._pools = {}
        if persistent or not maps:
            if name not in self.sems:
                self.sems[name] = self.stk.enter_context(self.nc.semaphore(name))
            h = self.sems[name]
        else:
            m = maps[-1]
            if name not in m:
                pool = self._pools.setdefault(klass, [])
                m[name] = (pool.pop() if pool else self.stk.enter_context(self.nc.semaphore(name)), klass)
            h = m[name][0]
        self.allsems[id(h)] = h
        return h

    def _key(self, name):
        return id(self.sem(name))

    def add(self, name, n=1):
        k = self._key(name)
        self.cnt[k] = self.cnt.get(k, 0) + n
        return self.cnt[k]

    def val(self, name):
        maps = getattr(self, "_scope_maps", [])
        persistent = name == "bar" or name.startswith("ch_")
        if not persistent and maps and name not in maps[-1]:
            return 0
        return self.cnt.get(self._key(name), 0)

    def inc(self, inst, name, n=1):
        inst.then_inc(self.sem(name), n)
        return self.add(name, n)

    def w(self, eng, name):
        if self.val(name) > 0:
            eng.wait_ge(self.sem(name), self.val(name))

    def dma(self, eng, queue, out, in_, slot):
        if not hasattr(self, "dma_by_q"):
            self.dma_by_q = {}
        self.dma_by_q.setdefault(queue, set()).add(slot)
        klass = "g" if queue == "gpsimd" else "h"
        h = self.sem(slot, klass)
        if self.val(slot) > 0:
            eng.wait_ge(h, self.val(slot))
        inst = eng.dma_start(out=out, in_=in_)
        inst.then_inc(h, 16)
        self.add(slot, 16)
        return self.val(slot)

    def sec(self, engname):
        dec = getattr(self.block, engname)
        if engname in ("vector", "scalar", "gpsimd"):
            def mydec(f):
                return dec(lambda eng: f(EngProxy(self, eng, engname)))
            return mydec
        return dec

    def barrier(self):
        self.nbar += 1
        tgt = self.val("bar") + 5
        bar = self.sem("bar")
        barkey = self._key("bar")
        waits = [(self.allsems[k], v) for k, v in sorted(self.cnt.items())
                 if v > 0 and k != barkey and k in getattr(self, "allsems", {})]
        chains = [(self.sem("ch_" + n), c) for n, c in sorted(self.ch.items()) if c > 0]
        for e in self.ENG:
            @self.sec(e)
            def _(eng, e=e):
                eng = getattr(eng, "_eng", eng)
                for h, v in waits:
                    eng.wait_ge(h, v)
                for h, c in chains:
                    eng.wait_ge(h, c)
                eng.nop().then_inc(bar, 1)
                eng.wait_ge(bar, tgt)
        self.cnt[barkey] = tgt


CPK_W = 5632

def dram_l1(nc, kio):
    return {
        "x": nc.dram_tensor("x", [T, D], F32, kind="ExternalInput").ap(),
        "wq": nc.dram_tensor("wq", [2048, 2048], F32, kind="ExternalInput").ap(),
        "wk": nc.dram_tensor("wk", [512, 2048], F32, kind="ExternalInput").ap(),
        "wv": nc.dram_tensor("wv", [512, 2048], F32, kind="ExternalInput").ap(),
        "wo": nc.dram_tensor("wo", [2048, 2048], F32, kind="ExternalInput").ap(),
        "cpk": nc.dram_tensor("cpk", [128, CPK_W], F32, kind="ExternalInput").ap(),
        "ident": nc.dram_tensor("ident", [128, 128], BF16, kind="ExternalInput").ap(),
        "xnT": nc.dram_tensor("xnT", [128, ND * T], BF16, kind=kio).ap(),
        "s0": nc.dram_tensor("s0", [T, D], BF16, kind=kio).ap(),
        "oTh": nc.dram_tensor("oTh", [128, ND * T], BF16, kind="Internal").ap(),
    }


def build_launch1():
    b = B()
    Dd = dram_l1(b.nc, "ExternalOutput")
    b.start()
    emit_phase1(b, Dd)
    b.close()
    return b


def emit_phase1(b, Dd):
    nc = b.nc
    x_d = Dd["x"]; wq_d = Dd["wq"]; wk_d = Dd["wk"]; wv_d = Dd["wv"]; wo_d = Dd["wo"]
    cp_d = Dd["cpk"]; id_d = Dd["ident"]; xnT_d = Dd["xnT"]; s0_d = Dd["s0"]; oT_d = Dd["oTh"]

    xt = [b.sb(f"xt{i}", [128, D], F32) for i in range(2)]
    xc = b.sb("xc", [128, D], F32)
    stats = b.sb("stats", [128, 64], F32)
    xnb = b.sb("xnb", [128, D], BF16)
    bigA = b.sb("bigA", [128, ND * T], BF16)
    ident = b.sb("ident_sb", [128, 128], BF16)
    cpk = b.sb("cpk_sb", [128, CPK_W], F32)
    wl = [b.sb(f"wl{i}", [128, 2048], BF16) for i in range(2)]
    wkvT = b.sb("wkvT", [128, ND * 256], BF16)
    wqTh = b.sb("wqTh", [128, ND * 128], BF16)
    kb = b.sb("kb", [128, NT * 128], BF16)
    kT = b.sb("kT", [128, T], BF16)
    vg = b.sb("vg", [128, NT * 128], BF16)
    qT = b.sb("qT", [128, T], BF16)
    ssb = b.sb("ssb", [128, 1152], F32)
    pbf = [b.sb(f"pbf{i}", [128, 1152], BF16) for i in range(2)]
    pts = b.sb("ptsb", [128, 1152], BF16)
    ost = b.sb("ost", [128, 128], BF16)
    oTg = b.sb("oTg", [128, 4 * T], BF16)
    oslab = [b.sb(f"oslab{i}", [128, ND * 128], BF16) for i in range(2)]
    s0t = b.sb("s0t", [128, D], BF16)
    kf = xt[0]
    sh = xt[1]
    qf = xc
    qb = kb

    ptA = b.ps("ptA", [128, 1024], BF16)
    ptB = b.ps("ptB", [128, 1024], BF16)
    pproj = [b.ps(f"pproj{i}", [128, 512], F32) for i in range(2)]
    psc = b.ps("psc", [128, 1536], F32)
    ppv = b.ps("ppv", [128, 512], F32)
    py = psc[:, 0:512]
    pjmark = [0, 0]
    wtmark = [0, 0]

    ropec = cpk[:, 0:2048]
    ropes = cpk[:, 2048:4096]
    causal = cpk[:, 4096:4224]
    para = cpk[:, 4224:5376]
    gqb = cpk[:, 5376:5504]
    gkb = cpk[:, 5504:5632]

    wl_guards = []

    def wl_load(srcs):
        guards = list(wl_guards)
        wl_guards.clear()

        @b.sec("gpsimd")
        def _(gp):
            for s, v in guards:
                gp.wait_ge(b.sem(s), v)
            for i, src in enumerate(srcs):
                b.dma(gp, "gpsimd", wl[i][:], src, f"gw{i}")

    @b.sec("sync")
    def _(s):
        b.dma(s, "sync", ident[:], id_d, "li")
        b.dma(s, "sync", cpk[:], cp_d, "lc")

    @b.sec("vector")
    def _(v):
        v.wait_ge(b.sem("lc"), b.val("lc"))

    @b.sec("scalar")
    def _(sc):
        sc.wait_ge(b.sem("lc"), b.val("lc"))

    xv = x_d.rearrange("(n p) d -> n p d", p=128)
    vln_marks = []
    for tb in range(NT):
        @b.sec("sync")
        def _(s, tb=tb):
            if tb >= 2:
                s.wait_ge(b.sem("ch_vector"), vln_marks[tb - 2])
            b.dma(s, "sync", xt[tb % 2][:], xv[tb], f"lx{tb % 2}")

        @b.sec("vector")
        def _(v, tb=tb):
            v.wait_ge(b.sem(f"lx{tb % 2}"), b.val(f"lx{tb % 2}"))
            src = xt[tb % 2][:]
            v.tensor_reduce(out=stats[:, 0:1], in_=src, axis=AX, op=ALU.add)
            v.tensor_scalar(out=stats[:, 1:2], in0=stats[:, 0:1], scalar1=-1.0 / D, scalar2=None, op0=ALU.mult)
            v.tensor_scalar(out=xc[:], in0=src, scalar1=stats[:, 1:2], scalar2=None, op0=ALU.add)
            vln_marks.append(b.chv("vector"))

        @b.sec("scalar")
        def _(sc, tb=tb):
            b.we(sc, "vector")
            sc.activation(out=xnb[:], in_=xc[:], func=AF.Square, accum_out=stats[:, 2:3])

        @b.sec("vector")
        def _(v, tb=tb):
            b.we(v, "scalar")
            v.tensor_scalar(out=stats[:, 3:4], in0=stats[:, 2:3], scalar1=1.0 / D, scalar2=1e-5, op0=ALU.mult, op1=ALU.add)

        @b.sec("scalar")
        def _(sc, tb=tb):
            b.we(sc, "vector")
            sc.activation(out=stats[:, 4:5], in_=stats[:, 3:4], func=AF.Sqrt)

        @b.sec("vector")
        def _(v, tb=tb):
            b.we(v, "scalar")
            v.reciprocal(out=stats[:, 5:6], in_=stats[:, 4:5])
            if tb > 0:
                v.wait_ge(b.sem("pln"), 2 * tb)
            v.tensor_scalar(out=xnb[:], in0=xc[:], scalar1=stats[:, 5:6], scalar2=None, op0=ALU.mult)

        @b.sec("tensor")
        def _(t, tb=tb):
            b.we(t, "vector")
            if tb == 0:
                t.wait_ge(b.sem("li"), b.val("li"))
            b.we(t, "scalar")
            for half in range(2):
                dst = ptA if half == 0 else ptB
                for j in range(8):
                    kd = half * 8 + j
                    st0, stride = perm_slice_params(kd)
                    inst = t.transpose(dst[:, j * 128:(j + 1) * 128], xnb[:, st0:st0 + 255:stride], ident[:])
                b.inc(inst, "pln")

        @b.sec("scalar")
        def _(sc, tb=tb):
            sc.wait_ge(b.sem("pln"), b.val("pln"))
            for half in range(2):
                src = ptA if half == 0 else ptB
                dstv = bigA[:].rearrange("p (n t) -> p n t", t=T)[:, half * 8:half * 8 + 8, tb * 128:(tb + 1) * 128]
                sc.copy(out=dstv, in_=src[:].rearrange("p (n t) -> p n t", t=128))

    @b.sec("sync")
    def _(s):
        s.wait_ge(b.sem("ch_scalar"), b.chv("scalar"))
        b.dma(s, "sync", xnT_d, bigA[:], "sxnT")

    b.barrier()

    def rope_norm(strip_f32, gamma):
        @b.sec("vector")
        def _(v):
            b.we(v, "scalar")
            v3 = strip_f32[:].rearrange("p (n w) -> p n w", w=128)
            sh3 = sh[:].rearrange("p (n w) -> p n w", w=128)
            c3 = ropec.rearrange("p (n w) -> p n w", w=128)
            s3 = ropes.rearrange("p (n w) -> p n w", w=128)
            v.tensor_copy(out=sh3[:, :, 0:64], in_=v3[:, :, 64:128])
            v.tensor_copy(out=sh3[:, :, 64:128], in_=v3[:, :, 0:64])
            v.tensor_tensor(out=sh3, in0=sh3, in1=s3, op=ALU.mult)
            v.tensor_tensor(out=v3, in0=v3, in1=c3, op=ALU.mult)
            v.tensor_tensor(out=v3, in0=v3, in1=sh3, op=ALU.add)
            v.tensor_tensor(out=sh3, in0=v3, in1=v3, op=ALU.mult)
            v.tensor_reduce(out=stats[:, 8:8 + NT].unsqueeze(2), in_=sh3, axis=AX, op=ALU.add)

        @b.sec("scalar")
        def _(sc):
            b.we(sc, "vector")
            sc.activation(out=stats[:, 24:24 + NT], in_=stats[:, 8:8 + NT], func=AF.Sqrt)

        @b.sec("vector")
        def _(v):
            b.we(v, "scalar")
            v.tensor_scalar(out=stats[:, 24:24 + NT], in0=stats[:, 24:24 + NT], scalar1=1e-6, scalar2=None, op0=ALU.add)
            v.reciprocal(out=stats[:, 40:40 + NT], in_=stats[:, 24:24 + NT])

    def rope_apply(strip_f32, gamma, out_bf):
        @b.sec("vector")
        def _(v):
            v3 = strip_f32[:].rearrange("p (n w) -> p n w", w=128)
            v.tensor_tensor(out=v3, in0=v3, in1=stats[:, 40:40 + NT].unsqueeze(2).broadcast_to([128, NT, 128]), op=ALU.mult)
            ob3 = out_bf[:].rearrange("p (n w) -> p n w", w=128)
            v.tensor_tensor(out=ob3, in0=v3, in1=gamma.unsqueeze(1).broadcast_to([128, NT, 128]), op=ALU.mult)

    for g in range(4):
        wl_load([wk_d[g * 128:(g + 1) * 128, :], wv_d[g * 128:(g + 1) * 128, :]])

        @b.sec("tensor")
        def _(t, g=g):
            t.wait_ge(b.sem("gw0"), b.val("gw0"))
            t.wait_ge(b.sem("gw1"), b.val("gw1"))
            for kd in range(ND):
                st0, stride = perm_slice_params(kd)
                dst = ptA if (kd % 2 == 0) else ptB
                if wtmark[kd % 2] > 0:
                    t.wait_ge(b.sem("ch_scalar"), wtmark[kd % 2])
                t.transpose(dst[:, 0:128], wl[0][:, st0:st0 + 255:stride], ident[:])
                inst = t.transpose(dst[:, 128:256], wl[1][:, st0:st0 + 255:stride], ident[:])
                b.inc(inst, "pwt")

                @b.sec("scalar")
                def _(sc, kd=kd, dst=dst):
                    sc.wait_ge(b.sem("pwt"), b.val("pwt"))
                    sc.copy(out=wkvT[:, kd * 256:(kd + 1) * 256], in_=dst[:, 0:256])
                    wtmark[kd % 2] = b.chv("scalar")
        wl_guards.append(("pwt", b.val("pwt")))

        for tb in range(NT):
            @b.sec("tensor")
            def _(t, tb=tb, g=g):
                if tb == 0:
                    b.we(t, "scalar")
                if pjmark[tb % 2] > 0:
                    t.wait_ge(b.sem("ch_vector"), pjmark[tb % 2])
                last = None
                for kd in range(ND):
                    last = t.matmul(pproj[tb % 2][:, 0:256], bigA[:, kd * T + tb * 128:kd * T + (tb + 1) * 128],
                                    wkvT[:, kd * 256:(kd + 1) * 256], start=(kd == 0), stop=(kd == ND - 1))
                b.inc(last, "pkv")

            @b.sec("vector")
            def _(v, tb=tb, g=g):
                v.wait_ge(b.sem("pkv"), b.val("pkv"))
                if g > 0 and tb == 0:
                    b.w(v, "ppv")
                v.tensor_copy(out=kf[:, tb * 128:(tb + 1) * 128], in_=pproj[tb % 2][:, 0:128])
                v.tensor_copy(out=vg[:, tb * 128:(tb + 1) * 128], in_=pproj[tb % 2][:, 128:256])
                pjmark[tb % 2] = b.chv("vector")

        @b.sec("vector")
        def _(v, g=g):
            b.w(v, "pkt")
            b.w(v, "pqt")
        rope_norm(kf, gkb)
        rope_apply(kf, gkb, kb)

        @b.sec("tensor")
        def _(t, g=g):
            b.we(t, "vector")
            for tb in range(NT):
                dst = ptA if (tb % 2 == 0) else ptB
                if wtmark[tb % 2] > 0:
                    t.wait_ge(b.sem("ch_scalar"), wtmark[tb % 2])
                b.inc(t.transpose(dst[:, 0:128], kb[:, tb * 128:(tb + 1) * 128], ident[:]), "pkt")

                @b.sec("scalar")
                def _(sc, tb=tb, dst=dst):
                    sc.wait_ge(b.sem("pkt"), b.val("pkt"))
                    sc.copy(out=kT[:, tb * 128:(tb + 1) * 128], in_=dst[:, 0:128])
                    wtmark[tb % 2] = b.chv("scalar")

        for hh in range(4):
            h = g * 4 + hh
            wl_load([wq_d[h * 128:(h + 1) * 128, :]])

            @b.sec("tensor")
            def _(t, h=h):
                t.wait_ge(b.sem("gw0"), b.val("gw0"))
                for kd in range(ND):
                    st0, stride = perm_slice_params(kd)
                    dst = ptA if (kd % 2 == 0) else ptB
                    if wtmark[kd % 2] > 0:
                        t.wait_ge(b.sem("ch_scalar"), wtmark[kd % 2])
                    b.inc(t.transpose(dst[:, 0:128], wl[0][:, st0:st0 + 255:stride], ident[:]), "pwq")

                    @b.sec("scalar")
                    def _(sc, kd=kd, dst=dst):
                        sc.wait_ge(b.sem("pwq"), b.val("pwq"))
                        sc.copy(out=wqTh[:, kd * 128:(kd + 1) * 128], in_=dst[:, 0:128])
                        wtmark[kd % 2] = b.chv("scalar")
            wl_guards.append(("pwq", b.val("pwq")))

            for tb in range(NT):
                @b.sec("tensor")
                def _(t, tb=tb):
                    if tb == 0:
                        b.we(t, "scalar")
                    if pjmark[tb % 2] > 0:
                        t.wait_ge(b.sem("ch_vector"), pjmark[tb % 2])
                    last = None
                    for kd in range(ND):
                        last = t.matmul(pproj[tb % 2][:, 0:128], bigA[:, kd * T + tb * 128:kd * T + (tb + 1) * 128],
                                        wqTh[:, kd * 128:(kd + 1) * 128], start=(kd == 0), stop=(kd == ND - 1))
                    b.inc(last, "pq")

                @b.sec("vector")
                def _(v, tb=tb):
                    v.wait_ge(b.sem("pq"), b.val("pq"))
                    v.tensor_copy(out=qf[:, tb * 128:(tb + 1) * 128], in_=pproj[tb % 2][:, 0:128])
                    pjmark[tb % 2] = b.chv("vector")

            @b.sec("vector")
            def _(v):
                b.w(v, "pkt")
                b.w(v, "pqt")
            rope_norm(qf, gqb)
            rope_apply(qf, gqb, qb)

            @b.sec("tensor")
            def _(t, h=h):
                b.we(t, "vector")
                for tb in range(NT):
                    dst = ptA if (tb % 2 == 0) else ptB
                    if wtmark[tb % 2] > 0:
                        t.wait_ge(b.sem("ch_scalar"), wtmark[tb % 2])
                    b.inc(t.transpose(dst[:, 0:128], qb[:, tb * 128:(tb + 1) * 128], ident[:]), "pqt")

                    @b.sec("scalar")
                    def _(sc, tb=tb, dst=dst):
                        sc.wait_ge(b.sem("pqt"), b.val("pqt"))
                        sc.copy(out=qT[:, tb * 128:(tb + 1) * 128], in_=dst[:, 0:128])
                        wtmark[tb % 2] = b.chv("scalar")

            sttmark = [0] * NT
            expmark = [0] * NT
            eptmark = [0] * NT
            vomark = [0] * NT
            eotmark = [0] * NT
            summark = [0] * NT

            def blk(tb):
                t0 = tb * 128
                ks0 = 0 if t0 < WIN else t0 - WIN
                ncols = t0 + 128 if t0 < WIN else 1152
                return t0, ks0, ncols, ncols // 128

            def emit_scores(tb):
                t0, ks0, ncols, nkt = blk(tb)

                @b.sec("tensor")
                def _(t, tb=tb, ks0=ks0, ncols=ncols):
                    if tb == 0:
                        b.we(t, "scalar")
                    if tb > 0:
                        t.wait_ge(b.sem("ch_vector"), sttmark[tb - 1])
                    c0 = 0
                    last = None
                    while c0 < ncols:
                        cw = min(512, ncols - c0)
                        last = t.matmul(psc[:, c0:c0 + cw], qT[:, tb * 128:(tb + 1) * 128],
                                        kT[:, ks0 + c0:ks0 + c0 + cw], start=True, stop=True)
                        c0 += cw
                    b.inc(last, "psc")

                @b.sec("vector")
                def _(v, tb=tb, ncols=ncols):
                    v.wait_ge(b.sem("psc"), b.val("psc"))
                    if tb > 0:
                        v.wait_ge(b.sem("ch_scalar"), expmark[tb - 1])
                    if tb < 8:
                        if ncols > 128:
                            v.tensor_scalar(out=ssb[:, 0:ncols - 128], in0=psc[:, 0:ncols - 128], scalar1=RSD, scalar2=None, op0=ALU.mult)
                        v.scalar_tensor_tensor(out=ssb[:, ncols - 128:ncols], in0=psc[:, ncols - 128:ncols],
                                               scalar=RSD, in1=causal, op0=ALU.mult, op1=ALU.add)
                    else:
                        v.scalar_tensor_tensor(out=ssb[:, 0:ncols], in0=psc[:, 0:ncols],
                                               scalar=RSD, in1=para, op0=ALU.mult, op1=ALU.add)
                    sttmark[tb] = b.chv("vector")
                    v.tensor_reduce(out=stats[:, 0:1], in_=ssb[:, 0:ncols], axis=AX, op=ALU.max)
                    v.tensor_scalar(out=stats[:, tb % 2 * 8 + 6:tb % 2 * 8 + 7], in0=stats[:, 0:1], scalar1=-1.0, scalar2=None, op0=ALU.mult)

                @b.sec("scalar")
                def _(sc, tb=tb, ncols=ncols):
                    b.we(sc, "vector")
                    if tb >= 2:
                        sc.wait_ge(b.sem("ppt"), ptmark[tb - 2])
                    sc.activation(out=pbf[tb % 2][:, 0:ncols], in_=ssb[:, 0:ncols], func=AF.Exp,
                                  bias=stats[:, tb % 2 * 8 + 6:tb % 2 * 8 + 7], scale=1.0)
                    expmark[tb] = b.chv("scalar")

                @b.sec("vector")
                def _(v, tb=tb, ncols=ncols):
                    v.wait_ge(b.sem("ch_scalar"), expmark[tb])
                    v.tensor_reduce(out=stats[:, 2:3], in_=pbf[tb % 2][:, 0:ncols], axis=AX, op=ALU.add)
                    v.reciprocal(out=stats[:, tb % 2 * 8 + 7:tb % 2 * 8 + 8], in_=stats[:, 2:3])
                    summark[tb] = b.chv("vector")

            ptmark = [0] * NT

            def emit_pv(tb, hh):
                t0, ks0, ncols, nkt = blk(tb)

                @b.sec("tensor")
                def _(t, tb=tb, ks0=ks0, nkt=nkt):
                    t.wait_ge(b.sem("ch_scalar"), expmark[tb])
                    for j in range(nkt):
                        dst = ptA if (j % 2 == 0) else ptB
                        b.we(t, "scalar")
                        b.inc(t.transpose(dst[:, 0:128], pbf[tb % 2][:, j * 128:(j + 1) * 128], ident[:]), "ppt")

                        @b.sec("scalar")
                        def _(sc, j=j, dst=dst):
                            sc.wait_ge(b.sem("ppt"), b.val("ppt"))
                            sc.copy(out=pts[:, j * 128:(j + 1) * 128], in_=dst[:, 0:128])
                    ptmark[tb] = b.val("ppt")

                    b.we(t, "scalar")
                    if tb > 0:
                        t.wait_ge(b.sem("ch_vector"), vomark[tb - 1])
                    last = None
                    for j in range(nkt):
                        last = t.matmul(ppv[:, 0:128], pts[:, j * 128:(j + 1) * 128],
                                        vg[:, (ks0 // 128 + j) * 128:(ks0 // 128 + j + 1) * 128],
                                        start=(j == 0), stop=(j == nkt - 1))
                    b.inc(last, "ppv")

                @b.sec("vector")
                def _(v, tb=tb):
                    v.wait_ge(b.sem("ppv"), b.val("ppv"))
                    v.wait_ge(b.sem("ch_vector"), summark[tb])
                    b.w(v, "pot")
                    v.tensor_scalar(out=ost[:], in0=ppv[:, 0:128], scalar1=stats[:, tb % 2 * 8 + 7:tb % 2 * 8 + 8], scalar2=None, op0=ALU.mult)
                    vomark[tb] = b.chv("vector")

                @b.sec("tensor")
                def _(t, tb=tb):
                    t.wait_ge(b.sem("ch_vector"), vomark[tb])
                    if tb > 0:
                        t.wait_ge(b.sem("ch_scalar"), eotmark[tb - 1])
                    b.inc(t.transpose(ptA[:, 896:1024], ost[:], ident[:]), "pot")

                @b.sec("scalar")
                def _(sc, tb=tb, hh=hh):
                    sc.wait_ge(b.sem("pot"), b.val("pot"))
                    b.w(sc, "sog")
                    sc.copy(out=oTg[:, hh * T + tb * 128:hh * T + (tb + 1) * 128], in_=ptA[:, 896:1024])
                    eotmark[tb] = b.chv("scalar")

            emit_scores(0)
            for tb in range(1, NT):
                emit_scores(tb)
                emit_pv(tb - 1, hh)
            emit_pv(NT - 1, hh)

        @b.sec("sync")
        def _(s, g=g):
            s.wait_ge(b.sem("ch_scalar"), b.chv("scalar"))
            b.dma(s, "sync", oT_d[:, g * 4 * T:(g + 1) * 4 * T], oTg[:], "soT")
            s.wait_ge(b.sem("soT"), b.val("soT"))
            b.inc(s.nop(), "sog")

    b.barrier()

    woT = bigA
    for kd in range(ND):
        wl_load([wo_d[kd * 128:(kd + 1) * 128, :]])

        @b.sec("tensor")
        def _(t, kd=kd):
            t.wait_ge(b.sem("gw0"), b.val("gw0"))
            for ke in range(ND):
                dst = ptA if (ke % 2 == 0) else ptB
                if wtmark[ke % 2] > 0:
                    t.wait_ge(b.sem("ch_scalar"), wtmark[ke % 2])
                b.inc(t.transpose(dst[:, 0:128], wl[0][:, ke * 128:(ke + 1) * 128], ident[:]), "pwo")

                @b.sec("scalar")
                def _(sc, ke=ke, kd=kd, dst=dst):
                    sc.wait_ge(b.sem("pwo"), b.val("pwo"))
                    sc.copy(out=woT[:, ke * T + kd * 128:ke * T + (kd + 1) * 128], in_=dst[:, 0:128])
                    wtmark[ke % 2] = b.chv("scalar")
        wl_guards.append(("pwo", b.val("pwo")))

    for tb in range(NT):
        @b.sec("sync")
        def _(s, tb=tb):
            if tb >= 2:
                s.wait_ge(b.sem("py"), 4 * (tb - 1))
            b.dma(s, "sync", oslab[tb % 2][:].rearrange("p (e t) -> p e t", t=128),
                  oT_d.rearrange("p (e t) -> p e t", t=T)[:, :, tb * 128:(tb + 1) * 128], f"lo{tb % 2}")

        @b.sec("tensor")
        def _(t, tb=tb):
            t.wait_ge(b.sem(f"lo{tb % 2}"), b.val(f"lo{tb % 2}"))
            if tb == 0:
                b.we(t, "scalar")
            for dch in range(4):
                b.we(t, "vector")
                last = None
                for ke in range(ND):
                    last = t.matmul(py[:, 0:512],
                                    oslab[tb % 2][:, ke * 128:(ke + 1) * 128],
                                    woT[:, ke * T + dch * 512:ke * T + (dch + 1) * 512],
                                    start=(ke == 0), stop=(ke == ND - 1))
                b.inc(last, "py")

                @b.sec("vector")
                def _(v, dch=dch):
                    v.wait_ge(b.sem("py"), b.val("py"))
                    b.w(v, "sst")
                    v.tensor_copy(out=s0t[:, dch * 512:(dch + 1) * 512], in_=py[:, 0:512])

        @b.sec("sync")
        def _(s, tb=tb):
            s.wait_ge(b.sem("ch_vector"), b.chv("vector"))
            b.dma(s, "sync", s0_d.rearrange("(n p) d -> n p d", p=128)[tb], s0t[:], "ss0")
            s.wait_ge(b.sem("ss0"), b.val("ss0"))
            b.inc(s.nop(), "sst", 4)

    @b.sec("sync")
    def _(s):
        s.wait_ge(b.sem("ss0"), b.val("ss0"))
        s.wait_ge(b.sem("sxnT"), b.val("sxnT"))


def _launch1_inmaps(inputs):
    consts = make_consts()
    cpk = np.zeros((128, CPK_W), dtype=np.float32)
    cpk[:, 0:2048] = consts["ropec"].reshape(16, 128, 128).transpose(1, 0, 2).reshape(128, 2048)
    cpk[:, 2048:4096] = consts["ropes"].reshape(16, 128, 128).transpose(1, 0, 2).reshape(128, 2048)
    cpk[:, 4096:4224] = consts["causal"]
    cpk[:, 4224:5376] = consts["para"]
    cpk[:, 5376:5504] = np.broadcast_to(inputs["L0_attn_gamma_q"], (128, 128)).astype(np.float32)
    cpk[:, 5504:5632] = np.broadcast_to(inputs["L0_attn_gamma_k"], (128, 128)).astype(np.float32)
    im = []
    for c in range(NCORES):
        im.append({
            "x": np.ascontiguousarray(inputs["L0_attn_x"][c]),
            "wq": inputs["L0_attn_wq"], "wk": inputs["L0_attn_wk"],
            "wv": inputs["L0_attn_wv"], "wo": inputs["L0_attn_wo"],
            "cpk": cpk, "ident": consts["ident"],
        })
    return im


def host_launch1(inputs):
    b = build_launch1()
    res = run_bass_kernel_spmd(b.nc, _launch1_inmaps(inputs), core_ids=list(range(NCORES)))
    return res.results


def dram_l2(nc, kio, with_ident=True):
    d = {
        "gWp": nc.dram_tensor("gWp", [16384, 1024], I32, kind="ExternalInput").ap(),
        "gSc": nc.dram_tensor("gSc", [16384, 64], F32, kind="ExternalInput").ap(),
        "uWp": nc.dram_tensor("uWp", [16384, 1024], I32, kind="ExternalInput").ap(),
        "uSc": nc.dram_tensor("uSc", [16384, 64], F32, kind="ExternalInput").ap(),
        "dWp": nc.dram_tensor("dWp", [2048, 8192], I32, kind="ExternalInput").ap(),
        "dSc": nc.dram_tensor("dSc", [2048, 512], F32, kind="ExternalInput").ap(),
        "WgT": nc.dram_tensor("WgT", [128, ND * 16384], BF16, kind=kio).ap(),
        "WuT": nc.dram_tensor("WuT", [128, ND * 16384], BF16, kind=kio).ap(),
        "Wdd": nc.dram_tensor("Wdd", [2048, 16384], BF16, kind=kio).ap(),
    }
    if with_ident:
        d["ident"] = nc.dram_tensor("ident", [128, 128], BF16, kind="ExternalInput").ap()
    return d


def build_launch2():
    b = B()
    Dd = dram_l2(b.nc, "ExternalOutput")
    b.start()
    emit_phase2(b, Dd)
    b.close()
    return b


def emit_phase2(b, Dd):
    nc = b.nc
    gW_d = Dd["gWp"]; gS_d = Dd["gSc"]; uW_d = Dd["uWp"]; uS_d = Dd["uSc"]
    dW_d = Dd["dWp"]; dS_d = Dd["dSc"]; id_d = Dd["ident"]
    WgT_d = Dd["WgT"]; WuT_d = Dd["WuT"]; Wd_d = Dd["Wdd"]

    NS = 2
    wp = [b.sb(f"wp{i}", [128, 1024], I32) for i in range(NS)]
    scs = [b.sb(f"scs{i}", [128, 64], F32) for i in range(NS)]
    ji = [b.sb(f"ji{i}", [128, 2048], I32) for i in range(NS)]
    b3i = [b.sb(f"b3i{i}", [128, 2048], I32) for i in range(NS)]
    jf = [b.sb(f"jf{i}", [128, 2048], BF16) for i in range(NS)]
    b3f = [b.sb(f"b3f{i}", [128, 2048], BF16) for i in range(NS)]
    tA = b.sb("tA", [128, 2048], BF16)
    tB = b.sb("tB", [128, 2048], BF16)
    tD = [b.sb(f"tD{i}", [128, 2048], BF16) for i in range(NS)]
    vals = [b.sb(f"vals{i}", [128, 2048], BF16) for i in range(NS)]
    stg = [b.sb(f"stg{i}", [128, ND * 128], BF16) for i in range(NS)]
    ident = b.sb("ident_sb", [128, 128], BF16)
    ptA = b.ps("ptA", [128, 1024], BF16)
    ptB = b.ps("ptB", [128, 1024], BF16)

    @b.sec("sync")
    def _(s):
        b.dma(s, "sync", ident[:], id_d, "li")

    @b.sec("tensor")
    def _(t):
        t.wait_ge(b.sem("li"), b.val("li"))

    tiles = []
    for Wp_d, Sc_d, Out_d in ((gW_d, gS_d, WgT_d), (uW_d, uS_d, WuT_d)):
        for r in range(128):
            rs = r * 256 if r < 64 else (r - 64) * 256 + 1
            tiles.append(("gu", Wp_d, Sc_d, Out_d, rs, r))
    for kd in range(ND):
        for cc in range(8):
            tiles.append(("d", kd, cc))

    NTILES = len(tiles)
    s1mark = [0] * NTILES
    s2mark = [0] * NTILES
    s3mark = [0] * NTILES

    def stage_load(k):
        kind = tiles[k][0]
        sl = k % NS

        @b.sec("sync")
        def _(s, k=k, sl=sl):
            if k >= NS:
                s.wait_ge(b.sem("ch_vector"), max(s1mark[k - NS], s3mark[k - NS]))
            if tiles[k][0] == "gu":
                _, Wp_d, Sc_d, Out_d, rs, r = tiles[k]
                b.dma(s, "sync", wp[sl][:], Wp_d[rs:rs + 255:2, :], f"lwp{sl}")
                b.dma(s, "sync", scs[sl][:], Sc_d[rs:rs + 255:2, :], f"lsc{sl}")
            else:
                _, kd, cc = tiles[k]
                c0 = cc * 1024
                b.dma(s, "sync", wp[sl][:], dW_d[kd * 128:(kd + 1) * 128, c0:c0 + 1024], f"lwp{sl}")
                b.dma(s, "sync", scs[sl][:], dS_d[kd * 128:(kd + 1) * 128, c0 // 16:c0 // 16 + 64], f"lsc{sl}")

    def stage_int(k):
        sl = k % NS

        @b.sec("vector")
        def _(v, k=k, sl=sl):
            v.wait_ge(b.sem(f"lwp{sl}"), b.val(f"lwp{sl}"))
            if k >= NS:
                v.wait_ge(b.sem("ch_scalar"), s2mark[k - NS])
            Kb = 1024
            v.tensor_scalar(out=ji[sl][:, 0:Kb], in0=wp[sl][:, 0:Kb], scalar1=15, scalar2=None, op0=ALU.bitwise_and)
            v.tensor_scalar(out=b3i[sl][:, 0:Kb], in0=wp[sl][:, 0:Kb], scalar1=3, scalar2=None, op0=ALU.logical_shift_right)
            v.tensor_scalar(out=b3i[sl][:, 0:Kb], in0=b3i[sl][:, 0:Kb], scalar1=1, scalar2=None, op0=ALU.bitwise_and)
            v.tensor_scalar(out=ji[sl][:, Kb:2 * Kb], in0=wp[sl][:, 0:Kb], scalar1=4, scalar2=None, op0=ALU.logical_shift_right)
            v.tensor_scalar(out=ji[sl][:, Kb:2 * Kb], in0=ji[sl][:, Kb:2 * Kb], scalar1=7, scalar2=None, op0=ALU.bitwise_and)
            v.tensor_scalar(out=b3i[sl][:, Kb:2 * Kb], in0=wp[sl][:, 0:Kb], scalar1=7, scalar2=None, op0=ALU.logical_shift_right)
            v.tensor_scalar(out=b3i[sl][:, Kb:2 * Kb], in0=b3i[sl][:, Kb:2 * Kb], scalar1=1, scalar2=None, op0=ALU.bitwise_and)
            v.tensor_scalar(out=ji[sl][:, 0:Kb], in0=ji[sl][:, 0:Kb], scalar1=7, scalar2=None, op0=ALU.bitwise_and)
            s1mark[k] = b.chv("vector")

    def stage_cvt(k):
        sl = k % NS

        @b.sec("scalar")
        def _(sc, k=k, sl=sl):
            b.we(sc, "vector")
            sc.copy(out=jf[sl][:], in_=ji[sl][:])
            sc.copy(out=b3f[sl][:], in_=b3i[sl][:])
            s2mark[k] = b.chv("scalar")

    def stage_pwl(k):
        sl = k % NS

        @b.sec("vector")
        def _(v, k=k, sl=sl):
            v.wait_ge(b.sem("ch_scalar"), s2mark[k])
            v.wait_ge(b.sem(f"lsc{sl}"), b.val(f"lsc{sl}"))
            if k >= NS and tiles[k - NS][0] == "gu":
                v.wait_ge(b.sem("pdT"), b.val("pdT"))
            if k >= NS and tiles[k - NS][0] == "d":
                for nm in (f"sdl{(k - NS) % NS}", f"sdh{(k - NS) % NS}"):
                    if b.val(nm) > 0:
                        v.wait_ge(b.sem(nm), b.val(nm))
            Kb = 1024
            v.tensor_scalar(out=tA[:], in0=jf[sl][:], scalar1=4.0, scalar2=0.0, op0=ALU.subtract, op1=ALU.max)
            v.tensor_scalar(out=tB[:], in0=jf[sl][:], scalar1=6.0, scalar2=0.0, op0=ALU.subtract, op1=ALU.max)
            v.tensor_tensor(out=tD[sl][:], in0=jf[sl][:], in1=tA[:], op=ALU.add)
            v.scalar_tensor_tensor(out=tD[sl][:], in0=tD[sl][:], scalar=0.5, in1=tB[:], op0=ALU.mult, op1=ALU.add)
            v.tensor_scalar(out=tA[:], in0=b3f[sl][:], scalar1=-2.0, scalar2=1.0, op0=ALU.mult, op1=ALU.add)
            v.tensor_tensor(out=tD[sl][:], in0=tD[sl][:], in1=tA[:], op=ALU.mult)
            nb_ = Kb // 16
            ov = vals[sl][:, 0:2 * Kb].rearrange("p (h bl i) -> p h bl i", h=2, i=16)
            sc4 = scs[sl][:, 0:nb_].unsqueeze(1).unsqueeze(3).broadcast_to([128, 2, nb_, 16])
            v.tensor_tensor(out=ov, in0=tD[sl][:, 0:2 * Kb].rearrange("p (h bl i) -> p h bl i", h=2, i=16), in1=sc4, op=ALU.mult)
            s3mark[k] = b.chv("vector")

    def stage_out(k):
        sl = k % NS
        kind = tiles[k][0]
        if kind == "gu":
            _, Wp_d, Sc_d, Out_d, rs, r = tiles[k]

            @b.sec("tensor")
            def _(t, k=k, sl=sl):
                t.wait_ge(b.sem("ch_vector"), s3mark[k])
                for half in range(2):
                    dst = ptA if half == 0 else ptB
                    b.we(t, "scalar")
                    for j in range(8):
                        kd2 = half * 8 + j
                        inst = t.transpose(dst[:, j * 128:(j + 1) * 128], vals[sl][:, kd2 * 128:(kd2 + 1) * 128], ident[:])
                    b.inc(inst, "pdT")

                    @b.sec("scalar")
                    def _(sc, half=half, dst=dst, sl=sl):
                        sc.wait_ge(b.sem("pdT"), b.val("pdT"))
                        soname = f"so{sl}"
                        if b.val(soname) > 0:
                            sc.wait_ge(b.sem(soname), b.val(soname))
                        sc.copy(out=stg[sl][:, half * 1024:(half + 1) * 1024], in_=dst[:, 0:1024])

            @b.sec("sync")
            def _(s, k=k, sl=sl, Out_d=Out_d, r=r):
                s.wait_ge(b.sem("ch_scalar"), b.chv("scalar"))
                b.dma(s, "sync", Out_d.rearrange("p (n t) -> p n t", t=16384)[:, :, r * 128:(r + 1) * 128],
                      stg[sl][:].rearrange("p (n t) -> p n t", t=128), f"so{sl}")
        else:
            _, kd, cc = tiles[k]
            c0 = cc * 1024

            @b.sec("sync")
            def _(s, k=k, sl=sl, kd=kd, c0=c0):
                s.wait_ge(b.sem("ch_vector"), s3mark[k])
                b.dma(s, "sync", Wd_d[kd * 128:(kd + 1) * 128, c0:c0 + 1024], vals[sl][:, 0:1024], f"sdl{sl}")
                b.dma(s, "sync", Wd_d[kd * 128:(kd + 1) * 128, 8192 + c0:8192 + c0 + 1024], vals[sl][:, 1024:2048], f"sdh{sl}")

    stage_load(0)
    stage_load(1)
    stage_int(0)
    stage_cvt(0)
    for k in range(1, NTILES):
        stage_int(k)
        stage_cvt(k)
        stage_pwl(k - 1)
        if k + 1 < NTILES:
            stage_load(k + 1)
        stage_out(k - 1)
    stage_pwl(NTILES - 1)
    stage_out(NTILES - 1)

    @b.sec("sync")
    def _(s):
        for nmn in ("so0", "so1", "sdl0", "sdl1", "sdh0", "sdh1"):
            if b.val(nmn) > 0:
                s.wait_ge(b.sem(nmn), b.val(nmn))


def _launch2_inmaps(inputs):
    consts = make_consts()
    im = []
    for c in range(NCORES):
        im.append({
            "gWp": inputs["L0_ffn_gate_W_packed"], "gSc": inputs["L0_ffn_gate_scales"],
            "uWp": inputs["L0_ffn_up_W_packed"], "uSc": inputs["L0_ffn_up_scales"],
            "dWp": inputs["L0_ffn_down_W_packed"], "dSc": inputs["L0_ffn_down_scales"],
            "ident": consts["ident"],
        })
    return im


def host_launch2(inputs, cores=1):
    b = build_launch2()
    res = run_bass_kernel_spmd(b.nc, _launch2_inmaps(inputs)[:cores], core_ids=list(range(cores)))
    return res.results


def build_launch3():
    b = B()
    nc = b.nc
    Dd = {
        "x": nc.dram_tensor("x", [T, D], F32, kind="ExternalInput").ap(),
        "xnTi": nc.dram_tensor("xnTi", [128, ND * T], BF16, kind="ExternalInput").ap(),
        "s0i": nc.dram_tensor("s0i", [T, D], BF16, kind="ExternalInput").ap(),
        "ident": nc.dram_tensor("ident", [128, 128], BF16, kind="ExternalInput").ap(),
        "x1": nc.dram_tensor("x1", [T, D], F32, kind="ExternalOutput").ap(),
        "xnT1": nc.dram_tensor("xnT1", [128, ND * T], BF16, kind="ExternalOutput").ap(),
    }
    Dd.update({k: v for k, v in dram_l2(nc, "ExternalInput", with_ident=False).items()
               if k in ("WgT", "WuT", "Wdd", "gWp", "gSc", "uWp", "uSc", "dWp", "dSc")})
    b.start()
    emit_phase3(b, Dd)
    b.close()
    return b


def emit_phase3(b, Dd):
    nc = b.nc
    x_d = Dd["x"]; xnT_d = Dd["xnTi"]; s0_d = Dd["s0i"]
    WgT_d = Dd["WgT"]; WuT_d = Dd["WuT"]; Wd_d = Dd["Wdd"]; id_d = Dd["ident"]
    x1_d = Dd["x1"]; xnT1_d = Dd["xnT1"]

    acc = b.sb("acc", [128, 4 * 2048], F32)
    xnTq = b.sb("xnTq", [128, ND * 512], BF16)
    wgc = [b.sb(f"wgc{i}", [128, ND * 512], BF16) for i in range(2)]
    wuc = [b.sb(f"wuc{i}", [128, ND * 512], BF16) for i in range(2)]
    wdc = [b.sb(f"wdc{i}", [128, 4 * 2048], BF16) for i in range(2)]
    ibuf = b.sb("ibuf", [128, 4 * 512], BF16)
    gf = b.sb("gf", [128, 512], F32)
    uf = b.sb("uf", [128, 512], F32)
    g2 = b.sb("g2", [128, 512], F32)
    ident = b.sb("ident_sb", [128, 128], BF16)
    xt = b.sb("xt", [128, D], F32)
    s0t = b.sb("s0t", [128, D], BF16)
    g0t = b.sb("g0t", [128, D], BF16)
    x1t = b.sb("x1t", [128, D], F32)
    xc = b.sb("xc", [128, D], F32)
    xnb = b.sb("xnb", [128, D], BF16)
    stg1 = b.sb("stg1", [128, ND * 128], BF16)
    stats = b.sb("stats", [128, 8], F32)

    pg = [b.ps(f"pg{i}", [128, 512], F32) for i in range(4)]
    pd = [b.ps(f"pd{i}", [128, 512], F32) for i in range(2)]
    ptA = b.ps("ptA", [128, 1024], BF16)
    ptB = b.ps("ptB", [128, 1024], BF16)

    @b.sec("sync")
    def _(s):
        b.dma(s, "sync", ident[:], id_d, "li")

    @b.sec("tensor")
    def _(t):
        t.wait_ge(b.sem("li"), b.val("li"))

    wg3 = WgT_d.rearrange("p (k n) -> p k n", n=16384)
    wu3 = WuT_d.rearrange("p (k n) -> p k n", n=16384)
    xnT3 = xnT_d.rearrange("p (k t) -> p k t", t=T)

    pgmark = [0] * 4
    pdmark = [0] * 2
    wmark = [0, 0]
    dmark = [0, 0]
    xqmark = [0]

    NCH = 32
    for quarter in range(4):
        tq0 = quarter * 512

        @b.sec("sync")
        def _(s, tq0=tq0):
            if b.val("pgu") > 0:
                s.wait_ge(b.sem("pgu"), xqmark[0])
            b.dma(s, "sync", xnTq[:].rearrange("p (k t) -> p k t", t=512),
                  xnT3[:, :, tq0:tq0 + 512], "lxh")

        for chunk in range(NCH):
            n0 = chunk * 512
            sl = chunk % 2

            @b.sec("sync")
            def _(s, n0=n0, sl=sl):
                if wmark[sl] > 0:
                    s.wait_ge(b.sem("pgu"), wmark[sl])
                b.dma(s, "sync", wgc[sl][:].rearrange("p (k n) -> p k n", n=512),
                      wg3[:, :, n0:n0 + 512], f"lwg{sl}")
                b.dma(s, "sync", wuc[sl][:].rearrange("p (k n) -> p k n", n=512),
                      wu3[:, :, n0:n0 + 512], f"lwu{sl}")

            @b.sec("scalar")
            def _(s, n0=n0, sl=sl):
                if dmark[sl] > 0:
                    s.wait_ge(b.sem("pdn"), dmark[sl])
                for kn in range(4):
                    slt = f"lwd{sl}{kn}"
                    h = b.sem(slt, "h")
                    if b.val(slt) > 0:
                        s.wait_ge(h, b.val(slt))
                    s.dma_start(out=wdc[sl][:, kn * 2048:(kn + 1) * 2048],
                                in_=Wd_d[:, n0 + kn * 128:n0 + (kn + 1) * 128],
                                transpose=True).then_inc(h, 16)
                    b.dma_by_q.setdefault("scalar", set()).add(slt)
                    b.add(slt, 16)

            for nt in range(4):
                @b.sec("tensor")
                def _(t, nt=nt, sl=sl):
                    t.wait_ge(b.sem(f"lwg{sl}"), b.val(f"lwg{sl}"))
                    t.wait_ge(b.sem(f"lwu{sl}"), b.val(f"lwu{sl}"))
                    t.wait_ge(b.sem("lxh"), b.val("lxh"))
                    for mi, wsrc in enumerate((wgc[sl], wuc[sl])):
                        pidx = nt % 2 * 2 + mi
                        if pgmark[pidx] > 0:
                            t.wait_ge(b.sem("ch_vector"), pgmark[pidx])
                        last = None
                        for kd in range(ND):
                            last = t.matmul(pg[pidx][:, 0:512],
                                            wsrc[:, kd * 512 + nt * 128:kd * 512 + (nt + 1) * 128],
                                            xnTq[:, kd * 512:(kd + 1) * 512],
                                            start=(kd == 0), stop=(kd == ND - 1))
                        b.inc(last, "pgu")

                @b.sec("vector")
                def _(v, nt=nt):
                    v.wait_ge(b.sem("pgu"), b.val("pgu"))
                    b.we(v, "scalar")
                    v.tensor_copy(out=gf[:], in_=pg[nt % 2 * 2][:, 0:512])
                    v.tensor_copy(out=uf[:], in_=pg[nt % 2 * 2 + 1][:, 0:512])
                    pgmark[nt % 2 * 2] = b.chv("vector")
                    pgmark[nt % 2 * 2 + 1] = b.chv("vector")

                @b.sec("scalar")
                def _(sc, nt=nt):
                    b.we(sc, "vector")
                    sc.activation(out=g2[:], in_=gf[:], func=AF.Square)

                @b.sec("vector")
                def _(v, nt=nt):
                    b.we(v, "scalar")
                    v.tensor_scalar(out=g2[:], in0=g2[:], scalar1=0.044715, scalar2=1.0, op0=ALU.mult, op1=ALU.add)
                    v.tensor_tensor(out=g2[:], in0=gf[:], in1=g2[:], op=ALU.mult)

                @b.sec("scalar")
                def _(sc, nt=nt):
                    b.we(sc, "vector")
                    sc.activation(out=g2[:], in_=g2[:], func=AF.Tanh, scale=0.7978845608028654)

                @b.sec("vector")
                def _(v, nt=nt):
                    b.we(v, "scalar")
                    v.tensor_tensor(out=g2[:], in0=gf[:], in1=g2[:], op=ALU.mult)
                    v.tensor_tensor(out=g2[:], in0=gf[:], in1=g2[:], op=ALU.add)
                    v.scalar_tensor_tensor(out=ibuf[:, nt * 512:(nt + 1) * 512], in0=g2[:],
                                           scalar=0.5, in1=uf[:], op0=ALU.mult, op1=ALU.mult)

            wmark[sl] = b.val("pgu")
            xqmark[0] = b.val("pgu")

            for tb in range(4):
                for dch in range(4):
                    @b.sec("tensor")
                    def _(t, tb=tb, dch=dch, sl=sl):
                        for kn in range(4):
                            t.wait_ge(b.sem(f"lwd{sl}{kn}"), b.val(f"lwd{sl}{kn}"))
                        pidx = (tb * 4 + dch) % 2
                        if pdmark[pidx] > 0:
                            t.wait_ge(b.sem("ch_vector"), pdmark[pidx])
                        b.we(t, "vector") if tb == 0 and dch == 0 else None
                        last = None
                        for kn in range(4):
                            last = t.matmul(pd[pidx][:, 0:512],
                                            ibuf[:, kn * 512 + tb * 128:kn * 512 + (tb + 1) * 128],
                                            wdc[sl][:, kn * 2048 + dch * 512:kn * 2048 + (dch + 1) * 512],
                                            start=(kn == 0), stop=(kn == 3))
                        b.inc(last, "pdn")

                    @b.sec("vector")
                    def _(v, tb=tb, dch=dch, chunk=chunk):
                        v.wait_ge(b.sem("pdn"), b.val("pdn"))
                        pidx = (tb * 4 + dch) % 2
                        asl = acc[:, tb * 2048 + dch * 512:tb * 2048 + (dch + 1) * 512]
                        if chunk == 0:
                            v.tensor_copy(out=asl, in_=pd[pidx][:, 0:512])
                        else:
                            v.tensor_tensor(out=asl, in0=asl, in1=pd[pidx][:, 0:512], op=ALU.add)
                        pdmark[pidx] = b.chv("vector")
            dmark[sl] = b.val("pdn")

        for tb in range(4):
            grow = tq0 + tb * 128

            @b.sec("sync")
            def _(s, grow=grow):
                b.we(s, "vector")
                b.dma(s, "sync", xt[:], x_d[grow:grow + 128, :], "lxt")
                b.dma(s, "sync", s0t[:], s0_d[grow:grow + 128, :], "ls0")

            @b.sec("scalar")
            def _(sc, tb=tb):
                b.we(sc, "vector")
                sc.activation(out=g0t[:], in_=acc[:, tb * 2048:(tb + 1) * 2048], func=AF.Silu)

            @b.sec("vector")
            def _(v, tb=tb):
                v.wait_ge(b.sem("lxt"), b.val("lxt"))
                v.wait_ge(b.sem("ls0"), b.val("ls0"))
                b.we(v, "scalar")
                b.w(v, "sx1")
                v.tensor_tensor(out=xc[:], in0=s0t[:], in1=g0t[:], op=ALU.mult)
                v.tensor_tensor(out=x1t[:], in0=xt[:], in1=xc[:], op=ALU.add)
                v.tensor_reduce(out=stats[:, 0:1], in_=x1t[:], axis=AX, op=ALU.add)
                v.tensor_scalar(out=stats[:, 1:2], in0=stats[:, 0:1], scalar1=-1.0 / D, scalar2=None, op0=ALU.mult)
                v.tensor_scalar(out=xc[:], in0=x1t[:], scalar1=stats[:, 1:2], scalar2=None, op0=ALU.add)

            @b.sec("sync")
            def _(s, grow=grow):
                b.we(s, "vector")
                b.dma(s, "sync", x1_d[grow:grow + 128, :], x1t[:], "sx1")
                s.wait_ge(b.sem("sx1"), b.val("sx1"))

            @b.sec("scalar")
            def _(sc):
                b.we(sc, "vector")
                sc.activation(out=xnb[:], in_=xc[:], func=AF.Square, accum_out=stats[:, 2:3])

            @b.sec("vector")
            def _(v):
                b.we(v, "scalar")
                v.tensor_scalar(out=stats[:, 3:4], in0=stats[:, 2:3], scalar1=1.0 / D, scalar2=1e-5, op0=ALU.mult, op1=ALU.add)

            @b.sec("scalar")
            def _(sc):
                b.we(sc, "vector")
                sc.activation(out=stats[:, 4:5], in_=stats[:, 3:4], func=AF.Sqrt)

            @b.sec("vector")
            def _(v):
                b.we(v, "scalar")
                v.reciprocal(out=stats[:, 5:6], in_=stats[:, 4:5])
                b.w(v, "pl2")
                v.tensor_scalar(out=xnb[:], in0=xc[:], scalar1=stats[:, 5:6], scalar2=None, op0=ALU.mult)

            @b.sec("tensor")
            def _(t):
                b.we(t, "vector")
                b.we(t, "scalar")
                for hf in range(2):
                    dst = ptA if hf == 0 else ptB
                    for j in range(8):
                        kd = hf * 8 + j
                        inst = t.transpose(dst[:, j * 128:(j + 1) * 128], xnb[:, kd * 128:(kd + 1) * 128], ident[:])
                    b.inc(inst, "pl2")

            @b.sec("scalar")
            def _(sc):
                sc.wait_ge(b.sem("pl2"), b.val("pl2"))
                b.w(sc, "sxn1")
                for hf in range(2):
                    src = ptA if hf == 0 else ptB
                    sc.copy(out=stg1[:, hf * 1024:(hf + 1) * 1024], in_=src[:, 0:1024])

            @b.sec("sync")
            def _(s, grow=grow):
                s.wait_ge(b.sem("ch_scalar"), b.chv("scalar"))
                b.dma(s, "sync", xnT1_d.rearrange("p (k t) -> p k t", t=T)[:, :, grow:grow + 128],
                      stg1[:].rearrange("p (k t) -> p k t", t=128), "sxn1")
                s.wait_ge(b.sem("sxn1"), b.val("sxn1"))

    @b.sec("sync")
    def _(s):
        s.wait_ge(b.sem("sx1"), b.val("sx1"))
        s.wait_ge(b.sem("sxn1"), b.val("sxn1"))


NH, DK, DV, CH = 6, 256, 512, 64

def dram_l4(nc, kio, with_common=True):
    d = {
        "l1wq": nc.dram_tensor("l1wq", [2048, 1536], F32, kind="ExternalInput").ap(),
        "l1wk": nc.dram_tensor("l1wk", [2048, 1536], F32, kind="ExternalInput").ap(),
        "l1wv": nc.dram_tensor("l1wv", [2048, 3072], F32, kind="ExternalInput").ap(),
        "l1wa": nc.dram_tensor("l1wa", [2048, 6], F32, kind="ExternalInput").ap(),
        "l1wb": nc.dram_tensor("l1wb", [2048, 6], F32, kind="ExternalInput").ap(),
        "l1wg": nc.dram_tensor("l1wg", [2048, 3072], F32, kind="ExternalInput").ap(),
        "l1wo": nc.dram_tensor("l1wo", [3072, 2048], F32, kind="ExternalInput").ap(),
        "cwq": nc.dram_tensor("cwq", [1536, 4], F32, kind="ExternalInput").ap(),
        "cwk": nc.dram_tensor("cwk", [1536, 4], F32, kind="ExternalInput").ap(),
        "cwv": nc.dram_tensor("cwv", [3072, 4], F32, kind="ExternalInput").ap(),
        "alog": nc.dram_tensor("alog", [6, 1], F32, kind="ExternalInput").ap(),
        "dtb": nc.dram_tensor("dtb", [6, 1], F32, kind="ExternalInput").ap(),
        "nrmw": nc.dram_tensor("nrmw", [1, 512], F32, kind="ExternalInput").ap(),
        "cp2": nc.dram_tensor("cp2", [128, 896], F32, kind="ExternalInput").ap(),
        "identf": nc.dram_tensor("identf", [128, 128], F32, kind="ExternalInput").ap(),
        "s1": nc.dram_tensor("s1", [T, D], BF16, kind=kio).ap(),
    }
    return d


def build_launch4():
    b = B()
    nc = b.nc
    Dd = dram_l4(nc, "ExternalOutput")
    Dd["xnTi"] = nc.dram_tensor("xnTi", [128, ND * T], BF16, kind="ExternalInput").ap()
    Dd["ident"] = nc.dram_tensor("ident", [128, 128], BF16, kind="ExternalInput").ap()
    b.start()
    emit_phase4(b, Dd)
    b.close()
    return b


def emit_phase4(b, Dd):
    nc = b.nc
    xnT_d = Dd["xnTi"]
    Wq_d = Dd["l1wq"]; Wk_d = Dd["l1wk"]; Wv_d = Dd["l1wv"]; Wa_d = Dd["l1wa"]
    Wb_d = Dd["l1wb"]; Wg_d = Dd["l1wg"]; Wo_d = Dd["l1wo"]
    cwq_d = Dd["cwq"]; cwk_d = Dd["cwk"]; cwv_d = Dd["cwv"]
    al_d = Dd["alog"]; dt_d = Dd["dtb"]; nw_d = Dd["nrmw"]
    cp2_d = Dd["cp2"]; id_d = Dd["ident"]; idf_d = Dd["identf"]; s1_d = Dd["s1"]
    CH = 128
    LN16 = 2.772588722239781

    xnq = b.sb("xnq", [128, ND * 512], BF16)
    wcol = [b.sb(f"wcol{i}", [128, ND * 128], BF16) for i in range(2)]
    wab = b.sb("wab", [128, 2 * ND * 6], BF16)
    wgh = b.sb("wgh", [128, ND * 512], BF16)
    wos = [b.sb(f"wos{i}", [128, 2048], BF16) for i in range(2)]
    qc = b.sb("qc", [128, 12 * 512], BF16)
    kc = b.sb("kc", [128, 12 * 512], BF16)
    vc = b.sb("vc", [128, 24 * 512], BF16)
    tails = b.sb("tails", [128, 48 * 4], F32)
    rw = b.sb("rw", [128, 520], F32)
    Sst = b.sb("Sst", [128, 6 * 2 * 512], F32)
    Sbf = b.sb("Sbf", [128, 6 * 2 * 512], BF16)
    ogT = b.sb("ogT", [128, 24 * 512], BF16)
    cwq = b.sb("cwq_sb", [128, 12 * 4], F32)
    cwk = b.sb("cwk_sb", [128, 12 * 4], F32)
    cwv = b.sb("cwv_sb", [128, 24 * 4], F32)
    alA = b.sb("alA", [6, 1], F32)
    dtb = b.sb("dtb_sb", [6, 1], F32)
    negA = b.sb("negA", [6, 1], F32)
    nw1 = b.sb("nw1", [1, 512], F32)
    nwb = b.sb("nwb", [128, 512], F32)
    cp2 = b.sb("cp2_sb", [128, 896], F32)
    ident = b.sb("ident_sb", [128, 128], BF16)
    identf = b.sb("identf_sb", [128, 128], F32)
    gbq = b.sb("gbq", [6, 1024], F32)
    gpp = b.sb("gpp", [6, 512], F32)
    gsp = b.sb("gsp", [1, 512], F32)
    bsp = b.sb("bsp", [1, 512], F32)
    brow = b.sb("brow", [1, 768], F32)
    bce = b.sb("bce", [128, 768], F32)
    colb = b.sb("colb", [128, 128], F32)
    colb2 = b.sb("colb2", [128, 128], F32)
    sqb = b.sb("sqb", [128, 512], F32)
    cacc = sqb
    lnb = b.sb("lnb", [1, 256], F32)
    qhT = b.sb("qhT", [128, 256], BF16)
    khT = b.sb("khT", [128, 256], BF16)
    ksc = b.sb("ksc", [128, 256], BF16)
    VT = b.sb("VT", [128, 512], BF16)
    EB = b.sb("EBb", [128, 128], F32)
    Einc = b.sb("Einc_sb", [128, 128], F32)
    kkb = b.sb("kkb", [128, 128], F32)
    kqb = b.sb("kqb", [128, 128], F32)
    w2b = b.sb("w2b", [128, 128], BF16)
    e1b = b.sb("e1b", [128, 128], F32)
    tf1 = b.sb("tf1", [128, 128], F32)
    tf2 = b.sb("tf2", [128, 128], F32)
    bdA = b.sb("bdA", [128, 128], BF16)
    bdTA = b.sb("bdTA", [128, 128], BF16)
    bdB = b.sb("bdB", [128, 128], BF16)
    bdTB = b.sb("bdTB", [128, 128], BF16)
    tdb = b.sb("tdb", [128, 128], BF16)
    tdTb = b.sb("tdTb", [128, 128], BF16)
    alb = b.sb("alb", [128, 128], BF16)
    nb = b.sb("nb", [128, 128], BF16)
    nbT = b.sb("nbT", [128, 128], BF16)
    n2b = b.sb("n2b", [128, 128], BF16)
    n2bT = b.sb("n2bT", [128, 128], BF16)
    n4b = b.sb("n4b", [128, 128], BF16)
    n4bT = b.sb("n4bT", [128, 128], BF16)
    betan = b.sb("betan", [128, 1], F32)
    eGc = b.sb("eGc", [128, 1], F32)
    rs_ = b.sb("rs_", [128, 4], F32)
    ubf = b.sb("ubf", [128, 512], F32)
    ub = b.sb("ub", [128, 512], BF16)
    obx = sqb
    sqd = ubf
    kscT = qhT
    gt_ = b.sb("gt_", [128, 512], BF16)
    ogb = b.sb("ogb", [128, 512], BF16)
    eGb = b.sb("eGb", [128, 128], F32)
    s1t = b.sb("s1t", [128, D], BF16)

    pA = b.ps("pA", [128, 512], F32)
    pB_ = b.ps("pB_", [128, 512], F32)
    pC = b.ps("pC", [128, 512], F32)
    pD = b.ps("pD", [128, 512], F32)
    pE2 = b.ps("pE2", [128, 512], F32)
    pF = b.ps("pF", [128, 512], F32)
    pTb = b.ps("pTb", [128, 512], BF16)

    @b.sec("sync")
    def _(s):
        b.dma(s, "sync", ident[:], id_d, "li")
        b.dma(s, "sync", identf[:], idf_d, "lif")
        b.dma(s, "sync", cp2[:], cp2_d, "lcp")
        b.dma(s, "sync", cwq[:].rearrange("p (n k) -> p n k", k=4), cwq_d.rearrange("(n p) k -> p n k", p=128), "lcwq")
        b.dma(s, "sync", cwk[:].rearrange("p (n k) -> p n k", k=4), cwk_d.rearrange("(n p) k -> p n k", p=128), "lcwk")
        b.dma(s, "sync", cwv[:].rearrange("p (n k) -> p n k", k=4), cwv_d.rearrange("(n p) k -> p n k", p=128), "lcwv")
        b.dma(s, "sync", alA[:], al_d, "lal")
        b.dma(s, "sync", dtb[:], dt_d, "ldt")
        b.dma(s, "sync", nw1[:], nw_d, "lnw")

    @b.sec("gpsimd")
    def _(gp):
        b.dma(gp, "gpsimd", wab[:].rearrange("p (m k h) -> p m k h", m=2, h=6)[:, 0],
              Wa_d.rearrange("(k p) h -> p k h", p=128), "lwa")
        b.dma(gp, "gpsimd", wab[:].rearrange("p (m k h) -> p m k h", m=2, h=6)[:, 1],
              Wb_d.rearrange("(k p) h -> p k h", p=128), "lwb")

    up_incl = cp2[:, 0:128]
    bd_up = cp2[:, 128:256]
    ones = cp2[:, 256:384]
    bd_lo = cp2[:, 384:512]
    ob_lo = cp2[:, 512:640]
    eye128 = cp2[:, 640:768]

    @b.sec("scalar")
    def _(sc):
        sc.wait_ge(b.sem("lal"), b.val("lal"))
        sc.wait_ge(b.sem("ldt"), b.val("ldt"))
        sc.wait_ge(b.sem("lcp"), b.val("lcp"))
        sc.wait_ge(b.sem("lcwq"), b.val("lcwq"))
        sc.wait_ge(b.sem("lcwk"), b.val("lcwk"))
        sc.wait_ge(b.sem("lcwv"), b.val("lcwv"))
        sc.activation(out=negA[:], in_=alA[:], func=AF.Exp)

    @b.sec("vector")
    def _(v):
        b.we(v, "scalar")
        v.tensor_scalar(out=negA[:], in0=negA[:], scalar1=-1.0, scalar2=None, op0=ALU.mult)
        v.memset(tails[:], 0.0)
        v.memset(Sst[:], 0.0)
        v.memset(Sbf[:], 0.0)
        v.memset(brow[:], 0.0)

    @b.sec("tensor")
    def _(t):
        t.wait_ge(b.sem("li"), b.val("li"))
        t.wait_ge(b.sem("lif"), b.val("lif"))
        t.wait_ge(b.sem("lcp"), b.val("lcp"))
        t.wait_ge(b.sem("lnw"), b.val("lnw"))
        b.inc(t.matmul(pA[:, 0:512], ones[0:1, :], nw1[0:1, :], start=True, stop=True), "pnw")

    @b.sec("vector")
    def _(v):
        v.wait_ge(b.sem("pnw"), b.val("pnw"))
        v.tensor_copy(out=nwb[:], in_=pA[:, 0:512])

    wobf_d = nc.dram_tensor("wobf", [3072, 2048], BF16, kind="Internal").ap()

    @b.sec("gpsimd")
    def _(gp):
        for ke in range(24):
            sl = ke % 2
            if ke >= 2:
                gp.wait_ge(b.sem(f"swo{sl}"), b.val(f"swo{sl}"))
            b.dma(gp, "gpsimd", wos[sl][:], Wo_d[ke * 128:(ke + 1) * 128, :], f"lwoc{sl}")

            @b.sec("scalar")
            def _(sc, ke=ke, sl=sl):
                sc.wait_ge(b.sem(f"lwoc{sl}"), b.val(f"lwoc{sl}"))
                b.dma(sc, "scalar", wobf_d[ke * 128:(ke + 1) * 128, :], wos[sl][:], f"swo{sl}")

    xnT3 = xnT_d.rearrange("p (k t) -> p k t", t=T)
    PROJ = (("q", Wq_d, 12, cwq, qc, 0), ("k", Wk_d, 12, cwk, kc, 12), ("v", Wv_d, 24, cwv, vc, 24))

    for q in range(4):
        q0 = q * 512

        @b.sec("sync")
        def _(s, q0=q0):
            b.w(s, "pgat")
            b.w(s, "pprj")
            b.w(s, "pab")
            b.dma(s, "sync", xnq[:].rearrange("p (k t) -> p k t", t=512),
                  xnT3[:, :, q0:q0 + 512], "lxq")

        @b.sec("tensor")
        def _(t, q=q):
            t.wait_ge(b.sem("lxq"), b.val("lxq"))
            t.wait_ge(b.sem("lwa"), b.val("lwa"))
            t.wait_ge(b.sem("lwb"), b.val("lwb"))
            b.we(t, "vector")
            wab4 = wab[:].rearrange("p (m k h) -> p m k h", m=2, h=6)
            last = None
            for kd in range(ND):
                last = t.matmul(pF[0:6, 0:512], wab4[:, 0, kd], xnq[:, kd * 512:(kd + 1) * 512],
                                start=(kd == 0), stop=(kd == ND - 1))
            b.inc(last, "pab")
            last = None
            for kd in range(ND):
                last = t.matmul(pE2[0:6, 0:512], wab4[:, 1, kd], xnq[:, kd * 512:(kd + 1) * 512],
                                start=(kd == 0), stop=(kd == ND - 1))
            b.inc(last, "pab")

        @b.sec("scalar")
        def _(sc, q=q):
            sc.wait_ge(b.sem("pab"), b.val("pab"))
            sc.activation(out=gbq[0:6, 0:512], in_=pF[0:6, 0:512], func=AF.Identity, bias=dtb[:])
            sc.activation(out=gbq[0:6, 512:1024], in_=pE2[0:6, 0:512], func=AF.Sigmoid)

        @b.sec("vector")
        def _(v, q=q):
            b.we(v, "scalar")
            v.tensor_scalar(out=gpp[0:6, 0:512], in0=gbq[0:6, 0:512], scalar1=0.0, scalar2=None, op0=ALU.max)

        @b.sec("scalar")
        def _(sc, q=q):
            b.we(sc, "vector")
            sc.activation(out=gbq[0:6, 0:512], in_=gbq[0:6, 0:512], func=AF.Abs)
            sc.activation(out=gbq[0:6, 0:512], in_=gbq[0:6, 0:512], func=AF.Exp, scale=-1.0)
            sc.activation(out=gbq[0:6, 0:512], in_=gbq[0:6, 0:512], func=AF.Ln, bias=1.0)

        @b.sec("vector")
        def _(v, q=q):
            b.we(v, "scalar")
            v.tensor_tensor(out=gbq[0:6, 0:512], in0=gbq[0:6, 0:512], in1=gpp[0:6, 0:512], op=ALU.add)
            v.tensor_scalar(out=gbq[0:6, 0:512], in0=gbq[0:6, 0:512], scalar1=negA[:], scalar2=None, op0=ALU.mult)
            src, dst = gbq[0:6, 0:512], gpp[0:6, 0:512]
            for st_ in (1, 2, 4, 8, 16, 32, 64):
                s3 = src.rearrange("p (c t) -> p c t", t=CH)
                d3 = dst.rearrange("p (c t) -> p c t", t=CH)
                v.tensor_tensor(out=d3[:, :, st_:CH], in0=s3[:, :, st_:CH], in1=s3[:, :, 0:CH - st_], op=ALU.add)
                v.tensor_copy(out=d3[:, :, 0:st_], in_=s3[:, :, 0:st_])
                src, dst = dst, src
            v.tensor_copy(out=gbq[0:6, 0:512], in_=gpp[0:6, 0:512])

        for pname, W_d, nct, cw, dstb, goff in PROJ:
            for ct in range(nct):
                sl = ct % 2

                @b.sec("gpsimd")
                def _(gp, W_d=W_d, ct=ct, sl=sl):
                    b.w(gp, "pprj")
                    b.dma(gp, "gpsimd", wcol[sl][:].rearrange("p (k c) -> p k c", c=128),
                          W_d.rearrange("(k p) c -> p k c", p=128)[:, :, ct * 128:(ct + 1) * 128], f"lwc{sl}")

                @b.sec("tensor")
                def _(t, ct=ct, sl=sl):
                    t.wait_ge(b.sem(f"lwc{sl}"), b.val(f"lwc{sl}"))
                    b.we(t, "vector")
                    last = None
                    for kd in range(ND):
                        last = t.matmul(pB_[:, 0:512], wcol[sl][:, kd * 128:(kd + 1) * 128],
                                        xnq[:, kd * 512:(kd + 1) * 512], start=(kd == 0), stop=(kd == ND - 1))
                    b.inc(last, "pprj")

                @b.sec("vector")
                def _(v, ct=ct, cw=cw, goff=goff):
                    v.wait_ge(b.sem("pprj"), b.val("pprj"))
                    b.we(v, "scalar")
                    gidx = goff + ct
                    v.tensor_copy(out=rw[:, 4:516], in_=pB_[:, 0:512])
                    v.tensor_copy(out=rw[:, 0:4], in_=tails[:, gidx * 4:(gidx + 1) * 4])
                    v.tensor_copy(out=tails[:, gidx * 4:(gidx + 1) * 4], in_=rw[:, 512:516])
                    v.tensor_scalar(out=cacc[:], in0=rw[:, 1:513], scalar1=cw[:, ct * 4:ct * 4 + 1], scalar2=None, op0=ALU.mult)
                    for i in (1, 2, 3):
                        v.scalar_tensor_tensor(out=cacc[:], in0=rw[:, 1 + i:513 + i], scalar=cw[:, ct * 4 + i:ct * 4 + i + 1],
                                               in1=cacc[:], op0=ALU.mult, op1=ALU.add)

                @b.sec("scalar")
                def _(sc, ct=ct, dstb=dstb):
                    b.we(sc, "vector")
                    sc.activation(out=dstb[:, ct * 512:(ct + 1) * 512], in_=cacc[:], func=AF.Silu)

        for h in range(6):
            @b.sec("gpsimd")
            def _(gp, h=h):
                b.w(gp, "pgat")
                b.dma(gp, "gpsimd", wgh[:].rearrange("p (k c) -> p k c", c=512),
                      Wg_d.rearrange("(k p) c -> p k c", p=128)[:, :, h * 512:(h + 1) * 512], "lwgh")
                b.we(gp, "vector")
                b.w(gp, "huse")
                b.dma(gp, "gpsimd", gsp[0:1, 0:512], gbq[h:h + 1, 0:512], "lgs")
                b.dma(gp, "gpsimd", bsp[0:1, 0:512], gbq[h:h + 1, 512:1024], "lbs")

            for cc in range(4):
                c0 = cc * CH
                gslice = gsp[0:1, c0:c0 + CH]
                bslice = bsp[0:1, c0:c0 + CH]
                kt0 = kc[:, (2 * h) * 512 + c0:(2 * h) * 512 + c0 + CH]
                kt1 = kc[:, (2 * h + 1) * 512 + c0:(2 * h + 1) * 512 + c0 + CH]
                qt0 = qc[:, (2 * h) * 512 + c0:(2 * h) * 512 + c0 + CH]
                qt1 = qc[:, (2 * h + 1) * 512 + c0:(2 * h + 1) * 512 + c0 + CH]

                @b.sec("vector")
                def _(v, kt0=kt0, kt1=kt1, qt0=qt0, qt1=qt1):
                    b.we(v, "scalar")
                    b.w(v, "psq")
                    v.tensor_tensor(out=sqb[:, 0:CH], in0=qt0, in1=qt0, op=ALU.mult)
                    v.tensor_tensor(out=sqb[:, CH:2 * CH], in0=kt0, in1=kt0, op=ALU.mult)
                    v.tensor_tensor(out=sqb[:, 2 * CH:3 * CH], in0=qt1, in1=qt1, op=ALU.mult)
                    v.tensor_tensor(out=sqb[:, 3 * CH:4 * CH], in0=kt1, in1=kt1, op=ALU.mult)

                @b.sec("tensor")
                def _(t):
                    b.we(t, "vector")
                    t.matmul(pC[0:1, 0:256], ones[:, 0:1], sqb[:, 0:256], start=True, stop=False)
                    b.inc(t.matmul(pC[0:1, 0:256], ones[:, 0:1], sqb[:, 256:512], start=False, stop=True), "psq")

                @b.sec("scalar")
                def _(sc, gslice=gslice):
                    sc.wait_ge(b.sem("psq"), b.val("psq"))
                    sc.wait_ge(b.sem("lgs"), b.val("lgs"))
                    sc.wait_ge(b.sem("lbs"), b.val("lbs"))
                    b.we(sc, "vector")
                    sc.activation(out=lnb[0:1, 0:128], in_=pC[0:1, 0:128], func=AF.Ln, scale=256.0)
                    sc.activation(out=lnb[0:1, 128:256], in_=pC[0:1, 128:256], func=AF.Ln)
                    sc.activation(out=brow[0:1, 256:512], in_=lnb[0:1, 0:256], func=AF.Exp, scale=-0.5)
                    GC = gslice[0:1, CH - 1:CH]
                    sc.activation(out=brow[0:1, 512:640], in_=gslice, func=AF.Exp, scale=-1.0, bias=GC)
                    sc.activation(out=brow[0:1, 640:641], in_=GC, func=AF.Exp)

                @b.sec("vector")
                def _(v, gslice=gslice, bslice=bslice, h=h):
                    b.we(v, "scalar")
                    v.wait_ge(b.sem("lgs"), b.val("lgs"))
                    v.wait_ge(b.sem("lbs"), b.val("lbs"))
                    v.tensor_copy(out=brow[0:1, 0:CH], in_=gslice)
                    v.tensor_copy(out=brow[0:1, CH:2 * CH], in_=bslice)

                @b.sec("tensor")
                def _(t):
                    b.we(t, "vector")
                    t.matmul(pA[:, 0:512], ones[0:1, :], brow[0:1, 0:512], start=True, stop=True)
                    b.inc(t.matmul(pB_[:, 0:256], ones[0:1, :], brow[0:1, 512:768], start=True, stop=True), "pbc")

                @b.sec("vector")
                def _(v):
                    v.wait_ge(b.sem("pbc"), b.val("pbc"))
                    v.tensor_copy(out=bce[:, 0:512], in_=pA[:, 0:512])
                    v.tensor_copy(out=bce[:, 512:768], in_=pB_[:, 0:256])

                @b.sec("tensor")
                def _(t):
                    b.we(t, "vector")
                    t.transpose(pA[:, 0:128], bce[:, 0:128], identf[:])
                    b.inc(t.transpose(pB_[:, 0:128], bce[:, 128:256], identf[:]), "pgc")

                @b.sec("vector")
                def _(v, qt0=qt0, qt1=qt1, kt0=kt0, kt1=kt1):
                    v.wait_ge(b.sem("pgc"), b.val("pgc"))
                    v.tensor_copy(out=colb[:], in_=pA[:, 0:128])
                    v.tensor_copy(out=colb2[:], in_=pB_[:, 0:128])
                    v.tensor_tensor(out=qhT[:, 0:CH], in0=qt0, in1=bce[:, 256:384], op=ALU.mult)
                    v.tensor_tensor(out=qhT[:, CH:2 * CH], in0=qt1, in1=bce[:, 256:384], op=ALU.mult)
                    v.tensor_tensor(out=khT[:, 0:CH], in0=kt0, in1=bce[:, 384:512], op=ALU.mult)
                    v.tensor_tensor(out=khT[:, CH:2 * CH], in0=kt1, in1=bce[:, 384:512], op=ALU.mult)
                    v.tensor_scalar(out=EB[:], in0=bce[:, 0:CH], scalar1=colb[:, 0:1], scalar2=None, op0=ALU.subtract)
                    v.tensor_scalar(out=EB[:], in0=EB[:], scalar1=-87.0, scalar2=87.0, op0=ALU.max, op1=ALU.min)
                    v.tensor_scalar(out=betan[:], in0=colb2[:, 0:1], scalar1=-1.0, scalar2=None, op0=ALU.mult)

                @b.sec("scalar")
                def _(sc):
                    b.we(sc, "vector")
                    sc.activation(out=eGb[:], in_=bce[:, 0:CH], func=AF.Exp)
                    sc.activation(out=eGc[:], in_=colb[:, 0:1], func=AF.Exp)
                    sc.activation(out=e1b[:], in_=EB[:], func=AF.Exp)
                    sc.activation(out=EB[:], in_=EB[:], func=AF.Exp, scale=-1.0)

                @b.sec("tensor")
                def _(t, h=h, c0=c0):
                    b.we(t, "scalar")
                    t.matmul(pC[0:128, 0:128], khT[:, 0:CH], khT[:, 0:CH], start=True, stop=False)
                    t.matmul(pC[0:128, 0:128], khT[:, CH:2 * CH], khT[:, CH:2 * CH], start=False, stop=True)
                    t.matmul(pC[0:128, 128:256], khT[:, 0:CH], qhT[:, 0:CH], start=True, stop=False)
                    b.inc(t.matmul(pC[0:128, 128:256], khT[:, CH:2 * CH], qhT[:, CH:2 * CH], start=False, stop=True), "pkk")
                    t.matmul(pD[0:128, 0:512], khT[:, 0:CH], Sbf[:, (h * 2) * 512:(h * 2 + 1) * 512], start=True, stop=False)
                    b.inc(t.matmul(pD[0:128, 0:512], khT[:, CH:2 * CH], Sbf[:, (h * 2 + 1) * 512:(h * 2 + 2) * 512], start=False, stop=True), "pks")
                    inst = None
                    for i in range(4):
                        inst = t.transpose(pTb[0:128, i * 128:(i + 1) * 128],
                                           vc[:, (4 * h + i) * 512 + c0:(4 * h + i) * 512 + c0 + CH], ident[:])
                    b.inc(inst, "pvt")

                @b.sec("vector")
                def _(v, cc=cc):
                    v.wait_ge(b.sem("pvt"), b.val("pvt"))
                    v.wait_ge(b.sem("pkk"), b.val("pkk"))
                    v.wait_ge(b.sem("pks"), b.val("pks"))
                    v.tensor_copy(out=VT[:], in_=pTb[0:128, 0:512])
                    v.tensor_copy(out=kkb[:], in_=pC[0:128, 0:128])
                    v.tensor_copy(out=kqb[:], in_=pC[0:128, 128:256])
                    v.tensor_tensor(out=Einc[:], in0=e1b[:], in1=up_incl[0:128, :], op=ALU.mult)
                    v.tensor_tensor(out=tf1[:], in0=e1b[:], in1=kkb[:], op=ALU.mult)
                    v.tensor_tensor(out=tf2[:], in0=EB[:], in1=kkb[:], op=ALU.mult)
                    v.tensor_tensor(out=e1b[:], in0=tf1[:], in1=bd_up[0:128, :], op=ALU.mult)
                    v.tensor_tensor(out=e1b[:], in0=e1b[:], in1=bce[:, 128:256], op=ALU.mult)
                    v.tensor_scalar(out=bdTA[:], in0=e1b[:], scalar1=-1.0, scalar2=None, op0=ALU.mult)
                    v.tensor_tensor(out=e1b[:], in0=tf2[:], in1=bd_lo[0:128, :], op=ALU.mult)
                    v.tensor_scalar(out=bdA[:], in0=e1b[:], scalar1=betan[:], scalar2=None, op0=ALU.mult)
                    v.tensor_tensor(out=e1b[:], in0=tf2[:], in1=ob_lo[0:128, :], op=ALU.mult)
                    v.tensor_scalar(out=alb[:], in0=e1b[:], scalar1=colb2[:, 0:1], scalar2=None, op0=ALU.mult)
                    v.tensor_tensor(out=tdb[:], in0=bdA[:], in1=eye128[0:128, :], op=ALU.add)
                    v.tensor_tensor(out=tdTb[:], in0=bdTA[:], in1=eye128[0:128, :], op=ALU.add)
                    v.tensor_scalar(out=ubf[:], in0=pD[0:128, 0:512], scalar1=eGc[:], scalar2=None, op0=ALU.mult)
                    v.tensor_tensor(out=ubf[:], in0=VT[:], in1=ubf[:], op=ALU.subtract)
                    v.tensor_scalar(out=ubf[:], in0=ubf[:], scalar1=colb2[:, 0:1], scalar2=None, op0=ALU.mult)
                    v.tensor_copy(out=ub[:], in_=ubf[:])

                for lvl in range(3):
                    src_bd, src_bdT = (bdA, bdTA) if lvl % 2 == 0 else (bdB, bdTB)
                    dst_bd, dst_bdT = (bdB, bdTB) if lvl % 2 == 0 else (bdA, bdTA)

                    @b.sec("tensor")
                    def _(t, src_bd=src_bd, src_bdT=src_bdT):
                        b.we(t, "vector")
                        t.matmul(pC[0:128, 0:128], src_bdT[:], src_bd[:], start=True, stop=True)
                        b.inc(t.matmul(pC[0:128, 128:256], src_bd[:], src_bdT[:], start=True, stop=True), "pbs")

                    @b.sec("vector")
                    def _(v, dst_bd=dst_bd, dst_bdT=dst_bdT):
                        v.wait_ge(b.sem("pbs"), b.val("pbs"))
                        v.tensor_copy(out=dst_bd[:], in_=pC[0:128, 0:128])
                        v.tensor_copy(out=dst_bdT[:], in_=pC[0:128, 128:256])

                    @b.sec("tensor")
                    def _(t, dst_bd=dst_bd):
                        b.we(t, "vector")
                        t.matmul(pC[0:128, 0:128], tdTb[:], dst_bd[:], start=True, stop=True)
                        b.inc(t.matmul(pC[0:128, 128:256], dst_bd[:], tdTb[:], start=True, stop=True), "pbs")

                    @b.sec("vector")
                    def _(v):
                        v.wait_ge(b.sem("pbs"), b.val("pbs"))
                        v.tensor_tensor(out=tdb[:], in0=tdb[:], in1=pC[0:128, 0:128], op=ALU.add)
                        v.tensor_tensor(out=tdTb[:], in0=tdTb[:], in1=pC[0:128, 128:256], op=ALU.add)

                @b.sec("tensor")
                def _(t):
                    b.we(t, "vector")
                    t.matmul(pC[0:128, 0:128], tdTb[:], alb[:], start=True, stop=True)
                    b.inc(t.matmul(pC[0:128, 128:256], alb[:], tdTb[:], start=True, stop=True), "pbs")

                @b.sec("vector")
                def _(v):
                    v.wait_ge(b.sem("pbs"), b.val("pbs"))
                    v.tensor_scalar(out=nb[:], in0=pC[0:128, 0:128], scalar1=-1.0, scalar2=None, op0=ALU.mult)
                    v.tensor_scalar(out=nbT[:], in0=pC[0:128, 128:256], scalar1=-1.0, scalar2=None, op0=ALU.mult)

                @b.sec("tensor")
                def _(t):
                    b.we(t, "vector")
                    t.matmul(pC[0:128, 0:128], nbT[:], nb[:], start=True, stop=True)
                    b.inc(t.matmul(pC[0:128, 128:256], nb[:], nbT[:], start=True, stop=True), "pbs")

                @b.sec("vector")
                def _(v):
                    v.wait_ge(b.sem("pbs"), b.val("pbs"))
                    v.tensor_copy(out=n2b[:], in_=pC[0:128, 0:128])
                    v.tensor_copy(out=n2bT[:], in_=pC[0:128, 128:256])

                @b.sec("tensor")
                def _(t):
                    b.we(t, "vector")
                    t.matmul(pC[0:128, 0:128], n2bT[:], n2b[:], start=True, stop=True)
                    b.inc(t.matmul(pC[0:128, 128:256], n2b[:], n2bT[:], start=True, stop=True), "pbs")

                @b.sec("vector")
                def _(v):
                    v.wait_ge(b.sem("pbs"), b.val("pbs"))
                    v.tensor_copy(out=n4b[:], in_=pC[0:128, 0:128])
                    v.tensor_copy(out=n4bT[:], in_=pC[0:128, 128:256])

                for step, lmat in enumerate((tdTb, nbT, n2bT, n4bT)):
                    @b.sec("tensor")
                    def _(t, lmat=lmat):
                        b.we(t, "vector")
                        b.inc(t.matmul(pD[0:128, 0:512], lmat[:], ub[:], start=True, stop=True), "puc")

                    @b.sec("vector")
                    def _(v, step=step):
                        v.wait_ge(b.sem("puc"), b.val("puc"))
                        if step == 0:
                            v.tensor_copy(out=ubf[:], in_=pD[0:128, 0:512])
                        else:
                            v.tensor_tensor(out=ubf[:], in0=ubf[:], in1=pD[0:128, 0:512], op=ALU.add)
                        v.tensor_copy(out=ub[:], in_=ubf[:])

                @b.sec("vector")
                def _(v):
                    v.tensor_tensor(out=qhT[:, 0:CH], in0=qhT[:, 0:CH], in1=eGb[:], op=ALU.mult)
                    v.tensor_tensor(out=qhT[:, CH:2 * CH], in0=qhT[:, CH:2 * CH], in1=eGb[:], op=ALU.mult)
                    v.tensor_tensor(out=w2b[:], in0=kqb[:], in1=Einc[:], op=ALU.mult)

                @b.sec("tensor")
                def _(t, h=h, c0=c0):
                    b.we(t, "vector")
                    t.wait_ge(b.sem("lwgh"), b.val("lwgh"))
                    t.matmul(pF[0:128, 0:512], qhT[:, 0:CH], Sbf[:, (h * 2) * 512:(h * 2 + 1) * 512], start=True, stop=False)
                    t.matmul(pF[0:128, 0:512], qhT[:, CH:2 * CH], Sbf[:, (h * 2 + 1) * 512:(h * 2 + 2) * 512], start=False, stop=False)
                    b.inc(t.matmul(pF[0:128, 0:512], w2b[:], ub[:], start=False, stop=True), "po")
                    last = None
                    for kd in range(ND):
                        last = t.matmul(pE2[0:128, 0:512], xnq[:, kd * 512 + c0:kd * 512 + c0 + CH],
                                        wgh[:, kd * 512:(kd + 1) * 512], start=(kd == 0), stop=(kd == ND - 1))
                    b.inc(last, "pgat")

                @b.sec("vector")
                def _(v):
                    v.wait_ge(b.sem("po"), b.val("po"))
                    v.tensor_copy(out=obx[:], in_=pF[0:128, 0:512])
                    v.tensor_tensor(out=sqd[:], in0=obx[:], in1=obx[:], op=ALU.mult)
                    v.tensor_reduce(out=rs_[:, 0:1], in_=sqd[:], axis=AX, op=ALU.add)
                    v.tensor_scalar(out=rs_[:, 1:2], in0=rs_[:, 0:1], scalar1=1.0 / 512, scalar2=1e-5, op0=ALU.mult, op1=ALU.add)

                @b.sec("scalar")
                def _(sc):
                    b.we(sc, "vector")
                    sc.activation(out=rs_[:, 2:3], in_=rs_[:, 1:2], func=AF.Ln)
                    sc.activation(out=rs_[:, 3:4], in_=rs_[:, 2:3], func=AF.Exp, scale=-0.5)
                    sc.wait_ge(b.sem("pgat"), b.val("pgat"))
                    sc.activation(out=gt_[:], in_=pE2[0:128, 0:512], func=AF.Silu)

                @b.sec("vector")
                def _(v):
                    b.we(v, "scalar")
                    v.tensor_scalar(out=obx[:], in0=obx[:], scalar1=rs_[:, 3:4], scalar2=None, op0=ALU.mult)
                    v.tensor_tensor(out=obx[:], in0=obx[:], in1=nwb[0:128, :], op=ALU.mult)
                    v.tensor_tensor(out=ogb[:], in0=obx[:], in1=gt_[:], op=ALU.mult)
                    v.tensor_tensor(out=ksc[:, 0:CH], in0=khT[:, 0:CH], in1=bce[:, 512:640], op=ALU.mult)
                    v.tensor_tensor(out=ksc[:, CH:2 * CH], in0=khT[:, CH:2 * CH], in1=bce[:, 512:640], op=ALU.mult)

                @b.sec("tensor")
                def _(t):
                    b.we(t, "vector")
                    inst = None
                    for i in range(4):
                        inst = t.transpose(pTb[0:128, i * 128:(i + 1) * 128], ogb[:, i * 128:(i + 1) * 128], ident[:])
                    b.inc(inst, "pog")

                @b.sec("vector")
                def _(v, h=h, c0=c0):
                    v.wait_ge(b.sem("pog"), b.val("pog"))
                    for i in range(4):
                        v.tensor_copy(out=ogT[:, (4 * h + i) * 512 + c0:(4 * h + i) * 512 + c0 + CH],
                                      in_=pTb[0:128, i * 128:(i + 1) * 128])

                @b.sec("tensor")
                def _(t):
                    b.we(t, "vector")
                    t.transpose(pTb[0:128, 0:128], ksc[:, 0:CH], ident[:])
                    b.inc(t.transpose(pTb[0:128, 128:256], ksc[:, CH:2 * CH], ident[:]), "pkt2")

                @b.sec("vector")
                def _(v):
                    v.wait_ge(b.sem("pkt2"), b.val("pkt2"))
                    v.tensor_copy(out=kscT[:, 0:256], in_=pTb[0:128, 0:256])

                @b.sec("tensor")
                def _(t):
                    b.we(t, "vector")
                    b.inc(t.matmul(pA[:, 0:512], kscT[:, 0:128], ub[:], start=True, stop=True), "psu")
                    b.inc(t.matmul(pB_[:, 0:512], kscT[:, 128:256], ub[:], start=True, stop=True), "psu")

                @b.sec("vector")
                def _(v, h=h):
                    v.wait_ge(b.sem("psu"), b.val("psu"))
                    for kt, ps_ in ((0, pA), (1, pB_)):
                        ssl = Sst[:, (h * 2 + kt) * 512:(h * 2 + kt + 1) * 512]
                        v.tensor_scalar(out=ssl, in0=ssl, scalar1=bce[:, 640:641], scalar2=None, op0=ALU.mult)
                        v.tensor_tensor(out=ssl, in0=ssl, in1=ps_[:, 0:512], op=ALU.add)
                        v.tensor_copy(out=Sbf[:, (h * 2 + kt) * 512:(h * 2 + kt + 1) * 512], in_=ssl)

            @b.sec("scalar")
            def _(sc, h=h):
                b.inc(sc.nop(), "huse")

        for tg in range(4):
            @b.sec("tensor")
            def _(t, tg=tg, q=q):
                b.we(t, "vector")
                for ke in range(24):
                    sl = ke % 2

                    @b.sec("gpsimd")
                    def _(gp, ke=ke, sl=sl):
                        gp.wait_ge(b.sem(f"swo{sl}"), b.val(f"swo{sl}"))
                        if b.val("pyq") > 0:
                            gp.wait_ge(b.sem("pyq"), b.val("pyq"))
                        b.dma(gp, "gpsimd", wos[sl][:], wobf_d[ke * 128:(ke + 1) * 128, :], f"lwo{sl}")

                    t.wait_ge(b.sem(f"lwo{sl}"), b.val(f"lwo{sl}"))
                    for dch in range(4):
                        ps_ = (pC, pD, pE2, pF)[dch]
                        inst = t.matmul(ps_[:, 0:512], ogT[:, ke * 512 + tg * 128:ke * 512 + (tg + 1) * 128],
                                        wos[sl][:, dch * 512:(dch + 1) * 512], start=(ke == 0), stop=(ke == 23))
                        if dch == 3:
                            b.inc(inst, "pyq")

            @b.sec("vector")
            def _(v, tg=tg):
                v.wait_ge(b.sem("pyq"), b.val("pyq"))
                b.w(v, "ss1")
                for dch in range(4):
                    ps_ = (pC, pD, pE2, pF)[dch]
                    v.tensor_copy(out=s1t[:, dch * 512:(dch + 1) * 512], in_=ps_[:, 0:512])

            @b.sec("sync")
            def _(s, tg=tg, q0=q0):
                s.wait_ge(b.sem("ch_vector"), b.chv("vector"))
                b.dma(s, "sync", s1_d[q0 + tg * 128:q0 + (tg + 1) * 128, :], s1t[:], "ss1")
                s.wait_ge(b.sem("ss1"), b.val("ss1"))

    @b.sec("sync")
    def _(s):
        s.wait_ge(b.sem("ss1"), b.val("ss1"))


def dram_l5(nc):
    return {
        "mgw": nc.dram_tensor("mgw", [2048, 16], F32, kind="ExternalInput").ap(),
        "mebias": nc.dram_tensor("mebias", [128, 16], F32, kind="ExternalInput").ap(),
        "msgw": nc.dram_tensor("msgw", [2048, 1024], F32, kind="ExternalInput").ap(),
        "msuw": nc.dram_tensor("msuw", [2048, 1024], F32, kind="ExternalInput").ap(),
        "msdw": nc.dram_tensor("msdw", [1024, 2048], F32, kind="ExternalInput").ap(),
        "megu": nc.dram_tensor("megu", [16, 2048, 2048], F32, kind="ExternalInput").ap(),
        "medn": nc.dram_tensor("medn", [16, 1024, 2048], F32, kind="ExternalInput").ap(),
        "out": nc.dram_tensor("out", [T, D], F32, kind="ExternalOutput").ap(),
    }


def build_launch5():
    b = B()
    nc = b.nc
    Dd = dram_l5(nc)
    Dd["xnTi"] = nc.dram_tensor("xnTi", [128, ND * T], BF16, kind="ExternalInput").ap()
    Dd["x1i"] = nc.dram_tensor("x1i", [T, D], F32, kind="ExternalInput").ap()
    Dd["s1i"] = nc.dram_tensor("s1i", [T, D], BF16, kind="ExternalInput").ap()
    b.start()
    emit_phase5(b, Dd)
    b.close()
    return b


def emit_phase5(b, Dd):
    nc = b.nc
    xnT_d = Dd["xnTi"]; x1_d = Dd["x1i"]; s1_d = Dd["s1i"]
    gw_d = Dd["mgw"]; eb_d = Dd["mebias"]; sg_d = Dd["msgw"]; su_d = Dd["msuw"]
    sd_d = Dd["msdw"]; egu_d = Dd["megu"]; edn_d = Dd["medn"]; out_d = Dd["out"]

    xnTh = b.sb("xnTh", [128, ND * 1024], BF16)
    acc = b.sb("acc", [128, 8 * 2048], F32)
    wcol = [b.sb(f"wcol{i}", [128, ND * 128], BF16) for i in range(2)]
    dnb = b.sb("dnb", [128, 8 * 2048], BF16)
    itb = b.sb("itb", [128, 8 * 1024], BF16)
    uf_ = b.sb("uf_", [128, 1024], F32)
    gw_sb = b.sb("gw_sb", [128, ND * 16], BF16)
    ebb = b.sb("ebb", [128, 16], F32)
    scs = b.sb("scs_sb", [128, 8 * 16], F32)
    comb = b.sb("comb_sb", [128, 8 * 16], F32)
    rstat = b.sb("rstat", [128, 16], F32)
    xt = b.sb("xt", [128, D], F32)
    s1t = b.sb("s1t", [128, D], BF16)
    ot = b.sb("ot", [128, D], F32)

    pg = [b.ps(f"pg{i}", [128, 512], F32) for i in range(4)]
    pd = [b.ps(f"pd{i}", [128, 512], F32) for i in range(2)]
    pgmark = [0] * 4
    rtmark = [0] * 2
    pdmark = [0] * 2
    wcmark = [0, 0]
    dnmark = [0]

    @b.sec("gpsimd")
    def _(gp):
        b.dma(gp, "gpsimd", gw_sb[:].rearrange("p (k e) -> p k e", e=16),
              gw_d.rearrange("(k p) e -> p k e", p=128), "lgw")

    @b.sec("sync")
    def _(s):
        b.dma(s, "sync", ebb[:], eb_d, "leb")

    xnT3 = xnT_d.rearrange("p (k t) -> p k t", t=T)

    for half in range(2):
        th0 = half * 1024

        @b.sec("sync")
        def _(s, th0=th0):
            b.w(s, "pgu")
            b.w(s, "prt")
            b.dma(s, "sync", xnTh[:].rearrange("p (k t) -> p k t", t=1024),
                  xnT3[:, :, th0:th0 + 1024], "lxh")

        for tb in range(8):
            @b.sec("tensor")
            def _(t, tb=tb):
                t.wait_ge(b.sem("lxh"), b.val("lxh"))
                t.wait_ge(b.sem("lgw"), b.val("lgw"))
                if rtmark[tb % 2] > 0:
                    t.wait_ge(b.sem("ch_scalar"), rtmark[tb % 2])
                b.we(t, "vector") if tb < 2 else None
                last = None
                for kd in range(ND):
                    last = t.matmul(pg[tb % 2][:, 0:16], xnTh[:, kd * 1024 + tb * 128:kd * 1024 + (tb + 1) * 128],
                                    gw_sb[:, kd * 16:(kd + 1) * 16], start=(kd == 0), stop=(kd == ND - 1))
                b.inc(last, "prt")

            @b.sec("scalar")
            def _(sc, tb=tb):
                sc.wait_ge(b.sem("prt"), b.val("prt"))
                sc.wait_ge(b.sem("leb"), b.val("leb"))
                sc.activation(out=scs[:, tb * 16:(tb + 1) * 16], in_=pg[tb % 2][:, 0:16], func=AF.Sigmoid)
                rtmark[tb % 2] = b.chv("scalar")

            @b.sec("vector")
            def _(v, tb=tb):
                b.we(v, "scalar")
                sl = scs[:, tb * 16:(tb + 1) * 16]
                cb = comb[:, tb * 16:(tb + 1) * 16]
                v.tensor_tensor(out=cb, in0=sl, in1=ebb[:], op=ALU.add)
                v.tensor_reduce(out=rstat[:, 0:1], in_=cb, axis=AX, op=ALU.max)
                v.tensor_scalar(out=rstat[:, 4:5], in0=rstat[:, 0:1], scalar1=None, scalar2=None, op0=ALU.bypass) if False else None
                v.tensor_scalar(out=uf_[:, 0:16], in0=cb, scalar1=rstat[:, 0:1], scalar2=None, op0=ALU.is_ge)
                v.tensor_tensor(out=uf_[:, 16:32], in0=sl, in1=uf_[:, 0:16], op=ALU.mult)
                v.tensor_reduce(out=rstat[:, 1:2], in_=uf_[:, 16:32], axis=AX, op=ALU.add)
                v.scalar_tensor_tensor(out=cb, in0=uf_[:, 0:16], scalar=-1e30, in1=cb, op0=ALU.mult, op1=ALU.add)
                v.tensor_reduce(out=rstat[:, 2:3], in_=cb, axis=AX, op=ALU.max)
                v.tensor_scalar(out=uf_[:, 32:48], in0=cb, scalar1=rstat[:, 2:3], scalar2=None, op0=ALU.is_ge)
                v.tensor_tensor(out=uf_[:, 48:64], in0=sl, in1=uf_[:, 32:48], op=ALU.mult)
                v.tensor_reduce(out=rstat[:, 3:4], in_=uf_[:, 48:64], axis=AX, op=ALU.add)
                v.tensor_tensor(out=rstat[:, 4:5], in0=rstat[:, 1:2], in1=rstat[:, 3:4], op=ALU.add)
                v.tensor_scalar(out=rstat[:, 4:5], in0=rstat[:, 4:5], scalar1=1e-20, scalar2=None, op0=ALU.add)
                v.reciprocal(out=rstat[:, 5:6], in_=rstat[:, 4:5])
                v.tensor_scalar(out=rstat[:, 6:7], in0=rstat[:, 1:2], scalar1=rstat[:, 5:6], scalar2=None, op0=ALU.mult)
                v.tensor_scalar(out=rstat[:, 7:8], in0=rstat[:, 3:4], scalar1=rstat[:, 5:6], scalar2=None, op0=ALU.mult)
                v.tensor_scalar(out=uf_[:, 0:16], in0=uf_[:, 0:16], scalar1=rstat[:, 6:7], scalar2=None, op0=ALU.mult)
                v.scalar_tensor_tensor(out=cb, in0=uf_[:, 32:48], scalar=rstat[:, 7:8], in1=uf_[:, 0:16], op0=ALU.mult, op1=ALU.add)

        @b.sec("tensor")
        def _(t):
            b.we(t, "scalar")
        for e in range(17):
            shared = (e == 16)
            for nt in range(16):
                sl = nt % 2
                @b.sec("gpsimd")
                def _(gp, e=e, nt=nt, sl=sl, shared=shared):
                    if wcmark[sl] > 0:
                        gp.wait_ge(b.sem("pgu"), wcmark[sl])
                    if shared:
                        src = (sg_d if nt < 8 else su_d).rearrange("(k p) c -> p k c", p=128)[:, :, (nt % 8) * 128:(nt % 8 + 1) * 128]
                    else:
                        src = egu_d[e].rearrange("(k p) c -> p k c", p=128)[:, :, nt * 128:(nt + 1) * 128]
                    b.dma(gp, "gpsimd", wcol[sl][:].rearrange("p (k c) -> p k c", c=128), src, f"lwc{sl}")

                for tc in range(2):
                    @b.sec("tensor")
                    def _(t, nt=nt, tc=tc, sl=sl):
                        t.wait_ge(b.sem(f"lwc{sl}"), b.val(f"lwc{sl}"))
                        pidx = (nt * 2 + tc) % 4
                        if pgmark[pidx] > 0:
                            t.wait_ge(b.sem("ch_vector"), pgmark[pidx])
                        last = None
                        for kd in range(ND):
                            last = t.matmul(pg[pidx][:, 0:512], wcol[sl][:, kd * 128:(kd + 1) * 128],
                                            xnTh[:, kd * 1024 + tc * 512:kd * 1024 + (tc + 1) * 512],
                                            start=(kd == 0), stop=(kd == ND - 1))
                        b.inc(last, "pgu")

                    @b.sec("vector")
                    def _(v, nt=nt, tc=tc, shared=shared):
                        v.wait_ge(b.sem("pgu"), b.val("pgu"))
                        b.we(v, "scalar")
                        ps_ = pg[(nt * 2 + tc) % 4]
                        if nt < 8:
                            v.tensor_copy(out=uf_[:, 0:512], in_=ps_[:, 0:512])
                        else:
                            v.tensor_tensor(out=itb[:, (nt - 8) * 1024 + tc * 512:(nt - 8) * 1024 + (tc + 1) * 512],
                                            in0=itb[:, (nt - 8) * 1024 + tc * 512:(nt - 8) * 1024 + (tc + 1) * 512],
                                            in1=ps_[:, 0:512], op=ALU.mult)
                        pgmark[(nt * 2 + tc) % 4] = b.chv("vector")

                    if nt < 8:
                        @b.sec("scalar")
                        def _(sc, nt=nt, tc=tc):
                            b.we(sc, "vector")
                            sc.activation(out=itb[:, nt * 1024 + tc * 512:nt * 1024 + (tc + 1) * 512],
                                          in_=uf_[:, 0:512], func=AF.Silu)
                wcmark[sl] = b.val("pgu")

            @b.sec("gpsimd")
            def _(gp, e=e, shared=shared):
                if dnmark[0] > 0:
                    gp.wait_ge(b.sem("pdn"), dnmark[0])
                src = sd_d if shared else edn_d[e]
                b.dma(gp, "gpsimd", dnb[:].rearrange("p (k c) -> p k c", c=2048),
                      src.rearrange("(k p) c -> p k c", p=128), "ldn")

            for tb in range(8):
                for dch in range(4):
                    @b.sec("tensor")
                    def _(t, tb=tb, dch=dch):
                        t.wait_ge(b.sem("ldn"), b.val("ldn"))
                        if tb == 0 and dch == 0:
                            b.we(t, "scalar")
                            b.we(t, "vector")
                        pidx = (tb * 4 + dch) % 2
                        if pdmark[pidx] > 0:
                            t.wait_ge(b.sem("ch_vector"), pdmark[pidx])
                        last = None
                        for kf in range(8):
                            last = t.matmul(pd[(tb * 4 + dch) % 2][:, 0:512],
                                            itb[:, kf * 1024 + tb * 128:kf * 1024 + (tb + 1) * 128],
                                            dnb[:, kf * 2048 + dch * 512:kf * 2048 + (dch + 1) * 512],
                                            start=(kf == 0), stop=(kf == 7))
                        b.inc(last, "pdn")

                    @b.sec("vector")
                    def _(v, tb=tb, dch=dch, e=e, shared=shared):
                        v.wait_ge(b.sem("pdn"), b.val("pdn"))
                        ps_ = pd[(tb * 4 + dch) % 2]
                        pdmark[(tb * 4 + dch) % 2] = b.chv("vector") + 1
                        asl = acc[:, tb * 2048 + dch * 512:tb * 2048 + (dch + 1) * 512]
                        if e == 0:
                            v.tensor_scalar(out=asl, in0=ps_[:, 0:512], scalar1=comb[:, tb * 16:tb * 16 + 1], scalar2=None, op0=ALU.mult)
                        elif shared:
                            v.tensor_tensor(out=asl, in0=asl, in1=ps_[:, 0:512], op=ALU.add)
                        else:
                            v.scalar_tensor_tensor(out=asl, in0=ps_[:, 0:512], scalar=comb[:, tb * 16 + e:tb * 16 + e + 1],
                                                   in1=asl, op0=ALU.mult, op1=ALU.add)
            dnmark[0] = b.val("pdn")

        for tb in range(8):
            grow = th0 + tb * 128

            @b.sec("sync")
            def _(s, grow=grow):
                b.we(s, "vector")
                b.dma(s, "sync", xt[:], x1_d[grow:grow + 128, :], "lxt")
                b.dma(s, "sync", s1t[:], s1_d[grow:grow + 128, :], "ls1")

            @b.sec("scalar")
            def _(sc, tb=tb):
                b.we(sc, "vector")
                sc.activation(out=s1t[:].bitcast(BF16) if False else xt[:], in_=acc[:, tb * 2048:(tb + 1) * 2048], func=AF.Silu) if False else None
                sc.wait_ge(b.sem("lxt"), b.val("lxt"))

            @b.sec("vector")
            def _(v, tb=tb):
                v.wait_ge(b.sem("lxt"), b.val("lxt"))
                v.wait_ge(b.sem("ls1"), b.val("ls1"))
                b.w(v, "sout")

            @b.sec("scalar")
            def _(sc, tb=tb):
                sc.activation(out=acc[:, tb * 2048:(tb + 1) * 2048], in_=acc[:, tb * 2048:(tb + 1) * 2048], func=AF.Silu)
                b.inc(sc.nop(), "s2v") if False else None

            @b.sec("vector")
            def _(v, tb=tb):
                b.we(v, "scalar")
                v.tensor_tensor(out=ot[:], in0=s1t[:], in1=acc[:, tb * 2048:(tb + 1) * 2048], op=ALU.mult)
                v.tensor_tensor(out=ot[:], in0=ot[:], in1=xt[:], op=ALU.add)

            @b.sec("sync")
            def _(s, grow=grow):
                b.we(s, "vector")
                b.dma(s, "sync", out_d[grow:grow + 128, :], ot[:], "sout")
                s.wait_ge(b.sem("sout"), b.val("sout"))

    @b.sec("sync")
    def _(s):
        s.wait_ge(b.sem("sout"), b.val("sout"))


def build_fused():
    b = B()
    nc = b.nc
    D1 = dram_l1(nc, "Internal")
    D2 = dram_l2(nc, "Internal", with_ident=False)
    D2["ident"] = D1["ident"]
    D4 = dram_l4(nc, "Internal")
    D4["ident"] = D1["ident"]
    D5 = dram_l5(nc)
    x1_t = nc.dram_tensor("x1", [T, D], F32, kind="Internal").ap()
    xnT1_t = nc.dram_tensor("xnT1", [128, ND * T], BF16, kind="Internal").ap()
    P3 = {"x": D1["x"], "xnTi": D1["xnT"], "s0i": D1["s0"],
          "WgT": D2["WgT"], "WuT": D2["WuT"], "Wdd": D2["Wdd"],
          "ident": D1["ident"], "x1": x1_t, "xnT1": xnT1_t}
    P4 = dict(D4)
    P4["xnTi"] = xnT1_t
    P5 = dict(D5)
    P5.update({"xnTi": xnT1_t, "x1i": x1_t, "s1i": D4["s1"]})
    b.start()
    for emit, Dd in ((emit_phase1, D1), (emit_phase2, D2), (emit_phase3, P3),
                     (emit_phase4, P4), (emit_phase5, P5)):
        b.push_scope()
        emit(b, Dd)
        b.pop_scope()
    b.close()
    return b


def _fused_inmaps(inputs):
    consts = make_consts()
    cpk = np.zeros((128, CPK_W), dtype=np.float32)
    cpk[:, 0:2048] = consts["ropec"].reshape(16, 128, 128).transpose(1, 0, 2).reshape(128, 2048)
    cpk[:, 2048:4096] = consts["ropes"].reshape(16, 128, 128).transpose(1, 0, 2).reshape(128, 2048)
    cpk[:, 4096:4224] = consts["causal"]
    cpk[:, 4224:5376] = consts["para"]
    cpk[:, 5376:5504] = np.broadcast_to(inputs["L0_attn_gamma_q"], (128, 128)).astype(np.float32)
    cpk[:, 5504:5632] = np.broadcast_to(inputs["L0_attn_gamma_k"], (128, 128)).astype(np.float32)
    ebb = np.ascontiguousarray(np.broadcast_to(inputs["L1_ffn_expert_bias"], (128, 16)).astype(np.float32))
    im = []
    for c in range(NCORES):
        im.append({
            "x": np.ascontiguousarray(inputs["L0_attn_x"][c]),
            "wq": inputs["L0_attn_wq"], "wk": inputs["L0_attn_wk"],
            "wv": inputs["L0_attn_wv"], "wo": inputs["L0_attn_wo"],
            "cpk": cpk, "ident": consts["ident"], "identf": consts["identf"],
            "cp2": consts["cp2"],
            "gWp": inputs["L0_ffn_gate_W_packed"], "gSc": inputs["L0_ffn_gate_scales"],
            "uWp": inputs["L0_ffn_up_W_packed"], "uSc": inputs["L0_ffn_up_scales"],
            "dWp": inputs["L0_ffn_down_W_packed"], "dSc": inputs["L0_ffn_down_scales"],
            "l1wq": inputs["L1_attn_Wq"], "l1wk": inputs["L1_attn_Wk"], "l1wv": inputs["L1_attn_Wv"],
            "l1wa": inputs["L1_attn_Wa"], "l1wb": inputs["L1_attn_Wb"], "l1wg": inputs["L1_attn_Wg"],
            "l1wo": inputs["L1_attn_Wo"],
            "cwq": inputs["L1_attn_conv_wq"], "cwk": inputs["L1_attn_conv_wk"], "cwv": inputs["L1_attn_conv_wv"],
            "alog": np.ascontiguousarray(inputs["L1_attn_A_log"].reshape(6, 1)),
            "dtb": np.ascontiguousarray(inputs["L1_attn_dt_bias"].reshape(6, 1)),
            "nrmw": np.ascontiguousarray(inputs["L1_attn_norm_weight"].reshape(1, 512)),
            "mgw": inputs["L1_ffn_gate_w"], "mebias": ebb,
            "msgw": inputs["L1_ffn_shared_gate_w"], "msuw": inputs["L1_ffn_shared_up_w"],
            "msdw": inputs["L1_ffn_shared_down_w"],
            "megu": inputs["L1_ffn_experts_gate_up"], "medn": inputs["L1_ffn_experts_down"],
        })
    return im


def kernel(**inputs):
    b = build_fused()
    res = run_bass_kernel_spmd(b.nc, _fused_inmaps(inputs), core_ids=list(range(NCORES)))
    out = np.stack([np.asarray(res.results[c]["out"]) for c in range(NCORES)], axis=0)
    return out


def kernel_multilaunch(**inputs):
    consts = make_consts()
    ncores = NCORES
    cores = list(range(ncores))

    b1 = build_launch1()
    r1 = run_bass_kernel_spmd(b1.nc, _launch1_inmaps(inputs), core_ids=cores).results

    b2 = build_launch2()
    r2 = run_bass_kernel_spmd(b2.nc, _launch2_inmaps(inputs), core_ids=cores).results

    b3 = build_launch3()
    im3 = []
    for c in cores:
        im3.append({
            "x": np.ascontiguousarray(inputs["L0_attn_x"][c]),
            "xnTi": np.asarray(r1[c]["xnT"]),
            "s0i": np.asarray(r1[c]["s0"]),
            "WgT": np.asarray(r2[c]["WgT"]), "WuT": np.asarray(r2[c]["WuT"]),
            "Wdd": np.asarray(r2[c]["Wdd"]),
            "ident": consts["ident"],
        })
    r3 = run_bass_kernel_spmd(b3.nc, im3, core_ids=cores).results

    b4 = build_launch4()
    im4 = []
    for c in cores:
        im4.append({
            "xnTi": np.asarray(r3[c]["xnT1"]),
            "l1wq": inputs["L1_attn_Wq"], "l1wk": inputs["L1_attn_Wk"], "l1wv": inputs["L1_attn_Wv"],
            "l1wa": inputs["L1_attn_Wa"], "l1wb": inputs["L1_attn_Wb"], "l1wg": inputs["L1_attn_Wg"],
            "l1wo": inputs["L1_attn_Wo"],
            "cwq": inputs["L1_attn_conv_wq"], "cwk": inputs["L1_attn_conv_wk"], "cwv": inputs["L1_attn_conv_wv"],
            "alog": np.ascontiguousarray(inputs["L1_attn_A_log"].reshape(6, 1)),
            "dtb": np.ascontiguousarray(inputs["L1_attn_dt_bias"].reshape(6, 1)),
            "nrmw": np.ascontiguousarray(inputs["L1_attn_norm_weight"].reshape(1, 512)),
            "cp2": consts["cp2"], "ident": consts["ident"], "identf": consts["identf"],
        })
    r4 = run_bass_kernel_spmd(b4.nc, im4, core_ids=cores).results

    b5 = build_launch5()
    ebb = np.ascontiguousarray(np.broadcast_to(inputs["L1_ffn_expert_bias"], (128, 16)).astype(np.float32))
    im5 = []
    for c in cores:
        im5.append({
            "xnTi": np.asarray(r3[c]["xnT1"]),
            "x1i": np.asarray(r3[c]["x1"]),
            "s1i": np.asarray(r4[c]["s1"]),
            "mgw": inputs["L1_ffn_gate_w"], "mebias": ebb,
            "msgw": inputs["L1_ffn_shared_gate_w"], "msuw": inputs["L1_ffn_shared_up_w"],
            "msdw": inputs["L1_ffn_shared_down_w"],
            "megu": inputs["L1_ffn_experts_gate_up"], "medn": inputs["L1_ffn_experts_down"],
        })
    r5 = run_bass_kernel_spmd(b5.nc, im5, core_ids=cores).results

    out = np.stack([np.asarray(r5[c]["out"]) for c in cores], axis=0)
    return out


# version 09 (final)
# speedup vs baseline: 1.8453x; 1.0035x over previous
import numpy as np
from contextlib import ExitStack

import ml_dtypes
import concourse.bass as bass
import concourse.mybir as mybir
from concourse.bass_utils import run_bass_kernel_spmd

F32 = mybir.dt.float32
BF16 = mybir.dt.bfloat16
I32 = mybir.dt.int32
AF = mybir.ActivationFunctionType
ALU = mybir.AluOpType
AX = mybir.AxisListType.X

NCORES = 8
T = 2048
D = 2048
NT = T // 128
ND = D // 128
WIN = 1024
RSD = 0.08838834764831845
NEG = -1.0e30


def perm_slice_params(kd):
    if kd < 8:
        return kd * 256, 2
    return (kd - 8) * 256 + 1, 2


_CONSTS = None

def make_consts():
    global _CONSTS
    if _CONSTS is not None:
        return _CONSTS
    ident = np.eye(128, dtype=ml_dtypes.bfloat16)
    Dh = 128
    inv_freq = 1.0 / (10000.0 ** (np.arange(0, Dh, 2, dtype=np.float64) / Dh))
    ang = np.arange(T, dtype=np.float64)[:, None] * inv_freq[None, :]
    cos = np.concatenate([np.cos(ang), np.cos(ang)], axis=1).astype(np.float32)
    sin = np.concatenate([np.sin(ang), np.sin(ang)], axis=1).astype(np.float32)
    sinm = sin.copy()
    sinm[:, :64] = -sinm[:, :64]
    i = np.arange(128)
    causal = np.where(i[:, None] >= i[None, :], 0.0, NEG).astype(np.float32)
    ti = i[:, None]
    tj = np.arange(1152)[None, :] - 1024
    para = np.where((ti - tj >= 0) & (ti - tj <= WIN), 0.0, NEG).astype(np.float32)
    s_ = np.arange(128)
    upS = (s_[:, None] < s_[None, :]).astype(np.float32)
    bdm = ((s_[:, None] // 16) == (s_[None, :] // 16)).astype(np.float32)
    cp2 = np.zeros((128, 896), dtype=np.float32)
    cp2[:, 0:128] = (s_[:, None] <= s_[None, :]).astype(np.float32)
    cp2[:, 128:256] = upS * bdm
    cp2[:, 256:384] = 1.0
    cp2[:, 384:512] = (upS * bdm).T
    cp2[:, 512:640] = upS.T * (1 - bdm)
    cp2[:, 640:768] = np.eye(128, dtype=np.float32)
    identf = np.eye(128, dtype=np.float32)
    _CONSTS = dict(ident=ident, ropec=cos, ropes=sinm, causal=causal, para=para,
                   cp2=cp2, identf=identf)
    return _CONSTS


_WRAP_OPS = {
    "tensor_scalar", "tensor_tensor", "scalar_tensor_tensor", "tensor_copy",
    "tensor_reduce", "reciprocal", "activation", "copy", "add", "mul",
    "memset", "select", "iota", "transpose", "reduce_max", "reduce_sum",
    "tensor_tensor_scan", "copy_predicated", "match_replace", "tensor_add",
    "tensor_mul", "tensor_sub", "tensor_max", "tensor_relu",
}


class EngProxy:
    def __init__(self, b, eng, name):
        self._b = b
        self._eng = eng
        self._name = name

    def __getattr__(self, attr):
        real = getattr(self._eng, attr)
        if attr not in _WRAP_OPS:
            return real
        b, name, eng = self._b, self._name, self._eng

        def f(*a, **k):
            sem = b.sem("ch_" + name)
            n = b.ch.get(name, 0)
            if n > 0:
                eng.wait_ge(sem, n)
            inst = real(*a, **k)
            inst.then_inc(sem, 1)
            b.ch[name] = n + 1
            return inst
        return f


class B:
    ENG = ("sync", "tensor", "vector", "scalar", "gpsimd")

    def __init__(self):
        self.nc = bass.Bass("TRN2", target_bir_lowering=False, debug=False)
        self.stk = ExitStack()
        self.scopes = []
        self.sems = {}
        self.cnt = {}
        self.ch = {}
        self.nbar = 0
        self.block = None

    def push_scope(self):
        self.scopes.append(ExitStack())
        self._scope_maps = getattr(self, "_scope_maps", [])
        self._scope_maps.append({})

    def pop_scope(self):
        self.barrier()
        m = self._scope_maps.pop()
        for n, (h, klass) in m.items():
            self._pools.setdefault(klass, []).append(h)
            for q in getattr(self, "dma_by_q", {}).values():
                q.discard(n)
        self.scopes.pop().close()

    def we(self, eng, name):
        n = self.ch.get(name, 0)
        if n > 0:
            eng.wait_ge(self.sem("ch_" + name), n)

    def chv(self, name):
        return self.ch.get(name, 0)

    def start(self):
        self.block = self.stk.enter_context(self.nc.Block())

    def close(self):
        self.stk.close()

    def _uname(self, name):
        if not hasattr(self, "_names"):
            self._names = {}
        n = self._names.get(name, 0)
        self._names[name] = n + 1
        return name if n == 0 else f"{name}_{n}"

    def sb(self, name, shape, dtype):
        stk = self.scopes[-1] if self.scopes else self.stk
        return stk.enter_context(self.nc.sbuf_tensor(self._uname(name), shape, dtype))

    def ps(self, name, shape, dtype):
        stk = self.scopes[-1] if self.scopes else self.stk
        return stk.enter_context(self.nc.psum_tensor(self._uname(name), shape, dtype))

    def sem(self, name, klass="c"):
        persistent = name == "bar" or name.startswith("ch_")
        maps = getattr(self, "_scope_maps", [])
        if not hasattr(self, "allsems"):
            self.allsems = {}
        if not hasattr(self, "_pools"):
            self._pools = {}
        if persistent or not maps:
            if name not in self.sems:
                self.sems[name] = self.stk.enter_context(self.nc.semaphore(name))
            h = self.sems[name]
        else:
            m = maps[-1]
            if name not in m:
                pool = self._pools.setdefault(klass, [])
                m[name] = (pool.pop() if pool else self.stk.enter_context(self.nc.semaphore(name)), klass)
            h = m[name][0]
        self.allsems[id(h)] = h
        return h

    def _key(self, name):
        return id(self.sem(name))

    def add(self, name, n=1):
        k = self._key(name)
        self.cnt[k] = self.cnt.get(k, 0) + n
        return self.cnt[k]

    def val(self, name):
        maps = getattr(self, "_scope_maps", [])
        persistent = name == "bar" or name.startswith("ch_")
        if not persistent and maps and name not in maps[-1]:
            return 0
        return self.cnt.get(self._key(name), 0)

    def inc(self, inst, name, n=1):
        inst.then_inc(self.sem(name), n)
        return self.add(name, n)

    def w(self, eng, name):
        if self.val(name) > 0:
            eng.wait_ge(self.sem(name), self.val(name))

    def dma(self, eng, queue, out, in_, slot):
        if not hasattr(self, "dma_by_q"):
            self.dma_by_q = {}
        self.dma_by_q.setdefault(queue, set()).add(slot)
        klass = "g" if queue == "gpsimd" else "h"
        h = self.sem(slot, klass)
        if self.val(slot) > 0:
            eng.wait_ge(h, self.val(slot))
        inst = eng.dma_start(out=out, in_=in_)
        inst.then_inc(h, 16)
        self.add(slot, 16)
        return self.val(slot)

    def sec(self, engname):
        dec = getattr(self.block, engname)
        if engname in ("vector", "scalar", "gpsimd"):
            def mydec(f):
                return dec(lambda eng: f(EngProxy(self, eng, engname)))
            return mydec
        return dec

    def barrier(self):
        self.nbar += 1
        tgt = self.val("bar") + 5
        bar = self.sem("bar")
        barkey = self._key("bar")
        waits = [(self.allsems[k], v) for k, v in sorted(self.cnt.items())
                 if v > 0 and k != barkey and k in getattr(self, "allsems", {})]
        chains = [(self.sem("ch_" + n), c) for n, c in sorted(self.ch.items()) if c > 0]
        for e in self.ENG:
            @self.sec(e)
            def _(eng, e=e):
                eng = getattr(eng, "_eng", eng)
                for h, v in waits:
                    eng.wait_ge(h, v)
                for h, c in chains:
                    eng.wait_ge(h, c)
                eng.nop().then_inc(bar, 1)
                eng.wait_ge(bar, tgt)
        self.cnt[barkey] = tgt


CPK_W = 5632

def dram_l1(nc, kio):
    return {
        "x": nc.dram_tensor("x", [T, D], F32, kind="ExternalInput").ap(),
        "wq": nc.dram_tensor("wq", [2048, 2048], F32, kind="ExternalInput").ap(),
        "wk": nc.dram_tensor("wk", [512, 2048], F32, kind="ExternalInput").ap(),
        "wv": nc.dram_tensor("wv", [512, 2048], F32, kind="ExternalInput").ap(),
        "wo": nc.dram_tensor("wo", [2048, 2048], F32, kind="ExternalInput").ap(),
        "cpk": nc.dram_tensor("cpk", [128, CPK_W], F32, kind="ExternalInput").ap(),
        "ident": nc.dram_tensor("ident", [128, 128], BF16, kind="ExternalInput").ap(),
        "xnT": nc.dram_tensor("xnT", [128, ND * T], BF16, kind=kio).ap(),
        "s0": nc.dram_tensor("s0", [T, D], BF16, kind=kio).ap(),
        "oTh": nc.dram_tensor("oTh", [128, ND * T], BF16, kind="Internal").ap(),
    }


def build_launch1():
    b = B()
    Dd = dram_l1(b.nc, "ExternalOutput")
    b.start()
    emit_phase1(b, Dd)
    b.close()
    return b


def emit_phase1(b, Dd):
    nc = b.nc
    x_d = Dd["x"]; wq_d = Dd["wq"]; wk_d = Dd["wk"]; wv_d = Dd["wv"]; wo_d = Dd["wo"]
    cp_d = Dd["cpk"]; id_d = Dd["ident"]; xnT_d = Dd["xnT"]; s0_d = Dd["s0"]; oT_d = Dd["oTh"]

    xt = [b.sb(f"xt{i}", [128, D], F32) for i in range(2)]
    xc = b.sb("xc", [128, D], F32)
    stats = b.sb("stats", [128, 64], F32)
    xnb = b.sb("xnb", [128, D], BF16)
    bigA = b.sb("bigA", [128, ND * T], BF16)
    ident = b.sb("ident_sb", [128, 128], BF16)
    cpk = b.sb("cpk_sb", [128, CPK_W], F32)
    wl = [b.sb(f"wl{i}", [128, 2048], BF16) for i in range(2)]
    wkvT = b.sb("wkvT", [128, ND * 256], BF16)
    wqTh = b.sb("wqTh", [128, ND * 128], BF16)
    kb = b.sb("kb", [128, NT * 128], BF16)
    kT = b.sb("kT", [128, T], BF16)
    vg = b.sb("vg", [128, NT * 128], BF16)
    qT = b.sb("qT", [128, T], BF16)
    ssb = b.sb("ssb", [128, 1152], F32)
    pbf = [b.sb(f"pbf{i}", [128, 1152], BF16) for i in range(2)]
    pts = b.sb("ptsb", [128, 1152], BF16)
    ost = b.sb("ost", [128, 128], BF16)
    oTg = b.sb("oTg", [128, 4 * T], BF16)
    oslab = [b.sb(f"oslab{i}", [128, ND * 128], BF16) for i in range(2)]
    s0t = b.sb("s0t", [128, D], BF16)
    kf = xt[0]
    sh = xt[1]
    qf = xc
    qb = kb

    ptA = b.ps("ptA", [128, 1024], BF16)
    ptB = b.ps("ptB", [128, 1024], BF16)
    pproj = [b.ps(f"pproj{i}", [128, 512], F32) for i in range(2)]
    psc = b.ps("psc", [128, 1536], F32)
    ppv = b.ps("ppv", [128, 512], F32)
    py = psc[:, 0:512]
    pjmark = [0, 0]
    wtmark = [0, 0]

    ropec = cpk[:, 0:2048]
    ropes = cpk[:, 2048:4096]
    causal = cpk[:, 4096:4224]
    para = cpk[:, 4224:5376]
    gqb = cpk[:, 5376:5504]
    gkb = cpk[:, 5504:5632]

    wl_guards = []

    def wl_load(srcs):
        guards = list(wl_guards)
        wl_guards.clear()

        @b.sec("gpsimd")
        def _(gp):
            for s, v in guards:
                gp.wait_ge(b.sem(s), v)
            for i, src in enumerate(srcs):
                b.dma(gp, "gpsimd", wl[i][:], src, f"gw{i}")

    @b.sec("sync")
    def _(s):
        b.dma(s, "sync", ident[:], id_d, "li")
        b.dma(s, "sync", cpk[:], cp_d, "lc")

    @b.sec("vector")
    def _(v):
        v.wait_ge(b.sem("lc"), b.val("lc"))

    @b.sec("scalar")
    def _(sc):
        sc.wait_ge(b.sem("lc"), b.val("lc"))

    xv = x_d.rearrange("(n p) d -> n p d", p=128)
    vln_marks = []
    for tb in range(NT):
        @b.sec("sync")
        def _(s, tb=tb):
            if tb >= 2:
                s.wait_ge(b.sem("ch_vector"), vln_marks[tb - 2])
            b.dma(s, "sync", xt[tb % 2][:], xv[tb], f"lx{tb % 2}")

        @b.sec("vector")
        def _(v, tb=tb):
            v.wait_ge(b.sem(f"lx{tb % 2}"), b.val(f"lx{tb % 2}"))
            src = xt[tb % 2][:]
            v.tensor_reduce(out=stats[:, 0:1], in_=src, axis=AX, op=ALU.add)
            v.tensor_scalar(out=stats[:, 1:2], in0=stats[:, 0:1], scalar1=-1.0 / D, scalar2=None, op0=ALU.mult)
            v.tensor_scalar(out=xc[:], in0=src, scalar1=stats[:, 1:2], scalar2=None, op0=ALU.add)
            vln_marks.append(b.chv("vector"))

        @b.sec("scalar")
        def _(sc, tb=tb):
            b.we(sc, "vector")
            sc.activation(out=xnb[:], in_=xc[:], func=AF.Square, accum_out=stats[:, 2:3])

        @b.sec("vector")
        def _(v, tb=tb):
            b.we(v, "scalar")
            v.tensor_scalar(out=stats[:, 3:4], in0=stats[:, 2:3], scalar1=1.0 / D, scalar2=1e-5, op0=ALU.mult, op1=ALU.add)

        @b.sec("scalar")
        def _(sc, tb=tb):
            b.we(sc, "vector")
            sc.activation(out=stats[:, 4:5], in_=stats[:, 3:4], func=AF.Sqrt)

        @b.sec("vector")
        def _(v, tb=tb):
            b.we(v, "scalar")
            v.reciprocal(out=stats[:, 5:6], in_=stats[:, 4:5])
            if tb > 0:
                v.wait_ge(b.sem("pln"), 2 * tb)
            v.tensor_scalar(out=xnb[:], in0=xc[:], scalar1=stats[:, 5:6], scalar2=None, op0=ALU.mult)

        @b.sec("tensor")
        def _(t, tb=tb):
            b.we(t, "vector")
            if tb == 0:
                t.wait_ge(b.sem("li"), b.val("li"))
            b.we(t, "scalar")
            for half in range(2):
                dst = ptA if half == 0 else ptB
                for j in range(8):
                    kd = half * 8 + j
                    st0, stride = perm_slice_params(kd)
                    inst = t.transpose(dst[:, j * 128:(j + 1) * 128], xnb[:, st0:st0 + 255:stride], ident[:])
                b.inc(inst, "pln")

        @b.sec("scalar")
        def _(sc, tb=tb):
            sc.wait_ge(b.sem("pln"), b.val("pln"))
            for half in range(2):
                src = ptA if half == 0 else ptB
                dstv = bigA[:].rearrange("p (n t) -> p n t", t=T)[:, half * 8:half * 8 + 8, tb * 128:(tb + 1) * 128]
                sc.copy(out=dstv, in_=src[:].rearrange("p (n t) -> p n t", t=128))

    @b.sec("sync")
    def _(s):
        s.wait_ge(b.sem("ch_scalar"), b.chv("scalar"))
        b.dma(s, "sync", xnT_d, bigA[:], "sxnT")

    b.barrier()

    def rope_norm(strip_f32, gamma):
        @b.sec("vector")
        def _(v):
            b.we(v, "scalar")
            v3 = strip_f32[:].rearrange("p (n w) -> p n w", w=128)
            sh3 = sh[:].rearrange("p (n w) -> p n w", w=128)
            c3 = ropec.rearrange("p (n w) -> p n w", w=128)
            s3 = ropes.rearrange("p (n w) -> p n w", w=128)
            v.tensor_copy(out=sh3[:, :, 0:64], in_=v3[:, :, 64:128])
            v.tensor_copy(out=sh3[:, :, 64:128], in_=v3[:, :, 0:64])
            v.tensor_tensor(out=sh3, in0=sh3, in1=s3, op=ALU.mult)
            v.tensor_tensor(out=v3, in0=v3, in1=c3, op=ALU.mult)
            v.tensor_tensor(out=v3, in0=v3, in1=sh3, op=ALU.add)
            v.tensor_tensor(out=sh3, in0=v3, in1=v3, op=ALU.mult)
            v.tensor_reduce(out=stats[:, 8:8 + NT].unsqueeze(2), in_=sh3, axis=AX, op=ALU.add)

        @b.sec("scalar")
        def _(sc):
            b.we(sc, "vector")
            sc.activation(out=stats[:, 24:24 + NT], in_=stats[:, 8:8 + NT], func=AF.Sqrt)

        @b.sec("vector")
        def _(v):
            b.we(v, "scalar")
            v.tensor_scalar(out=stats[:, 24:24 + NT], in0=stats[:, 24:24 + NT], scalar1=1e-6, scalar2=None, op0=ALU.add)
            v.reciprocal(out=stats[:, 40:40 + NT], in_=stats[:, 24:24 + NT])

    def rope_apply(strip_f32, gamma, out_bf):
        @b.sec("vector")
        def _(v):
            v3 = strip_f32[:].rearrange("p (n w) -> p n w", w=128)
            v.tensor_tensor(out=v3, in0=v3, in1=stats[:, 40:40 + NT].unsqueeze(2).broadcast_to([128, NT, 128]), op=ALU.mult)
            ob3 = out_bf[:].rearrange("p (n w) -> p n w", w=128)
            v.tensor_tensor(out=ob3, in0=v3, in1=gamma.unsqueeze(1).broadcast_to([128, NT, 128]), op=ALU.mult)

    for g in range(4):
        wl_load([wk_d[g * 128:(g + 1) * 128, :], wv_d[g * 128:(g + 1) * 128, :]])

        @b.sec("tensor")
        def _(t, g=g):
            t.wait_ge(b.sem("gw0"), b.val("gw0"))
            t.wait_ge(b.sem("gw1"), b.val("gw1"))
            for kd in range(ND):
                st0, stride = perm_slice_params(kd)
                dst = ptA if (kd % 2 == 0) else ptB
                if wtmark[kd % 2] > 0:
                    t.wait_ge(b.sem("ch_scalar"), wtmark[kd % 2])
                t.transpose(dst[:, 0:128], wl[0][:, st0:st0 + 255:stride], ident[:])
                inst = t.transpose(dst[:, 128:256], wl[1][:, st0:st0 + 255:stride], ident[:])
                b.inc(inst, "pwt")

                @b.sec("scalar")
                def _(sc, kd=kd, dst=dst):
                    sc.wait_ge(b.sem("pwt"), b.val("pwt"))
                    sc.copy(out=wkvT[:, kd * 256:(kd + 1) * 256], in_=dst[:, 0:256])
                    wtmark[kd % 2] = b.chv("scalar")
        wl_guards.append(("pwt", b.val("pwt")))

        for tb in range(NT):
            @b.sec("tensor")
            def _(t, tb=tb, g=g):
                if tb == 0:
                    b.we(t, "scalar")
                if pjmark[tb % 2] > 0:
                    t.wait_ge(b.sem("ch_vector"), pjmark[tb % 2])
                last = None
                for kd in range(ND):
                    last = t.matmul(pproj[tb % 2][:, 0:256], bigA[:, kd * T + tb * 128:kd * T + (tb + 1) * 128],
                                    wkvT[:, kd * 256:(kd + 1) * 256], start=(kd == 0), stop=(kd == ND - 1))
                b.inc(last, "pkv")

            @b.sec("vector")
            def _(v, tb=tb, g=g):
                v.wait_ge(b.sem("pkv"), b.val("pkv"))
                if g > 0 and tb == 0:
                    b.w(v, "ppv")
                v.tensor_copy(out=kf[:, tb * 128:(tb + 1) * 128], in_=pproj[tb % 2][:, 0:128])
                v.tensor_copy(out=vg[:, tb * 128:(tb + 1) * 128], in_=pproj[tb % 2][:, 128:256])
                pjmark[tb % 2] = b.chv("vector")

        @b.sec("vector")
        def _(v, g=g):
            b.w(v, "pkt")
            b.w(v, "pqt")
        rope_norm(kf, gkb)
        rope_apply(kf, gkb, kb)

        @b.sec("tensor")
        def _(t, g=g):
            b.we(t, "vector")
            for tb in range(NT):
                dst = ptA if (tb % 2 == 0) else ptB
                if wtmark[tb % 2] > 0:
                    t.wait_ge(b.sem("ch_scalar"), wtmark[tb % 2])
                b.inc(t.transpose(dst[:, 0:128], kb[:, tb * 128:(tb + 1) * 128], ident[:]), "pkt")

                @b.sec("scalar")
                def _(sc, tb=tb, dst=dst):
                    sc.wait_ge(b.sem("pkt"), b.val("pkt"))
                    sc.copy(out=kT[:, tb * 128:(tb + 1) * 128], in_=dst[:, 0:128])
                    wtmark[tb % 2] = b.chv("scalar")

        for hh in range(4):
            h = g * 4 + hh
            wl_load([wq_d[h * 128:(h + 1) * 128, :]])

            @b.sec("tensor")
            def _(t, h=h):
                t.wait_ge(b.sem("gw0"), b.val("gw0"))
                for kd in range(ND):
                    st0, stride = perm_slice_params(kd)
                    dst = ptA if (kd % 2 == 0) else ptB
                    if wtmark[kd % 2] > 0:
                        t.wait_ge(b.sem("ch_scalar"), wtmark[kd % 2])
                    b.inc(t.transpose(dst[:, 0:128], wl[0][:, st0:st0 + 255:stride], ident[:]), "pwq")

                    @b.sec("scalar")
                    def _(sc, kd=kd, dst=dst):
                        sc.wait_ge(b.sem("pwq"), b.val("pwq"))
                        sc.copy(out=wqTh[:, kd * 128:(kd + 1) * 128], in_=dst[:, 0:128])
                        wtmark[kd % 2] = b.chv("scalar")
            wl_guards.append(("pwq", b.val("pwq")))

            for tb in range(NT):
                @b.sec("tensor")
                def _(t, tb=tb):
                    if tb == 0:
                        b.we(t, "scalar")
                    if pjmark[tb % 2] > 0:
                        t.wait_ge(b.sem("ch_vector"), pjmark[tb % 2])
                    last = None
                    for kd in range(ND):
                        last = t.matmul(pproj[tb % 2][:, 0:128], bigA[:, kd * T + tb * 128:kd * T + (tb + 1) * 128],
                                        wqTh[:, kd * 128:(kd + 1) * 128], start=(kd == 0), stop=(kd == ND - 1))
                    b.inc(last, "pq")

                @b.sec("vector")
                def _(v, tb=tb):
                    v.wait_ge(b.sem("pq"), b.val("pq"))
                    v.tensor_copy(out=qf[:, tb * 128:(tb + 1) * 128], in_=pproj[tb % 2][:, 0:128])
                    pjmark[tb % 2] = b.chv("vector")

            @b.sec("vector")
            def _(v):
                b.w(v, "pkt")
                b.w(v, "pqt")
            rope_norm(qf, gqb)
            rope_apply(qf, gqb, qb)

            @b.sec("tensor")
            def _(t, h=h):
                b.we(t, "vector")
                for tb in range(NT):
                    dst = ptA if (tb % 2 == 0) else ptB
                    if wtmark[tb % 2] > 0:
                        t.wait_ge(b.sem("ch_scalar"), wtmark[tb % 2])
                    b.inc(t.transpose(dst[:, 0:128], qb[:, tb * 128:(tb + 1) * 128], ident[:]), "pqt")

                    @b.sec("scalar")
                    def _(sc, tb=tb, dst=dst):
                        sc.wait_ge(b.sem("pqt"), b.val("pqt"))
                        sc.copy(out=qT[:, tb * 128:(tb + 1) * 128], in_=dst[:, 0:128])
                        wtmark[tb % 2] = b.chv("scalar")

            sttmark = [0] * NT
            expmark = [0] * NT
            eptmark = [0] * NT
            vomark = [0] * NT
            eotmark = [0] * NT
            summark = [0] * NT

            def blk(tb):
                t0 = tb * 128
                ks0 = 0 if t0 < WIN else t0 - WIN
                ncols = t0 + 128 if t0 < WIN else 1152
                return t0, ks0, ncols, ncols // 128

            def emit_scores(tb):
                t0, ks0, ncols, nkt = blk(tb)

                @b.sec("tensor")
                def _(t, tb=tb, ks0=ks0, ncols=ncols):
                    if tb == 0:
                        b.we(t, "scalar")
                    if tb > 0:
                        t.wait_ge(b.sem("ch_vector"), sttmark[tb - 1])
                    c0 = 0
                    last = None
                    while c0 < ncols:
                        cw = min(512, ncols - c0)
                        last = t.matmul(psc[:, c0:c0 + cw], qT[:, tb * 128:(tb + 1) * 128],
                                        kT[:, ks0 + c0:ks0 + c0 + cw], start=True, stop=True)
                        c0 += cw
                    b.inc(last, "psc")

                @b.sec("vector")
                def _(v, tb=tb, ncols=ncols):
                    v.wait_ge(b.sem("psc"), b.val("psc"))
                    if tb > 0:
                        v.wait_ge(b.sem("ch_scalar"), expmark[tb - 1])
                    if tb < 8:
                        if ncols > 128:
                            v.tensor_scalar(out=ssb[:, 0:ncols - 128], in0=psc[:, 0:ncols - 128], scalar1=RSD, scalar2=None, op0=ALU.mult)
                        v.scalar_tensor_tensor(out=ssb[:, ncols - 128:ncols], in0=psc[:, ncols - 128:ncols],
                                               scalar=RSD, in1=causal, op0=ALU.mult, op1=ALU.add)
                    else:
                        v.scalar_tensor_tensor(out=ssb[:, 0:ncols], in0=psc[:, 0:ncols],
                                               scalar=RSD, in1=para, op0=ALU.mult, op1=ALU.add)
                    sttmark[tb] = b.chv("vector")
                    v.tensor_reduce(out=stats[:, 0:1], in_=ssb[:, 0:ncols], axis=AX, op=ALU.max)
                    v.tensor_scalar(out=stats[:, tb % 2 * 8 + 6:tb % 2 * 8 + 7], in0=stats[:, 0:1], scalar1=-1.0, scalar2=None, op0=ALU.mult)

                @b.sec("scalar")
                def _(sc, tb=tb, ncols=ncols):
                    b.we(sc, "vector")
                    if tb >= 2:
                        sc.wait_ge(b.sem("ppt"), ptmark[tb - 2])
                    sc.activation(out=pbf[tb % 2][:, 0:ncols], in_=ssb[:, 0:ncols], func=AF.Exp,
                                  bias=stats[:, tb % 2 * 8 + 6:tb % 2 * 8 + 7], scale=1.0)
                    expmark[tb] = b.chv("scalar")

                @b.sec("vector")
                def _(v, tb=tb, ncols=ncols):
                    v.wait_ge(b.sem("ch_scalar"), expmark[tb])
                    v.tensor_reduce(out=stats[:, 2:3], in_=pbf[tb % 2][:, 0:ncols], axis=AX, op=ALU.add)
                    v.reciprocal(out=stats[:, tb % 2 * 8 + 7:tb % 2 * 8 + 8], in_=stats[:, 2:3])
                    summark[tb] = b.chv("vector")

            ptmark = [0] * NT

            def emit_pv(tb, hh):
                t0, ks0, ncols, nkt = blk(tb)

                @b.sec("tensor")
                def _(t, tb=tb, ks0=ks0, nkt=nkt):
                    t.wait_ge(b.sem("ch_scalar"), expmark[tb])
                    for j in range(nkt):
                        dst = ptA if (j % 2 == 0) else ptB
                        b.we(t, "scalar")
                        b.inc(t.transpose(dst[:, 0:128], pbf[tb % 2][:, j * 128:(j + 1) * 128], ident[:]), "ppt")

                        @b.sec("scalar")
                        def _(sc, j=j, dst=dst):
                            sc.wait_ge(b.sem("ppt"), b.val("ppt"))
                            sc.copy(out=pts[:, j * 128:(j + 1) * 128], in_=dst[:, 0:128])
                    ptmark[tb] = b.val("ppt")

                    b.we(t, "scalar")
                    if tb > 0:
                        t.wait_ge(b.sem("ch_vector"), vomark[tb - 1])
                    last = None
                    for j in range(nkt):
                        last = t.matmul(ppv[:, 0:128], pts[:, j * 128:(j + 1) * 128],
                                        vg[:, (ks0 // 128 + j) * 128:(ks0 // 128 + j + 1) * 128],
                                        start=(j == 0), stop=(j == nkt - 1))
                    b.inc(last, "ppv")

                @b.sec("vector")
                def _(v, tb=tb):
                    v.wait_ge(b.sem("ppv"), b.val("ppv"))
                    v.wait_ge(b.sem("ch_vector"), summark[tb])
                    b.w(v, "pot")
                    v.tensor_scalar(out=ost[:], in0=ppv[:, 0:128], scalar1=stats[:, tb % 2 * 8 + 7:tb % 2 * 8 + 8], scalar2=None, op0=ALU.mult)
                    vomark[tb] = b.chv("vector")

                @b.sec("tensor")
                def _(t, tb=tb):
                    t.wait_ge(b.sem("ch_vector"), vomark[tb])
                    if tb > 0:
                        t.wait_ge(b.sem("ch_scalar"), eotmark[tb - 1])
                    b.inc(t.transpose(ptA[:, 896:1024], ost[:], ident[:]), "pot")

                @b.sec("scalar")
                def _(sc, tb=tb, hh=hh):
                    sc.wait_ge(b.sem("pot"), b.val("pot"))
                    b.w(sc, "sog")
                    sc.copy(out=oTg[:, hh * T + tb * 128:hh * T + (tb + 1) * 128], in_=ptA[:, 896:1024])
                    eotmark[tb] = b.chv("scalar")

            emit_scores(0)
            for tb in range(1, NT):
                emit_scores(tb)
                emit_pv(tb - 1, hh)
            emit_pv(NT - 1, hh)

        @b.sec("sync")
        def _(s, g=g):
            s.wait_ge(b.sem("ch_scalar"), b.chv("scalar"))
            b.dma(s, "sync", oT_d[:, g * 4 * T:(g + 1) * 4 * T], oTg[:], "soT")
            s.wait_ge(b.sem("soT"), b.val("soT"))
            b.inc(s.nop(), "sog")

    b.barrier()

    woT = bigA
    for kd in range(ND):
        wl_load([wo_d[kd * 128:(kd + 1) * 128, :]])

        @b.sec("tensor")
        def _(t, kd=kd):
            t.wait_ge(b.sem("gw0"), b.val("gw0"))
            for ke in range(ND):
                dst = ptA if (ke % 2 == 0) else ptB
                if wtmark[ke % 2] > 0:
                    t.wait_ge(b.sem("ch_scalar"), wtmark[ke % 2])
                b.inc(t.transpose(dst[:, 0:128], wl[0][:, ke * 128:(ke + 1) * 128], ident[:]), "pwo")

                @b.sec("scalar")
                def _(sc, ke=ke, kd=kd, dst=dst):
                    sc.wait_ge(b.sem("pwo"), b.val("pwo"))
                    sc.copy(out=woT[:, ke * T + kd * 128:ke * T + (kd + 1) * 128], in_=dst[:, 0:128])
                    wtmark[ke % 2] = b.chv("scalar")
        wl_guards.append(("pwo", b.val("pwo")))

    for tb in range(NT):
        @b.sec("sync")
        def _(s, tb=tb):
            if tb >= 2:
                s.wait_ge(b.sem("py"), 4 * (tb - 1))
            b.dma(s, "sync", oslab[tb % 2][:].rearrange("p (e t) -> p e t", t=128),
                  oT_d.rearrange("p (e t) -> p e t", t=T)[:, :, tb * 128:(tb + 1) * 128], f"lo{tb % 2}")

        @b.sec("tensor")
        def _(t, tb=tb):
            t.wait_ge(b.sem(f"lo{tb % 2}"), b.val(f"lo{tb % 2}"))
            if tb == 0:
                b.we(t, "scalar")
            for dch in range(4):
                b.we(t, "vector")
                last = None
                for ke in range(ND):
                    last = t.matmul(py[:, 0:512],
                                    oslab[tb % 2][:, ke * 128:(ke + 1) * 128],
                                    woT[:, ke * T + dch * 512:ke * T + (dch + 1) * 512],
                                    start=(ke == 0), stop=(ke == ND - 1))
                b.inc(last, "py")

                @b.sec("vector")
                def _(v, dch=dch):
                    v.wait_ge(b.sem("py"), b.val("py"))
                    b.w(v, "sst")
                    v.tensor_copy(out=s0t[:, dch * 512:(dch + 1) * 512], in_=py[:, 0:512])

        @b.sec("sync")
        def _(s, tb=tb):
            s.wait_ge(b.sem("ch_vector"), b.chv("vector"))
            b.dma(s, "sync", s0_d.rearrange("(n p) d -> n p d", p=128)[tb], s0t[:], "ss0")
            s.wait_ge(b.sem("ss0"), b.val("ss0"))
            b.inc(s.nop(), "sst", 4)

    @b.sec("sync")
    def _(s):
        s.wait_ge(b.sem("ss0"), b.val("ss0"))
        s.wait_ge(b.sem("sxnT"), b.val("sxnT"))


def _launch1_inmaps(inputs):
    consts = make_consts()
    cpk = np.zeros((128, CPK_W), dtype=np.float32)
    cpk[:, 0:2048] = consts["ropec"].reshape(16, 128, 128).transpose(1, 0, 2).reshape(128, 2048)
    cpk[:, 2048:4096] = consts["ropes"].reshape(16, 128, 128).transpose(1, 0, 2).reshape(128, 2048)
    cpk[:, 4096:4224] = consts["causal"]
    cpk[:, 4224:5376] = consts["para"]
    cpk[:, 5376:5504] = np.broadcast_to(inputs["L0_attn_gamma_q"], (128, 128)).astype(np.float32)
    cpk[:, 5504:5632] = np.broadcast_to(inputs["L0_attn_gamma_k"], (128, 128)).astype(np.float32)
    im = []
    for c in range(NCORES):
        im.append({
            "x": np.ascontiguousarray(inputs["L0_attn_x"][c]),
            "wq": inputs["L0_attn_wq"], "wk": inputs["L0_attn_wk"],
            "wv": inputs["L0_attn_wv"], "wo": inputs["L0_attn_wo"],
            "cpk": cpk, "ident": consts["ident"],
        })
    return im


def host_launch1(inputs):
    b = build_launch1()
    res = run_bass_kernel_spmd(b.nc, _launch1_inmaps(inputs), core_ids=list(range(NCORES)))
    return res.results


def dram_l2(nc, kio, with_ident=True):
    d = {
        "gWp": nc.dram_tensor("gWp", [16384, 1024], I32, kind="ExternalInput").ap(),
        "gSc": nc.dram_tensor("gSc", [16384, 64], F32, kind="ExternalInput").ap(),
        "uWp": nc.dram_tensor("uWp", [16384, 1024], I32, kind="ExternalInput").ap(),
        "uSc": nc.dram_tensor("uSc", [16384, 64], F32, kind="ExternalInput").ap(),
        "dWp": nc.dram_tensor("dWp", [2048, 8192], I32, kind="ExternalInput").ap(),
        "dSc": nc.dram_tensor("dSc", [2048, 512], F32, kind="ExternalInput").ap(),
        "WgT": nc.dram_tensor("WgT", [128, ND * 16384], BF16, kind=kio).ap(),
        "WuT": nc.dram_tensor("WuT", [128, ND * 16384], BF16, kind=kio).ap(),
        "Wdd": nc.dram_tensor("Wdd", [2048, 16384], BF16, kind=kio).ap(),
    }
    if with_ident:
        d["ident"] = nc.dram_tensor("ident", [128, 128], BF16, kind="ExternalInput").ap()
    return d


def build_launch2():
    b = B()
    Dd = dram_l2(b.nc, "ExternalOutput")
    b.start()
    emit_phase2(b, Dd)
    b.close()
    return b


def emit_phase2(b, Dd):
    nc = b.nc
    gW_d = Dd["gWp"]; gS_d = Dd["gSc"]; uW_d = Dd["uWp"]; uS_d = Dd["uSc"]
    dW_d = Dd["dWp"]; dS_d = Dd["dSc"]; id_d = Dd["ident"]
    WgT_d = Dd["WgT"]; WuT_d = Dd["WuT"]; Wd_d = Dd["Wdd"]

    NS = 2
    wp = [b.sb(f"wp{i}", [128, 1024], I32) for i in range(NS)]
    scs = [b.sb(f"scs{i}", [128, 64], F32) for i in range(NS)]
    ji = [b.sb(f"ji{i}", [128, 2048], I32) for i in range(NS)]
    b3i = [b.sb(f"b3i{i}", [128, 2048], I32) for i in range(NS)]
    jf = [b.sb(f"jf{i}", [128, 2048], BF16) for i in range(NS)]
    b3f = [b.sb(f"b3f{i}", [128, 2048], BF16) for i in range(NS)]
    tA = b.sb("tA", [128, 2048], BF16)
    tB = b.sb("tB", [128, 2048], BF16)
    tD = [b.sb(f"tD{i}", [128, 2048], BF16) for i in range(NS)]
    vals = [b.sb(f"vals{i}", [128, 2048], BF16) for i in range(NS)]
    stg = [b.sb(f"stg{i}", [128, ND * 128], BF16) for i in range(NS)]
    ident = b.sb("ident_sb", [128, 128], BF16)
    ptA = b.ps("ptA", [128, 1024], BF16)
    ptB = b.ps("ptB", [128, 1024], BF16)

    @b.sec("sync")
    def _(s):
        b.dma(s, "sync", ident[:], id_d, "li")

    @b.sec("tensor")
    def _(t):
        t.wait_ge(b.sem("li"), b.val("li"))

    tiles = []
    for Wp_d, Sc_d, Out_d in ((gW_d, gS_d, WgT_d), (uW_d, uS_d, WuT_d)):
        for r in range(128):
            rs = r * 256 if r < 64 else (r - 64) * 256 + 1
            tiles.append(("gu", Wp_d, Sc_d, Out_d, rs, r))
    for kd in range(ND):
        for cc in range(8):
            tiles.append(("d", kd, cc))

    NTILES = len(tiles)
    s1mark = [0] * NTILES
    s2mark = [0] * NTILES
    s3mark = [0] * NTILES

    def stage_load(k):
        kind = tiles[k][0]
        sl = k % NS

        @b.sec("sync")
        def _(s, k=k, sl=sl):
            if k >= NS:
                s.wait_ge(b.sem("ch_vector"), max(s1mark[k - NS], s3mark[k - NS]))
            if tiles[k][0] == "gu":
                _, Wp_d, Sc_d, Out_d, rs, r = tiles[k]
                b.dma(s, "sync", wp[sl][:], Wp_d[rs:rs + 255:2, :], f"lwp{sl}")
                b.dma(s, "sync", scs[sl][:], Sc_d[rs:rs + 255:2, :], f"lsc{sl}")
            else:
                _, kd, cc = tiles[k]
                c0 = cc * 1024
                b.dma(s, "sync", wp[sl][:], dW_d[kd * 128:(kd + 1) * 128, c0:c0 + 1024], f"lwp{sl}")
                b.dma(s, "sync", scs[sl][:], dS_d[kd * 128:(kd + 1) * 128, c0 // 16:c0 // 16 + 64], f"lsc{sl}")

    def stage_int(k):
        sl = k % NS

        @b.sec("vector")
        def _(v, k=k, sl=sl):
            v.wait_ge(b.sem(f"lwp{sl}"), b.val(f"lwp{sl}"))
            if k >= NS:
                v.wait_ge(b.sem("ch_scalar"), s2mark[k - NS])
            Kb = 1024
            v.tensor_scalar(out=ji[sl][:, 0:Kb], in0=wp[sl][:, 0:Kb], scalar1=15, scalar2=None, op0=ALU.bitwise_and)
            v.tensor_scalar(out=b3i[sl][:, 0:Kb], in0=wp[sl][:, 0:Kb], scalar1=3, scalar2=None, op0=ALU.logical_shift_right)
            v.tensor_scalar(out=b3i[sl][:, 0:Kb], in0=b3i[sl][:, 0:Kb], scalar1=1, scalar2=None, op0=ALU.bitwise_and)
            v.tensor_scalar(out=ji[sl][:, Kb:2 * Kb], in0=wp[sl][:, 0:Kb], scalar1=4, scalar2=None, op0=ALU.logical_shift_right)
            v.tensor_scalar(out=ji[sl][:, Kb:2 * Kb], in0=ji[sl][:, Kb:2 * Kb], scalar1=7, scalar2=None, op0=ALU.bitwise_and)
            v.tensor_scalar(out=b3i[sl][:, Kb:2 * Kb], in0=wp[sl][:, 0:Kb], scalar1=7, scalar2=None, op0=ALU.logical_shift_right)
            v.tensor_scalar(out=b3i[sl][:, Kb:2 * Kb], in0=b3i[sl][:, Kb:2 * Kb], scalar1=1, scalar2=None, op0=ALU.bitwise_and)
            v.tensor_scalar(out=ji[sl][:, 0:Kb], in0=ji[sl][:, 0:Kb], scalar1=7, scalar2=None, op0=ALU.bitwise_and)
            s1mark[k] = b.chv("vector")

    def stage_cvt(k):
        sl = k % NS

        @b.sec("scalar")
        def _(sc, k=k, sl=sl):
            b.we(sc, "vector")
            sc.copy(out=jf[sl][:], in_=ji[sl][:])
            sc.copy(out=b3f[sl][:], in_=b3i[sl][:])
            s2mark[k] = b.chv("scalar")

    def stage_pwl(k):
        sl = k % NS

        @b.sec("vector")
        def _(v, k=k, sl=sl):
            v.wait_ge(b.sem("ch_scalar"), s2mark[k])
            v.wait_ge(b.sem(f"lsc{sl}"), b.val(f"lsc{sl}"))
            if k >= NS and tiles[k - NS][0] == "gu":
                v.wait_ge(b.sem("pdT"), b.val("pdT"))
            if k >= NS and tiles[k - NS][0] == "d":
                for nm in (f"sdl{(k - NS) % NS}", f"sdh{(k - NS) % NS}"):
                    if b.val(nm) > 0:
                        v.wait_ge(b.sem(nm), b.val(nm))
            Kb = 1024
            v.tensor_scalar(out=tA[:], in0=jf[sl][:], scalar1=4.0, scalar2=0.0, op0=ALU.subtract, op1=ALU.max)
            v.tensor_scalar(out=tB[:], in0=jf[sl][:], scalar1=6.0, scalar2=0.0, op0=ALU.subtract, op1=ALU.max)
            v.tensor_tensor(out=tD[sl][:], in0=jf[sl][:], in1=tA[:], op=ALU.add)
            v.scalar_tensor_tensor(out=tD[sl][:], in0=tD[sl][:], scalar=0.5, in1=tB[:], op0=ALU.mult, op1=ALU.add)
            v.tensor_scalar(out=tA[:], in0=b3f[sl][:], scalar1=-2.0, scalar2=1.0, op0=ALU.mult, op1=ALU.add)
            v.tensor_tensor(out=tD[sl][:], in0=tD[sl][:], in1=tA[:], op=ALU.mult)
            nb_ = Kb // 16
            ov = vals[sl][:, 0:2 * Kb].rearrange("p (h bl i) -> p h bl i", h=2, i=16)
            sc4 = scs[sl][:, 0:nb_].unsqueeze(1).unsqueeze(3).broadcast_to([128, 2, nb_, 16])
            v.tensor_tensor(out=ov, in0=tD[sl][:, 0:2 * Kb].rearrange("p (h bl i) -> p h bl i", h=2, i=16), in1=sc4, op=ALU.mult)
            s3mark[k] = b.chv("vector")

    def stage_out(k):
        sl = k % NS
        kind = tiles[k][0]
        if kind == "gu":
            _, Wp_d, Sc_d, Out_d, rs, r = tiles[k]

            @b.sec("tensor")
            def _(t, k=k, sl=sl):
                t.wait_ge(b.sem("ch_vector"), s3mark[k])
                for half in range(2):
                    dst = ptA if half == 0 else ptB
                    b.we(t, "scalar")
                    for j in range(8):
                        kd2 = half * 8 + j
                        inst = t.transpose(dst[:, j * 128:(j + 1) * 128], vals[sl][:, kd2 * 128:(kd2 + 1) * 128], ident[:])
                    b.inc(inst, "pdT")

                    @b.sec("scalar")
                    def _(sc, half=half, dst=dst, sl=sl):
                        sc.wait_ge(b.sem("pdT"), b.val("pdT"))
                        soname = f"so{sl}"
                        if b.val(soname) > 0:
                            sc.wait_ge(b.sem(soname), b.val(soname))
                        sc.copy(out=stg[sl][:, half * 1024:(half + 1) * 1024], in_=dst[:, 0:1024])

            @b.sec("sync")
            def _(s, k=k, sl=sl, Out_d=Out_d, r=r):
                s.wait_ge(b.sem("ch_scalar"), b.chv("scalar"))
                b.dma(s, "sync", Out_d.rearrange("p (n t) -> p n t", t=16384)[:, :, r * 128:(r + 1) * 128],
                      stg[sl][:].rearrange("p (n t) -> p n t", t=128), f"so{sl}")
        else:
            _, kd, cc = tiles[k]
            c0 = cc * 1024

            @b.sec("sync")
            def _(s, k=k, sl=sl, kd=kd, c0=c0):
                s.wait_ge(b.sem("ch_vector"), s3mark[k])
                b.dma(s, "sync", Wd_d[kd * 128:(kd + 1) * 128, c0:c0 + 1024], vals[sl][:, 0:1024], f"sdl{sl}")
                b.dma(s, "sync", Wd_d[kd * 128:(kd + 1) * 128, 8192 + c0:8192 + c0 + 1024], vals[sl][:, 1024:2048], f"sdh{sl}")

    stage_load(0)
    stage_load(1)
    stage_int(0)
    stage_cvt(0)
    for k in range(1, NTILES):
        stage_int(k)
        stage_cvt(k)
        stage_pwl(k - 1)
        if k + 1 < NTILES:
            stage_load(k + 1)
        stage_out(k - 1)
    stage_pwl(NTILES - 1)
    stage_out(NTILES - 1)

    @b.sec("sync")
    def _(s):
        for nmn in ("so0", "so1", "sdl0", "sdl1", "sdh0", "sdh1"):
            if b.val(nmn) > 0:
                s.wait_ge(b.sem(nmn), b.val(nmn))


def _launch2_inmaps(inputs):
    consts = make_consts()
    im = []
    for c in range(NCORES):
        im.append({
            "gWp": inputs["L0_ffn_gate_W_packed"], "gSc": inputs["L0_ffn_gate_scales"],
            "uWp": inputs["L0_ffn_up_W_packed"], "uSc": inputs["L0_ffn_up_scales"],
            "dWp": inputs["L0_ffn_down_W_packed"], "dSc": inputs["L0_ffn_down_scales"],
            "ident": consts["ident"],
        })
    return im


def host_launch2(inputs, cores=1):
    b = build_launch2()
    res = run_bass_kernel_spmd(b.nc, _launch2_inmaps(inputs)[:cores], core_ids=list(range(cores)))
    return res.results


def build_launch3():
    b = B()
    nc = b.nc
    Dd = {
        "x": nc.dram_tensor("x", [T, D], F32, kind="ExternalInput").ap(),
        "xnTi": nc.dram_tensor("xnTi", [128, ND * T], BF16, kind="ExternalInput").ap(),
        "s0i": nc.dram_tensor("s0i", [T, D], BF16, kind="ExternalInput").ap(),
        "ident": nc.dram_tensor("ident", [128, 128], BF16, kind="ExternalInput").ap(),
        "x1": nc.dram_tensor("x1", [T, D], F32, kind="ExternalOutput").ap(),
        "xnT1": nc.dram_tensor("xnT1", [128, ND * T], BF16, kind="ExternalOutput").ap(),
    }
    Dd.update({k: v for k, v in dram_l2(nc, "ExternalInput", with_ident=False).items()
               if k in ("WgT", "WuT", "Wdd", "gWp", "gSc", "uWp", "uSc", "dWp", "dSc")})
    b.start()
    emit_phase3(b, Dd)
    b.close()
    return b


def emit_phase3(b, Dd):
    nc = b.nc
    x_d = Dd["x"]; xnT_d = Dd["xnTi"]; s0_d = Dd["s0i"]
    WgT_d = Dd["WgT"]; WuT_d = Dd["WuT"]; Wd_d = Dd["Wdd"]; id_d = Dd["ident"]
    x1_d = Dd["x1"]; xnT1_d = Dd["xnT1"]

    acc = b.sb("acc", [128, 4 * 2048], F32)
    xnTq = b.sb("xnTq", [128, ND * 512], BF16)
    wgc = [b.sb(f"wgc{i}", [128, ND * 512], BF16) for i in range(2)]
    wuc = [b.sb(f"wuc{i}", [128, ND * 512], BF16) for i in range(2)]
    wdc = [b.sb(f"wdc{i}", [128, 4 * 2048], BF16) for i in range(2)]
    ibuf = b.sb("ibuf", [128, 4 * 512], BF16)
    gf = b.sb("gf", [128, 512], F32)
    uf = b.sb("uf", [128, 512], F32)
    g2 = b.sb("g2", [128, 512], F32)
    ident = b.sb("ident_sb", [128, 128], BF16)
    xt = b.sb("xt", [128, D], F32)
    s0t = b.sb("s0t", [128, D], BF16)
    g0t = b.sb("g0t", [128, D], BF16)
    x1t = b.sb("x1t", [128, D], F32)
    xc = b.sb("xc", [128, D], F32)
    xnb = b.sb("xnb", [128, D], BF16)
    stg1 = b.sb("stg1", [128, ND * 128], BF16)
    stats = b.sb("stats", [128, 8], F32)

    pg = [b.ps(f"pg{i}", [128, 512], F32) for i in range(4)]
    pd = [b.ps(f"pd{i}", [128, 512], F32) for i in range(2)]
    ptA = b.ps("ptA", [128, 1024], BF16)
    ptB = b.ps("ptB", [128, 1024], BF16)

    @b.sec("sync")
    def _(s):
        b.dma(s, "sync", ident[:], id_d, "li")

    @b.sec("tensor")
    def _(t):
        t.wait_ge(b.sem("li"), b.val("li"))

    wg3 = WgT_d.rearrange("p (k n) -> p k n", n=16384)
    wu3 = WuT_d.rearrange("p (k n) -> p k n", n=16384)
    xnT3 = xnT_d.rearrange("p (k t) -> p k t", t=T)

    pgmark = [0] * 4
    pdmark = [0] * 2
    wmark = [0, 0]
    dmark = [0, 0]
    xqmark = [0]

    NCH = 32
    for quarter in range(4):
        tq0 = quarter * 512

        @b.sec("sync")
        def _(s, tq0=tq0):
            if b.val("pgu") > 0:
                s.wait_ge(b.sem("pgu"), xqmark[0])
            b.dma(s, "sync", xnTq[:].rearrange("p (k t) -> p k t", t=512),
                  xnT3[:, :, tq0:tq0 + 512], "lxh")

        for chunk in range(NCH):
            n0 = chunk * 512
            sl = chunk % 2

            @b.sec("sync")
            def _(s, n0=n0, sl=sl):
                if wmark[sl] > 0:
                    s.wait_ge(b.sem("pgu"), wmark[sl])
                b.dma(s, "sync", wgc[sl][:].rearrange("p (k n) -> p k n", n=512),
                      wg3[:, :, n0:n0 + 512], f"lwg{sl}")
                b.dma(s, "sync", wuc[sl][:].rearrange("p (k n) -> p k n", n=512),
                      wu3[:, :, n0:n0 + 512], f"lwu{sl}")

            @b.sec("scalar")
            def _(s, n0=n0, sl=sl):
                if dmark[sl] > 0:
                    s.wait_ge(b.sem("pdn"), dmark[sl])
                for kn in range(4):
                    slt = f"lwd{sl}{kn}"
                    h = b.sem(slt, "h")
                    if b.val(slt) > 0:
                        s.wait_ge(h, b.val(slt))
                    s.dma_start(out=wdc[sl][:, kn * 2048:(kn + 1) * 2048],
                                in_=Wd_d[:, n0 + kn * 128:n0 + (kn + 1) * 128],
                                transpose=True).then_inc(h, 16)
                    b.dma_by_q.setdefault("scalar", set()).add(slt)
                    b.add(slt, 16)

            for nt in range(4):
                @b.sec("tensor")
                def _(t, nt=nt, sl=sl):
                    t.wait_ge(b.sem(f"lwg{sl}"), b.val(f"lwg{sl}"))
                    t.wait_ge(b.sem(f"lwu{sl}"), b.val(f"lwu{sl}"))
                    t.wait_ge(b.sem("lxh"), b.val("lxh"))
                    for mi, wsrc in enumerate((wgc[sl], wuc[sl])):
                        pidx = nt % 2 * 2 + mi
                        if pgmark[pidx] > 0:
                            t.wait_ge(b.sem("ch_vector"), pgmark[pidx])
                        last = None
                        for kd in range(ND):
                            last = t.matmul(pg[pidx][:, 0:512],
                                            wsrc[:, kd * 512 + nt * 128:kd * 512 + (nt + 1) * 128],
                                            xnTq[:, kd * 512:(kd + 1) * 512],
                                            start=(kd == 0), stop=(kd == ND - 1))
                        b.inc(last, "pgu")

                @b.sec("vector")
                def _(v, nt=nt):
                    v.wait_ge(b.sem("pgu"), b.val("pgu"))
                    b.we(v, "scalar")
                    v.tensor_copy(out=gf[:], in_=pg[nt % 2 * 2][:, 0:512])
                    v.tensor_copy(out=uf[:], in_=pg[nt % 2 * 2 + 1][:, 0:512])
                    pgmark[nt % 2 * 2] = b.chv("vector")
                    pgmark[nt % 2 * 2 + 1] = b.chv("vector")

                @b.sec("scalar")
                def _(sc, nt=nt):
                    b.we(sc, "vector")
                    sc.activation(out=g2[:], in_=gf[:], func=AF.Square)

                @b.sec("vector")
                def _(v, nt=nt):
                    b.we(v, "scalar")
                    v.tensor_scalar(out=g2[:], in0=g2[:], scalar1=0.044715, scalar2=1.0, op0=ALU.mult, op1=ALU.add)
                    v.tensor_tensor(out=g2[:], in0=gf[:], in1=g2[:], op=ALU.mult)

                @b.sec("scalar")
                def _(sc, nt=nt):
                    b.we(sc, "vector")
                    sc.activation(out=g2[:], in_=g2[:], func=AF.Tanh, scale=0.7978845608028654)

                @b.sec("vector")
                def _(v, nt=nt):
                    b.we(v, "scalar")
                    v.tensor_tensor(out=g2[:], in0=gf[:], in1=g2[:], op=ALU.mult)
                    v.tensor_tensor(out=g2[:], in0=gf[:], in1=g2[:], op=ALU.add)
                    v.scalar_tensor_tensor(out=ibuf[:, nt * 512:(nt + 1) * 512], in0=g2[:],
                                           scalar=0.5, in1=uf[:], op0=ALU.mult, op1=ALU.mult)

            wmark[sl] = b.val("pgu")
            xqmark[0] = b.val("pgu")

            for tb in range(4):
                for dch in range(4):
                    @b.sec("tensor")
                    def _(t, tb=tb, dch=dch, sl=sl):
                        for kn in range(4):
                            t.wait_ge(b.sem(f"lwd{sl}{kn}"), b.val(f"lwd{sl}{kn}"))
                        pidx = (tb * 4 + dch) % 2
                        if pdmark[pidx] > 0:
                            t.wait_ge(b.sem("ch_vector"), pdmark[pidx])
                        b.we(t, "vector") if tb == 0 and dch == 0 else None
                        last = None
                        for kn in range(4):
                            last = t.matmul(pd[pidx][:, 0:512],
                                            ibuf[:, kn * 512 + tb * 128:kn * 512 + (tb + 1) * 128],
                                            wdc[sl][:, kn * 2048 + dch * 512:kn * 2048 + (dch + 1) * 512],
                                            start=(kn == 0), stop=(kn == 3))
                        b.inc(last, "pdn")

                    @b.sec("vector")
                    def _(v, tb=tb, dch=dch, chunk=chunk):
                        v.wait_ge(b.sem("pdn"), b.val("pdn"))
                        pidx = (tb * 4 + dch) % 2
                        asl = acc[:, tb * 2048 + dch * 512:tb * 2048 + (dch + 1) * 512]
                        if chunk == 0:
                            v.tensor_copy(out=asl, in_=pd[pidx][:, 0:512])
                        else:
                            v.tensor_tensor(out=asl, in0=asl, in1=pd[pidx][:, 0:512], op=ALU.add)
                        pdmark[pidx] = b.chv("vector")
            dmark[sl] = b.val("pdn")

        for tb in range(4):
            grow = tq0 + tb * 128

            @b.sec("sync")
            def _(s, grow=grow):
                b.we(s, "vector")
                b.dma(s, "sync", xt[:], x_d[grow:grow + 128, :], "lxt")
                b.dma(s, "sync", s0t[:], s0_d[grow:grow + 128, :], "ls0")

            @b.sec("scalar")
            def _(sc, tb=tb):
                b.we(sc, "vector")
                sc.activation(out=g0t[:], in_=acc[:, tb * 2048:(tb + 1) * 2048], func=AF.Silu)

            @b.sec("vector")
            def _(v, tb=tb):
                v.wait_ge(b.sem("lxt"), b.val("lxt"))
                v.wait_ge(b.sem("ls0"), b.val("ls0"))
                b.we(v, "scalar")
                b.w(v, "sx1")
                v.tensor_tensor(out=xc[:], in0=s0t[:], in1=g0t[:], op=ALU.mult)
                v.tensor_tensor(out=x1t[:], in0=xt[:], in1=xc[:], op=ALU.add)
                v.tensor_reduce(out=stats[:, 0:1], in_=x1t[:], axis=AX, op=ALU.add)
                v.tensor_scalar(out=stats[:, 1:2], in0=stats[:, 0:1], scalar1=-1.0 / D, scalar2=None, op0=ALU.mult)
                v.tensor_scalar(out=xc[:], in0=x1t[:], scalar1=stats[:, 1:2], scalar2=None, op0=ALU.add)

            @b.sec("sync")
            def _(s, grow=grow):
                b.we(s, "vector")
                b.dma(s, "sync", x1_d[grow:grow + 128, :], x1t[:], "sx1")
                s.wait_ge(b.sem("sx1"), b.val("sx1"))

            @b.sec("scalar")
            def _(sc):
                b.we(sc, "vector")
                sc.activation(out=xnb[:], in_=xc[:], func=AF.Square, accum_out=stats[:, 2:3])

            @b.sec("vector")
            def _(v):
                b.we(v, "scalar")
                v.tensor_scalar(out=stats[:, 3:4], in0=stats[:, 2:3], scalar1=1.0 / D, scalar2=1e-5, op0=ALU.mult, op1=ALU.add)

            @b.sec("scalar")
            def _(sc):
                b.we(sc, "vector")
                sc.activation(out=stats[:, 4:5], in_=stats[:, 3:4], func=AF.Sqrt)

            @b.sec("vector")
            def _(v):
                b.we(v, "scalar")
                v.reciprocal(out=stats[:, 5:6], in_=stats[:, 4:5])
                b.w(v, "pl2")
                v.tensor_scalar(out=xnb[:], in0=xc[:], scalar1=stats[:, 5:6], scalar2=None, op0=ALU.mult)

            @b.sec("tensor")
            def _(t):
                b.we(t, "vector")
                b.we(t, "scalar")
                for hf in range(2):
                    dst = ptA if hf == 0 else ptB
                    for j in range(8):
                        kd = hf * 8 + j
                        inst = t.transpose(dst[:, j * 128:(j + 1) * 128], xnb[:, kd * 128:(kd + 1) * 128], ident[:])
                    b.inc(inst, "pl2")

            @b.sec("scalar")
            def _(sc):
                sc.wait_ge(b.sem("pl2"), b.val("pl2"))
                b.w(sc, "sxn1")
                for hf in range(2):
                    src = ptA if hf == 0 else ptB
                    sc.copy(out=stg1[:, hf * 1024:(hf + 1) * 1024], in_=src[:, 0:1024])

            @b.sec("sync")
            def _(s, grow=grow):
                s.wait_ge(b.sem("ch_scalar"), b.chv("scalar"))
                b.dma(s, "sync", xnT1_d.rearrange("p (k t) -> p k t", t=T)[:, :, grow:grow + 128],
                      stg1[:].rearrange("p (k t) -> p k t", t=128), "sxn1")
                s.wait_ge(b.sem("sxn1"), b.val("sxn1"))

    @b.sec("sync")
    def _(s):
        s.wait_ge(b.sem("sx1"), b.val("sx1"))
        s.wait_ge(b.sem("sxn1"), b.val("sxn1"))


NH, DK, DV, CH = 6, 256, 512, 64

def dram_l4(nc, kio, with_common=True):
    d = {
        "l1wq": nc.dram_tensor("l1wq", [2048, 1536], F32, kind="ExternalInput").ap(),
        "l1wk": nc.dram_tensor("l1wk", [2048, 1536], F32, kind="ExternalInput").ap(),
        "l1wv": nc.dram_tensor("l1wv", [2048, 3072], F32, kind="ExternalInput").ap(),
        "l1wa": nc.dram_tensor("l1wa", [2048, 6], F32, kind="ExternalInput").ap(),
        "l1wb": nc.dram_tensor("l1wb", [2048, 6], F32, kind="ExternalInput").ap(),
        "l1wg": nc.dram_tensor("l1wg", [2048, 3072], F32, kind="ExternalInput").ap(),
        "l1wo": nc.dram_tensor("l1wo", [3072, 2048], F32, kind="ExternalInput").ap(),
        "cwq": nc.dram_tensor("cwq", [1536, 4], F32, kind="ExternalInput").ap(),
        "cwk": nc.dram_tensor("cwk", [1536, 4], F32, kind="ExternalInput").ap(),
        "cwv": nc.dram_tensor("cwv", [3072, 4], F32, kind="ExternalInput").ap(),
        "alog": nc.dram_tensor("alog", [6, 1], F32, kind="ExternalInput").ap(),
        "dtb": nc.dram_tensor("dtb", [6, 1], F32, kind="ExternalInput").ap(),
        "nrmw": nc.dram_tensor("nrmw", [1, 512], F32, kind="ExternalInput").ap(),
        "cp2": nc.dram_tensor("cp2", [128, 896], F32, kind="ExternalInput").ap(),
        "identf": nc.dram_tensor("identf", [128, 128], F32, kind="ExternalInput").ap(),
        "s1": nc.dram_tensor("s1", [T, D], BF16, kind=kio).ap(),
    }
    return d


def build_launch4():
    b = B()
    nc = b.nc
    Dd = dram_l4(nc, "ExternalOutput")
    Dd["xnTi"] = nc.dram_tensor("xnTi", [128, ND * T], BF16, kind="ExternalInput").ap()
    Dd["ident"] = nc.dram_tensor("ident", [128, 128], BF16, kind="ExternalInput").ap()
    b.start()
    emit_phase4(b, Dd)
    b.close()
    return b


def emit_phase4(b, Dd):
    nc = b.nc
    xnT_d = Dd["xnTi"]
    Wq_d = Dd["l1wq"]; Wk_d = Dd["l1wk"]; Wv_d = Dd["l1wv"]; Wa_d = Dd["l1wa"]
    Wb_d = Dd["l1wb"]; Wg_d = Dd["l1wg"]; Wo_d = Dd["l1wo"]
    cwq_d = Dd["cwq"]; cwk_d = Dd["cwk"]; cwv_d = Dd["cwv"]
    al_d = Dd["alog"]; dt_d = Dd["dtb"]; nw_d = Dd["nrmw"]
    cp2_d = Dd["cp2"]; id_d = Dd["ident"]; idf_d = Dd["identf"]; s1_d = Dd["s1"]
    CH = 128
    LN16 = 2.772588722239781

    xnq = b.sb("xnq", [128, ND * 512], BF16)
    wcol = [b.sb(f"wcol{i}", [128, ND * 128], BF16) for i in range(2)]
    wab = b.sb("wab", [128, 2 * ND * 6], BF16)
    wgh = b.sb("wgh", [128, ND * 512], BF16)
    wos = [b.sb(f"wos{i}", [128, 2048], BF16) for i in range(2)]
    qbuf = [b.sb(f"qbuf{i}", [128, 1024], BF16) for i in range(2)]
    kbuf = [b.sb(f"kbuf{i}", [128, 1024], BF16) for i in range(2)]
    vbuf = [b.sb(f"vbuf{i}", [128, 2048], BF16) for i in range(2)]
    pst = [b.sb(f"pst{i}", [128, 512], BF16) for i in range(2)]
    tails = b.sb("tails", [128, 48 * 4], F32)
    rw = b.sb("rw", [128, 520], F32)
    Sst = b.sb("Sst", [128, 6 * 2 * 512], F32)
    ogT = b.sb("ogT", [128, 24 * 512], BF16)
    cwq = b.sb("cwq_sb", [128, 12 * 4], F32)
    cwk = b.sb("cwk_sb", [128, 12 * 4], F32)
    cwv = b.sb("cwv_sb", [128, 24 * 4], F32)
    alA = b.sb("alA", [6, 1], F32)
    dtb = b.sb("dtb_sb", [6, 1], F32)
    negA = b.sb("negA", [6, 1], F32)
    nw1 = b.sb("nw1", [1, 512], F32)
    nwb = b.sb("nwb", [128, 512], F32)
    cp2 = b.sb("cp2_sb", [128, 896], F32)
    ident = b.sb("ident_sb", [128, 128], BF16)
    identf = b.sb("identf_sb", [128, 128], F32)
    gbq = b.sb("gbq", [6, 1024], F32)
    gpp = b.sb("gpp", [6, 512], F32)
    gsp = [b.sb(f"gsp{i}", [1, 512], F32) for i in range(2)]
    bsp = [b.sb(f"bsp{i}", [1, 512], F32) for i in range(2)]
    sqb = b.sb("sqb", [128, 512], F32)
    cacc = sqb

    def mk_bs(pi):
        s = {}
        s["sqb"] = b.sb(f"zsq{pi}", [128, 512], F32)
        s["brow"] = b.sb(f"zbr{pi}", [1, 648], F32)
        s["bce"] = b.sb(f"zbc{pi}", [128, 648], F32)
        s["colb"] = b.sb(f"zcb{pi}", [128, 128], F32)
        s["colb2"] = b.sb(f"zcb2{pi}", [128, 128], F32)
        s["qhT"] = b.sb(f"zqh{pi}", [128, 256], BF16)
        s["khT"] = b.sb(f"zkh{pi}", [128, 256], BF16)
        s["ksc"] = b.sb(f"zks{pi}", [128, 256], BF16)
        s["VT"] = b.sb(f"zvt{pi}", [128, 512], BF16)
        s["EB"] = b.sb(f"zeb{pi}", [128, 128], F32)
        s["Einc"] = b.sb(f"zei{pi}", [128, 128], F32)
        s["kkb"] = b.sb(f"zkk{pi}", [128, 128], F32)
        s["kqb"] = b.sb(f"zkq{pi}", [128, 128], F32)
        s["w2b"] = b.sb(f"zw2{pi}", [128, 128], BF16)
        s["e1b"] = b.sb(f"ze1{pi}", [128, 128], F32)
        for nm in ("bdA", "bdTA", "bdB", "bdTB", "tdTb", "alb", "nb", "nbT", "n2b", "n2bT", "n4bT"):
            s[nm] = b.sb(f"z{nm}{pi}", [128, 128], BF16)
        s["betan"] = b.sb(f"zbt{pi}", [128, 1], F32)
        s["eGc"] = b.sb(f"zec{pi}", [128, 1], F32)
        s["rs_"] = b.sb(f"zrs{pi}", [128, 4], F32)
        s["ubf"] = b.sb(f"zub{pi}", [128, 512], F32)
        s["ub"] = b.sb(f"zu{pi}", [128, 512], BF16)
        s["eGb"] = b.sb(f"zeg{pi}", [128, 128], F32)
        s["ogb"] = b.sb(f"zog{pi}", [128, 512], BF16)
        return s
    BS = [mk_bs(0), mk_bs(1)]
    Sbp = [b.sb(f"Sbp{i}", [128, 2 * 512], BF16) for i in range(2)]
    gtq = [b.sb(f"gtq{i}", [128, 4 * 512], BF16) for i in range(2)]
    ogs = [b.sb(f"ogs{i}", [128, 512], BF16) for i in range(2)]
    ybuf = b.sb("ybuf", [128, 24 * 128], BF16)
    gmark = [0]
    s1t = b.sb("s1t", [128, D], BF16)

    pA = b.ps("pA", [128, 512], F32)
    pB_ = b.ps("pB_", [128, 512], F32)
    pC = b.ps("pC", [128, 512], F32)
    pD = b.ps("pD", [128, 512], F32)
    pE2 = b.ps("pE2", [128, 512], F32)
    pF = b.ps("pF", [128, 512], F32)
    pTb = b.ps("pTb", [128, 512], BF16)

    @b.sec("sync")
    def _(s):
        b.dma(s, "sync", ident[:], id_d, "li")
        b.dma(s, "sync", identf[:], idf_d, "lif")
        b.dma(s, "sync", cp2[:], cp2_d, "lcp")
        b.dma(s, "sync", cwq[:].rearrange("p (n k) -> p n k", k=4), cwq_d.rearrange("(n p) k -> p n k", p=128), "lcwq")
        b.dma(s, "sync", cwk[:].rearrange("p (n k) -> p n k", k=4), cwk_d.rearrange("(n p) k -> p n k", p=128), "lcwk")
        b.dma(s, "sync", cwv[:].rearrange("p (n k) -> p n k", k=4), cwv_d.rearrange("(n p) k -> p n k", p=128), "lcwv")
        b.dma(s, "sync", alA[:], al_d, "lal")
        b.dma(s, "sync", dtb[:], dt_d, "ldt")
        b.dma(s, "sync", nw1[:], nw_d, "lnw")

    @b.sec("gpsimd")
    def _(gp):
        b.dma(gp, "gpsimd", wab[:].rearrange("p (m k h) -> p m k h", m=2, h=6)[:, 0],
              Wa_d.rearrange("(k p) h -> p k h", p=128), "lwa")
        b.dma(gp, "gpsimd", wab[:].rearrange("p (m k h) -> p m k h", m=2, h=6)[:, 1],
              Wb_d.rearrange("(k p) h -> p k h", p=128), "lwb")

    up_incl = cp2[:, 0:128]
    bd_up = cp2[:, 128:256]
    ones = cp2[:, 256:384]
    bd_lo = cp2[:, 384:512]
    ob_lo = cp2[:, 512:640]
    eye128 = cp2[:, 640:768]

    @b.sec("scalar")
    def _(sc):
        sc.wait_ge(b.sem("lal"), b.val("lal"))
        sc.wait_ge(b.sem("ldt"), b.val("ldt"))
        sc.wait_ge(b.sem("lcp"), b.val("lcp"))
        sc.wait_ge(b.sem("lcwq"), b.val("lcwq"))
        sc.wait_ge(b.sem("lcwk"), b.val("lcwk"))
        sc.wait_ge(b.sem("lcwv"), b.val("lcwv"))
        sc.activation(out=negA[:], in_=alA[:], func=AF.Exp)

    @b.sec("vector")
    def _(v):
        b.we(v, "scalar")
        v.tensor_scalar(out=negA[:], in0=negA[:], scalar1=-1.0, scalar2=None, op0=ALU.mult)
        v.memset(tails[:], 0.0)
        v.memset(Sst[:], 0.0)
        v.memset(BS[0]["brow"][:], 0.0)
        v.memset(BS[1]["brow"][:], 0.0)

    @b.sec("tensor")
    def _(t):
        t.wait_ge(b.sem("li"), b.val("li"))
        t.wait_ge(b.sem("lif"), b.val("lif"))
        t.wait_ge(b.sem("lcp"), b.val("lcp"))
        t.wait_ge(b.sem("lnw"), b.val("lnw"))
        b.inc(t.matmul(pA[:, 0:512], ones[0:1, :], nw1[0:1, :], start=True, stop=True), "pnw")

    @b.sec("vector")
    def _(v):
        v.wait_ge(b.sem("pnw"), b.val("pnw"))
        v.tensor_copy(out=nwb[:], in_=pA[:, 0:512])

    wobf_d = nc.dram_tensor("wobf", [3072, 2048], BF16, kind="Internal").ap()
    ogTd = nc.dram_tensor("ogTd", [128, 24 * 512], BF16, kind="Internal").ap()
    ogTd3 = ogTd.rearrange("p (e t) -> p e t", t=512)
    qkvd = nc.dram_tensor("qkvd", [128, 48 * 512], BF16, kind="Internal").ap()
    qkvd3 = qkvd.rearrange("p (g t) -> p g t", t=512)

    @b.sec("gpsimd")
    def _(gp):
        for ke in range(24):
            sl = ke % 2
            if ke >= 2:
                gp.wait_ge(b.sem(f"swo{sl}"), b.val(f"swo{sl}"))
            b.dma(gp, "gpsimd", wos[sl][:], Wo_d[ke * 128:(ke + 1) * 128, :], f"lwoc{sl}")

            @b.sec("scalar")
            def _(sc, ke=ke, sl=sl):
                sc.wait_ge(b.sem(f"lwoc{sl}"), b.val(f"lwoc{sl}"))
                b.dma(sc, "scalar", wobf_d[ke * 128:(ke + 1) * 128, :], wos[sl][:], f"swo{sl}")

    xnT3 = xnT_d.rearrange("p (k t) -> p k t", t=T)
    PROJ = (("q", Wq_d, 12, cwq, None, 0), ("k", Wk_d, 12, cwk, None, 12), ("v", Wv_d, 24, cwv, None, 24))

    for q in range(4):
        q0 = q * 512

        @b.sec("sync")
        def _(s, q0=q0):
            b.w(s, "pgat")
            b.w(s, "pprj")
            b.w(s, "pab")
            b.dma(s, "sync", xnq[:].rearrange("p (k t) -> p k t", t=512),
                  xnT3[:, :, q0:q0 + 512], "lxq")

        @b.sec("tensor")
        def _(t, q=q):
            t.wait_ge(b.sem("lxq"), b.val("lxq"))
            t.wait_ge(b.sem("lwa"), b.val("lwa"))
            t.wait_ge(b.sem("lwb"), b.val("lwb"))
            b.we(t, "vector")
            wab4 = wab[:].rearrange("p (m k h) -> p m k h", m=2, h=6)
            last = None
            for kd in range(ND):
                last = t.matmul(pF[0:6, 0:512], wab4[:, 0, kd], xnq[:, kd * 512:(kd + 1) * 512],
                                start=(kd == 0), stop=(kd == ND - 1))
            b.inc(last, "pab")
            last = None
            for kd in range(ND):
                last = t.matmul(pE2[0:6, 0:512], wab4[:, 1, kd], xnq[:, kd * 512:(kd + 1) * 512],
                                start=(kd == 0), stop=(kd == ND - 1))
            b.inc(last, "pab")

        @b.sec("scalar")
        def _(sc, q=q):
            sc.wait_ge(b.sem("pab"), b.val("pab"))
            sc.activation(out=gbq[0:6, 0:512], in_=pF[0:6, 0:512], func=AF.Identity, bias=dtb[:])
            sc.activation(out=gbq[0:6, 512:1024], in_=pE2[0:6, 0:512], func=AF.Sigmoid)

        @b.sec("vector")
        def _(v, q=q):
            b.we(v, "scalar")
            v.tensor_scalar(out=gpp[0:6, 0:512], in0=gbq[0:6, 0:512], scalar1=0.0, scalar2=None, op0=ALU.max)

        @b.sec("scalar")
        def _(sc, q=q):
            b.we(sc, "vector")
            sc.activation(out=gbq[0:6, 0:512], in_=gbq[0:6, 0:512], func=AF.Abs)
            sc.activation(out=gbq[0:6, 0:512], in_=gbq[0:6, 0:512], func=AF.Exp, scale=-1.0)
            sc.activation(out=gbq[0:6, 0:512], in_=gbq[0:6, 0:512], func=AF.Ln, bias=1.0)

        @b.sec("vector")
        def _(v, q=q):
            b.we(v, "scalar")
            v.tensor_tensor(out=gbq[0:6, 0:512], in0=gbq[0:6, 0:512], in1=gpp[0:6, 0:512], op=ALU.add)
            v.tensor_scalar(out=gbq[0:6, 0:512], in0=gbq[0:6, 0:512], scalar1=negA[:], scalar2=None, op0=ALU.mult)
            src, dst = gbq[0:6, 0:512], gpp[0:6, 0:512]
            for st_ in (1, 2, 4, 8, 16, 32, 64):
                s3 = src.rearrange("p (c t) -> p c t", t=CH)
                d3 = dst.rearrange("p (c t) -> p c t", t=CH)
                v.tensor_tensor(out=d3[:, :, st_:CH], in0=s3[:, :, st_:CH], in1=s3[:, :, 0:CH - st_], op=ALU.add)
                v.tensor_copy(out=d3[:, :, 0:st_], in_=s3[:, :, 0:st_])
                src, dst = dst, src
            v.tensor_copy(out=gbq[0:6, 0:512], in_=gpp[0:6, 0:512])

        for pname, W_d, nct, cw, dstb, goff in PROJ:
            for ct in range(nct):
                sl = ct % 2

                @b.sec("gpsimd")
                def _(gp, W_d=W_d, ct=ct, sl=sl):
                    b.w(gp, "pprj")
                    b.dma(gp, "gpsimd", wcol[sl][:].rearrange("p (k c) -> p k c", c=128),
                          W_d.rearrange("(k p) c -> p k c", p=128)[:, :, ct * 128:(ct + 1) * 128], f"lwc{sl}")

                @b.sec("tensor")
                def _(t, ct=ct, sl=sl):
                    t.wait_ge(b.sem(f"lwc{sl}"), b.val(f"lwc{sl}"))
                    b.we(t, "vector")
                    last = None
                    for kd in range(ND):
                        last = t.matmul(pB_[:, 0:512], wcol[sl][:, kd * 128:(kd + 1) * 128],
                                        xnq[:, kd * 512:(kd + 1) * 512], start=(kd == 0), stop=(kd == ND - 1))
                    b.inc(last, "pprj")

                @b.sec("vector")
                def _(v, ct=ct, cw=cw, goff=goff):
                    v.wait_ge(b.sem("pprj"), b.val("pprj"))
                    b.we(v, "scalar")
                    gidx = goff + ct
                    v.tensor_copy(out=rw[:, 4:516], in_=pB_[:, 0:512])
                    v.tensor_copy(out=rw[:, 0:4], in_=tails[:, gidx * 4:(gidx + 1) * 4])
                    v.tensor_copy(out=tails[:, gidx * 4:(gidx + 1) * 4], in_=rw[:, 512:516])
                    v.tensor_scalar(out=cacc[:], in0=rw[:, 1:513], scalar1=cw[:, ct * 4:ct * 4 + 1], scalar2=None, op0=ALU.mult)
                    for i in (1, 2, 3):
                        v.scalar_tensor_tensor(out=cacc[:], in0=rw[:, 1 + i:513 + i], scalar=cw[:, ct * 4 + i:ct * 4 + i + 1],
                                               in1=cacc[:], op0=ALU.mult, op1=ALU.add)

                @b.sec("scalar")
                def _(sc, ct=ct, goff=goff):
                    b.we(sc, "vector")
                    gidx = goff + ct
                    sl = gidx % 2
                    if gidx == 0:
                        for nm in ("lqb0", "lqb1", "lkb0", "lkb1", "lvb0", "lvb1"):
                            if b.val(nm) > 0:
                                sc.wait_ge(b.sem(nm), b.val(nm))
                    b.w(sc, f"sqkv{sl}")
                    sc.activation(out=pst[sl][:], in_=cacc[:], func=AF.Silu)
                    sc.wait_ge(b.sem("ch_scalar"), b.chv("scalar"))
                    b.dma(sc, "scalar", qkvd[:, gidx * 512:(gidx + 1) * 512], pst[sl][:], f"sqkv{sl}")

        for hp in range(3):
            h0p, h1p = 2 * hp, 2 * hp + 1

            @b.sec("gpsimd")
            def _(gp, h0p=h0p, h1p=h1p):
                b.we(gp, "vector")
                b.we(gp, "scalar")
                b.dma(gp, "gpsimd", gsp[0][0:1, 0:512], gbq[h0p:h0p + 1, 0:512], "lgs0")
                b.dma(gp, "gpsimd", bsp[0][0:1, 0:512], gbq[h0p:h0p + 1, 512:1024], "lbs0")
                b.dma(gp, "gpsimd", gsp[1][0:1, 0:512], gbq[h1p:h1p + 1, 0:512], "lgs1")
                b.dma(gp, "gpsimd", bsp[1][0:1, 0:512], gbq[h1p:h1p + 1, 512:1024], "lbs1")

            @b.sec("vector")
            def _(v, h0p=h0p, h1p=h1p):
                v.wait_ge(b.sem("po"), b.val("po"))
                v.wait_ge(b.sem("pks"), b.val("pks"))
                for pi, h in ((0, h0p), (1, h1p)):
                    for kt in range(2):
                        v.tensor_copy(out=Sbp[pi][:, kt * 512:(kt + 1) * 512],
                                      in_=Sst[:, (h * 2 + kt) * 512:(h * 2 + kt + 1) * 512])

            @b.sec("sync")
            def _(s, h0p=h0p, h1p=h1p):
                s.wait_ge(b.sem("ch_vector"), b.chv("vector"))
                s.wait_ge(b.sem("pvt"), b.val("pvt"))
                s.wait_ge(b.sem("sqkv0"), b.val("sqkv0"))
                s.wait_ge(b.sem("sqkv1"), b.val("sqkv1"))
                for pi, h in ((0, h0p), (1, h1p)):
                    b.dma(s, "sync", qbuf[pi][:].rearrange("p (g t) -> p g t", t=512),
                          qkvd3[:, 2 * h:2 * h + 2, :], f"lqb{pi}")
                    b.dma(s, "sync", kbuf[pi][:].rearrange("p (g t) -> p g t", t=512),
                          qkvd3[:, 12 + 2 * h:14 + 2 * h, :], f"lkb{pi}")
                    b.dma(s, "sync", vbuf[pi][:].rearrange("p (g t) -> p g t", t=512),
                          qkvd3[:, 24 + 4 * h:28 + 4 * h, :], f"lvb{pi}")

            for pi, h in ((0, h0p), (1, h1p)):
                @b.sec("gpsimd")
                def _(gp, h=h):
                    b.w(gp, "pgat")
                    b.dma(gp, "gpsimd", wgh[:].rearrange("p (k c) -> p k c", c=512),
                          Wg_d.rearrange("(k p) c -> p k c", p=128)[:, :, h * 512:(h + 1) * 512], "lwgh")

                for cc in range(4):
                    @b.sec("tensor")
                    def _(t, cc=cc, pi=pi):
                        t.wait_ge(b.sem("lwgh"), b.val("lwgh"))
                        if pi == 0 and cc == 0:
                            b.we(t, "vector")
                        if gmark[0] > 0:
                            t.wait_ge(b.sem("ch_scalar"), gmark[0])
                        c0g = cc * CH
                        last = None
                        for kd in range(ND):
                            last = t.matmul(pE2[0:128, 0:512], xnq[:, kd * 512 + c0g:kd * 512 + c0g + CH],
                                            wgh[:, kd * 512:(kd + 1) * 512], start=(kd == 0), stop=(kd == ND - 1))
                        b.inc(last, "pgat")

                    @b.sec("scalar")
                    def _(sc, cc=cc, pi=pi):
                        sc.wait_ge(b.sem("pgat"), b.val("pgat"))
                        if pi == 0 and cc == 0:
                            b.we(sc, "vector")
                        sc.activation(out=gtq[pi][:, cc * 512:(cc + 1) * 512], in_=pE2[0:128, 0:512], func=AF.Silu)
                        gmark[0] = b.chv("scalar")

            def chunk_secs(h, pi, cc):
                bs = BS[pi]
                CHN = pC if pi == 0 else pE2
                WPS = pD if pi == 0 else pB_
                c0 = cc * CH
                gslice = gsp[pi][0:1, c0:c0 + CH]
                bslice = bsp[pi][0:1, c0:c0 + CH]
                kt0 = kbuf[pi][:, c0:c0 + CH]
                kt1 = kbuf[pi][:, 512 + c0:512 + c0 + CH]
                qt0 = qbuf[pi][:, c0:c0 + CH]
                qt1 = qbuf[pi][:, 512 + c0:512 + c0 + CH]
                sqb_, brow_, bce_ = bs["sqb"], bs["brow"], bs["bce"]
                colb_, colb2_, qhT_, khT_ = bs["colb"], bs["colb2"], bs["qhT"], bs["khT"]
                lnb_ = brow_
                ksc_, VT_, EB_, Einc_ = bs["ksc"], bs["VT"], bs["EB"], bs["Einc"]
                kkb_, kqb_, w2b_, e1b_ = bs["kkb"], bs["kqb"], bs["w2b"], bs["e1b"]
                bdA_, bdTA_, bdB_, bdTB_ = bs["bdA"], bs["bdTA"], bs["bdB"], bs["bdTB"]
                tdTb_, alb_ = bs["tdTb"], bs["alb"]
                nb_, nbT_, n2b_, n2bT_, n4bT_ = bs["nb"], bs["nbT"], bs["n2b"], bs["n2bT"], bs["n4bT"]
                betan_, eGc_, rs__ = bs["betan"], bs["eGc"], bs["rs_"]
                ubf_, ub_, eGb_, ogb_ = bs["ubf"], bs["ub"], bs["eGb"], bs["ogb"]
                obx_, sqd_, kscT_ = sqb_, ubf_, qhT_
                S = []

                def s01(v):
                    b.we(v, "scalar")
                    b.w(v, "psq")
                    v.wait_ge(b.sem(f"lqb{pi}"), b.val(f"lqb{pi}"))
                    v.wait_ge(b.sem(f"lkb{pi}"), b.val(f"lkb{pi}"))
                    v.tensor_tensor(out=sqb_[:, 0:CH], in0=qt0, in1=qt0, op=ALU.mult)
                    v.tensor_tensor(out=sqb_[:, CH:2 * CH], in0=kt0, in1=kt0, op=ALU.mult)
                    v.tensor_tensor(out=sqb_[:, 2 * CH:3 * CH], in0=qt1, in1=qt1, op=ALU.mult)
                    v.tensor_tensor(out=sqb_[:, 3 * CH:4 * CH], in0=kt1, in1=kt1, op=ALU.mult)
                S.append(("vector", s01))

                def s02(t):
                    b.we(t, "vector")
                    b.we(t, "scalar")
                    t.matmul(CHN[0:1, 0:256], ones[:, 0:1], sqb_[:, 0:256], start=True, stop=False)
                    b.inc(t.matmul(CHN[0:1, 0:256], ones[:, 0:1], sqb_[:, 256:512], start=False, stop=True), "psq")
                S.append(("tensor", s02))

                def s03(sc):
                    sc.wait_ge(b.sem("psq"), b.val("psq"))
                    sc.wait_ge(b.sem(f"lgs{pi}"), b.val(f"lgs{pi}"))
                    sc.wait_ge(b.sem(f"lbs{pi}"), b.val(f"lbs{pi}"))
                    b.we(sc, "vector")
                    sc.activation(out=brow_[0:1, 0:128], in_=CHN[0:1, 0:128], func=AF.Ln, scale=256.0)
                    sc.activation(out=brow_[0:1, 128:256], in_=CHN[0:1, 128:256], func=AF.Ln)
                    sc.activation(out=brow_[0:1, 256:512], in_=brow_[0:1, 0:256], func=AF.Exp, scale=-0.5)
                    GC = gslice[0:1, CH - 1:CH]
                    sc.activation(out=brow_[0:1, 512:640], in_=gslice, func=AF.Exp, scale=-1.0, bias=GC)
                    sc.activation(out=brow_[0:1, 640:641], in_=GC, func=AF.Exp)
                S.append(("scalar", s03))

                def s04(v):
                    b.we(v, "scalar")
                    v.wait_ge(b.sem(f"lgs{pi}"), b.val(f"lgs{pi}"))
                    v.wait_ge(b.sem(f"lbs{pi}"), b.val(f"lbs{pi}"))
                    v.tensor_copy(out=brow_[0:1, 0:CH], in_=gslice)
                    v.tensor_copy(out=brow_[0:1, CH:2 * CH], in_=bslice)
                S.append(("vector", s04))

                def s05(t):
                    b.we(t, "vector")
                    t.matmul(pA[:, 0:512], ones[0:1, :], brow_[0:1, 0:512], start=True, stop=True)
                    b.inc(t.matmul(pB_[:, 0:136], ones[0:1, :], brow_[0:1, 512:648], start=True, stop=True), "pbc")
                S.append(("tensor", s05))

                def s06(v):
                    v.wait_ge(b.sem("pbc"), b.val("pbc"))
                    v.tensor_copy(out=bce_[:, 0:512], in_=pA[:, 0:512])
                    v.tensor_copy(out=bce_[:, 512:648], in_=pB_[:, 0:136])
                S.append(("vector", s06))

                def s07(t):
                    b.we(t, "vector")
                    b.we(t, "scalar")
                    t.transpose(CHN[0:128, 0:128], bce_[:, 0:128], identf[:])
                    b.inc(t.transpose(CHN[0:128, 128:256], bce_[:, 128:256], identf[:]), "pgc")
                S.append(("tensor", s07))

                def s08(v):
                    v.wait_ge(b.sem("pgc"), b.val("pgc"))
                    v.tensor_copy(out=colb_[:], in_=CHN[0:128, 0:128])
                    v.tensor_copy(out=colb2_[:], in_=CHN[0:128, 128:256])
                    v.tensor_tensor(out=qhT_[:, 0:CH], in0=qt0, in1=bce_[:, 256:384], op=ALU.mult)
                    v.tensor_tensor(out=qhT_[:, CH:2 * CH], in0=qt1, in1=bce_[:, 256:384], op=ALU.mult)
                    v.tensor_tensor(out=khT_[:, 0:CH], in0=kt0, in1=bce_[:, 384:512], op=ALU.mult)
                    v.tensor_tensor(out=khT_[:, CH:2 * CH], in0=kt1, in1=bce_[:, 384:512], op=ALU.mult)
                    v.tensor_scalar(out=EB_[:], in0=bce_[:, 0:CH], scalar1=colb_[:, 0:1], scalar2=None, op0=ALU.subtract)
                    v.tensor_scalar(out=EB_[:], in0=EB_[:], scalar1=-87.0, scalar2=87.0, op0=ALU.max, op1=ALU.min)
                    v.tensor_scalar(out=betan_[:], in0=colb2_[:, 0:1], scalar1=-1.0, scalar2=None, op0=ALU.mult)
                S.append(("vector", s08))

                def s09(sc):
                    b.we(sc, "vector")
                    sc.activation(out=eGb_[:], in_=bce_[:, 0:CH], func=AF.Exp)
                    sc.activation(out=eGc_[:], in_=colb_[:, 0:1], func=AF.Exp)
                    sc.activation(out=e1b_[:], in_=EB_[:], func=AF.Exp)
                    sc.activation(out=EB_[:], in_=EB_[:], func=AF.Exp, scale=-1.0)
                S.append(("scalar", s09))

                def s10(t):
                    b.we(t, "scalar")
                    b.we(t, "vector")
                    t.matmul(CHN[0:128, 0:128], khT_[:, 0:CH], khT_[:, 0:CH], start=True, stop=False)
                    t.matmul(CHN[0:128, 0:128], khT_[:, CH:2 * CH], khT_[:, CH:2 * CH], start=False, stop=True)
                    t.matmul(CHN[0:128, 128:256], khT_[:, 0:CH], qhT_[:, 0:CH], start=True, stop=False)
                    b.inc(t.matmul(CHN[0:128, 128:256], khT_[:, CH:2 * CH], qhT_[:, CH:2 * CH], start=False, stop=True), "pkk")
                    t.matmul(pD[0:128, 0:512], khT_[:, 0:CH], Sbp[pi][:, 0:512], start=True, stop=False)
                    b.inc(t.matmul(pD[0:128, 0:512], khT_[:, CH:2 * CH], Sbp[pi][:, 512:1024], start=False, stop=True), "pks")
                    t.wait_ge(b.sem(f"lvb{pi}"), b.val(f"lvb{pi}"))
                    inst = None
                    for i in range(4):
                        inst = t.transpose(pTb[0:128, i * 128:(i + 1) * 128],
                                           vbuf[pi][:, i * 512 + c0:i * 512 + c0 + CH], ident[:])
                    b.inc(inst, "pvt")
                S.append(("tensor", s10))

                def s11(v):
                    v.wait_ge(b.sem("pvt"), b.val("pvt"))
                    v.wait_ge(b.sem("pkk"), b.val("pkk"))
                    v.wait_ge(b.sem("pks"), b.val("pks"))
                    v.tensor_copy(out=VT_[:], in_=pTb[0:128, 0:512])
                    v.tensor_copy(out=kkb_[:], in_=CHN[0:128, 0:128])
                    v.tensor_copy(out=kqb_[:], in_=CHN[0:128, 128:256])
                    v.tensor_tensor(out=Einc_[:], in0=e1b_[:], in1=up_incl[0:128, :], op=ALU.mult)
                    v.tensor_tensor(out=EB_[:], in0=EB_[:], in1=kkb_[:], op=ALU.mult)
                    v.tensor_tensor(out=kkb_[:], in0=e1b_[:], in1=kkb_[:], op=ALU.mult)
                    v.tensor_tensor(out=e1b_[:], in0=kkb_[:], in1=bd_up[0:128, :], op=ALU.mult)
                    v.tensor_tensor(out=e1b_[:], in0=e1b_[:], in1=bce_[:, 128:256], op=ALU.mult)
                    v.tensor_scalar(out=bdTA_[:], in0=e1b_[:], scalar1=-1.0, scalar2=None, op0=ALU.mult)
                    v.tensor_tensor(out=e1b_[:], in0=EB_[:], in1=bd_lo[0:128, :], op=ALU.mult)
                    v.tensor_scalar(out=bdA_[:], in0=e1b_[:], scalar1=betan_[:], scalar2=None, op0=ALU.mult)
                    v.tensor_tensor(out=e1b_[:], in0=EB_[:], in1=ob_lo[0:128, :], op=ALU.mult)
                    v.tensor_scalar(out=alb_[:], in0=e1b_[:], scalar1=colb2_[:, 0:1], scalar2=None, op0=ALU.mult)
                    v.tensor_tensor(out=tdTb_[:], in0=bdTA_[:], in1=eye128[0:128, :], op=ALU.add)
                    v.tensor_scalar(out=ubf_[:], in0=pD[0:128, 0:512], scalar1=eGc_[:], scalar2=None, op0=ALU.mult)
                    v.tensor_tensor(out=ubf_[:], in0=VT_[:], in1=ubf_[:], op=ALU.subtract)
                    v.tensor_scalar(out=ubf_[:], in0=ubf_[:], scalar1=colb2_[:, 0:1], scalar2=None, op0=ALU.mult)
                    v.tensor_copy(out=ub_[:], in_=ubf_[:])
                S.append(("vector", s11))

                for lvl in range(3):
                    src_bd, src_bdT = (bdA_, bdTA_) if lvl % 2 == 0 else (bdB_, bdTB_)
                    dst_bd, dst_bdT = (bdB_, bdTB_) if lvl % 2 == 0 else (bdA_, bdTA_)

                    def sl1(t, src_bd=src_bd, src_bdT=src_bdT):
                        b.we(t, "vector")
                        t.matmul(CHN[0:128, 0:128], src_bdT[:], src_bd[:], start=True, stop=True)
                        b.inc(t.matmul(CHN[0:128, 128:256], src_bd[:], src_bdT[:], start=True, stop=True), "pbs")
                    S.append(("tensor", sl1))

                    def sl2(v, dst_bd=dst_bd, dst_bdT=dst_bdT):
                        v.wait_ge(b.sem("pbs"), b.val("pbs"))
                        v.tensor_copy(out=dst_bd[:], in_=CHN[0:128, 0:128])
                        v.tensor_copy(out=dst_bdT[:], in_=CHN[0:128, 128:256])
                    S.append(("vector", sl2))

                    def sl3(t, dst_bd=dst_bd):
                        b.we(t, "vector")
                        t.matmul(CHN[0:128, 0:128], tdTb_[:], dst_bd[:], start=True, stop=True)
                        b.inc(t.matmul(CHN[0:128, 128:256], dst_bd[:], tdTb_[:], start=True, stop=True), "pbs")
                    S.append(("tensor", sl3))

                    def sl4(v):
                        v.wait_ge(b.sem("pbs"), b.val("pbs"))
                        v.tensor_tensor(out=tdTb_[:], in0=tdTb_[:], in1=CHN[0:128, 128:256], op=ALU.add)
                    S.append(("vector", sl4))

                def s18(t):
                    b.we(t, "vector")
                    t.matmul(CHN[0:128, 0:128], tdTb_[:], alb_[:], start=True, stop=True)
                    b.inc(t.matmul(CHN[0:128, 128:256], alb_[:], tdTb_[:], start=True, stop=True), "pbs")
                S.append(("tensor", s18))

                def s19(v):
                    v.wait_ge(b.sem("pbs"), b.val("pbs"))
                    v.tensor_scalar(out=nb_[:], in0=CHN[0:128, 0:128], scalar1=-1.0, scalar2=None, op0=ALU.mult)
                    v.tensor_scalar(out=nbT_[:], in0=CHN[0:128, 128:256], scalar1=-1.0, scalar2=None, op0=ALU.mult)
                S.append(("vector", s19))

                def s20(t):
                    b.we(t, "vector")
                    t.matmul(CHN[0:128, 0:128], nbT_[:], nb_[:], start=True, stop=True)
                    b.inc(t.matmul(CHN[0:128, 128:256], nb_[:], nbT_[:], start=True, stop=True), "pbs")
                S.append(("tensor", s20))

                def s21(v):
                    v.wait_ge(b.sem("pbs"), b.val("pbs"))
                    v.tensor_copy(out=n2b_[:], in_=CHN[0:128, 0:128])
                    v.tensor_copy(out=n2bT_[:], in_=CHN[0:128, 128:256])
                S.append(("vector", s21))

                def s22(t):
                    b.we(t, "vector")
                    b.inc(t.matmul(CHN[0:128, 128:256], n2b_[:], n2bT_[:], start=True, stop=True), "pbs")
                S.append(("tensor", s22))

                def s23(v):
                    v.wait_ge(b.sem("pbs"), b.val("pbs"))
                    v.tensor_copy(out=n4bT_[:], in_=CHN[0:128, 128:256])
                S.append(("vector", s23))

                for step, lmat in enumerate((tdTb_, nbT_, n2bT_, n4bT_)):
                    def sw1(t, lmat=lmat):
                        b.we(t, "vector")
                        b.inc(t.matmul(WPS[0:128, 0:512], lmat[:], ub_[:], start=True, stop=True), "puc")
                    S.append(("tensor", sw1))

                    def sw2(v, step=step):
                        v.wait_ge(b.sem("puc"), b.val("puc"))
                        if step == 0:
                            v.tensor_copy(out=ubf_[:], in_=WPS[0:128, 0:512])
                        else:
                            v.tensor_tensor(out=ubf_[:], in0=ubf_[:], in1=WPS[0:128, 0:512], op=ALU.add)
                        v.tensor_copy(out=ub_[:], in_=ubf_[:])
                    S.append(("vector", sw2))

                def s32(v):
                    v.tensor_tensor(out=qhT_[:, 0:CH], in0=qhT_[:, 0:CH], in1=eGb_[:], op=ALU.mult)
                    v.tensor_tensor(out=qhT_[:, CH:2 * CH], in0=qhT_[:, CH:2 * CH], in1=eGb_[:], op=ALU.mult)
                    v.tensor_tensor(out=w2b_[:], in0=kqb_[:], in1=Einc_[:], op=ALU.mult)
                S.append(("vector", s32))

                def s33(t):
                    b.we(t, "vector")
                    t.matmul(pF[0:128, 0:512], qhT_[:, 0:CH], Sbp[pi][:, 0:512], start=True, stop=False)
                    t.matmul(pF[0:128, 0:512], qhT_[:, CH:2 * CH], Sbp[pi][:, 512:1024], start=False, stop=False)
                    b.inc(t.matmul(pF[0:128, 0:512], w2b_[:], ub_[:], start=False, stop=True), "po")
                S.append(("tensor", s33))

                def s34(v):
                    v.wait_ge(b.sem("po"), b.val("po"))
                    v.tensor_copy(out=obx_[:], in_=pF[0:128, 0:512])
                    v.tensor_tensor(out=sqd_[:], in0=obx_[:], in1=obx_[:], op=ALU.mult)
                    v.tensor_reduce(out=rs__[:, 0:1], in_=sqd_[:], axis=AX, op=ALU.add)
                    v.tensor_scalar(out=rs__[:, 1:2], in0=rs__[:, 0:1], scalar1=1.0 / 512, scalar2=1e-5, op0=ALU.mult, op1=ALU.add)
                S.append(("vector", s34))

                def s35(sc):
                    b.we(sc, "vector")
                    sc.activation(out=rs__[:, 2:3], in_=rs__[:, 1:2], func=AF.Ln)
                    sc.activation(out=rs__[:, 3:4], in_=rs__[:, 2:3], func=AF.Exp, scale=-0.5)
                S.append(("scalar", s35))

                def s36(v):
                    b.we(v, "scalar")
                    v.tensor_scalar(out=obx_[:], in0=obx_[:], scalar1=rs__[:, 3:4], scalar2=None, op0=ALU.mult)
                    v.tensor_tensor(out=obx_[:], in0=obx_[:], in1=nwb[0:128, :], op=ALU.mult)
                    v.tensor_tensor(out=ogb_[:], in0=obx_[:], in1=gtq[pi][:, cc * 512:(cc + 1) * 512], op=ALU.mult)
                    v.tensor_tensor(out=ksc_[:, 0:CH], in0=khT_[:, 0:CH], in1=bce_[:, 512:640], op=ALU.mult)
                    v.tensor_tensor(out=ksc_[:, CH:2 * CH], in0=khT_[:, CH:2 * CH], in1=bce_[:, 512:640], op=ALU.mult)
                S.append(("vector", s36))

                def s37(t):
                    b.we(t, "vector")
                    inst = None
                    for i in range(4):
                        inst = t.transpose(pTb[0:128, i * 128:(i + 1) * 128], ogb_[:, i * 128:(i + 1) * 128], ident[:])
                    b.inc(inst, "pog")
                S.append(("tensor", s37))

                def s38(v):
                    v.wait_ge(b.sem("pog"), b.val("pog"))
                    b.w(v, f"sog{pi}")
                    v.tensor_copy(out=ogs[pi][:], in_=pTb[0:128, 0:512])
                S.append(("vector", s38))

                def s39(s):
                    s.wait_ge(b.sem("ch_vector"), b.chv("vector"))
                    b.dma(s, "sync", ogTd3[:, 4 * h:4 * h + 4, c0:c0 + CH],
                          ogs[pi][:].rearrange("p (e t) -> p e t", t=128), f"sog{pi}")
                S.append(("sync", s39))

                def s40(t):
                    b.we(t, "vector")
                    t.transpose(pTb[0:128, 0:128], ksc_[:, 0:CH], ident[:])
                    b.inc(t.transpose(pTb[0:128, 128:256], ksc_[:, CH:2 * CH], ident[:]), "pkt2")
                S.append(("tensor", s40))

                def s41(v):
                    v.wait_ge(b.sem("pkt2"), b.val("pkt2"))
                    v.tensor_copy(out=kscT_[:, 0:256], in_=pTb[0:128, 0:256])
                S.append(("vector", s41))

                def s42(t):
                    b.we(t, "vector")
                    b.inc(t.matmul(pA[:, 0:512], kscT_[:, 0:128], ub_[:], start=True, stop=True), "psu")
                    b.inc(t.matmul(pB_[:, 0:512], kscT_[:, 128:256], ub_[:], start=True, stop=True), "psu")
                S.append(("tensor", s42))

                def s43(v):
                    v.wait_ge(b.sem("psu"), b.val("psu"))
                    for kt, ps_ in ((0, pA), (1, pB_)):
                        ssl = Sst[:, (h * 2 + kt) * 512:(h * 2 + kt + 1) * 512]
                        v.tensor_scalar(out=ssl, in0=ssl, scalar1=bce_[:, 640:641], scalar2=None, op0=ALU.mult)
                        v.tensor_tensor(out=ssl, in0=ssl, in1=ps_[:, 0:512], op=ALU.add)
                        v.tensor_copy(out=Sbp[pi][:, kt * 512:(kt + 1) * 512], in_=ssl)
                S.append(("vector", s43))
                return S

            SA, SB = [], []
            for cc in range(4):
                SA += chunk_secs(h0p, 0, cc)
                SB += chunk_secs(h1p, 1, cc)
            b.sec(SA[0][0])(SA[0][1])
            for k in range(1, len(SA)):
                b.sec(SA[k][0])(SA[k][1])
                b.sec(SB[k - 1][0])(SB[k - 1][1])
            b.sec(SB[-1][0])(SB[-1][1])

        for tg in range(4):
            @b.sec("sync")
            def _(s, tg=tg):
                s.wait_ge(b.sem("sog0"), b.val("sog0"))
                s.wait_ge(b.sem("sog1"), b.val("sog1"))
                if b.val("pyq") > 0:
                    s.wait_ge(b.sem("pyq"), b.val("pyq"))
                b.dma(s, "sync", ybuf[:].rearrange("p (e t) -> p e t", t=128),
                      ogTd3[:, :, tg * 128:(tg + 1) * 128], "lyb")

            @b.sec("tensor")
            def _(t, tg=tg, q=q):
                b.we(t, "vector")
                t.wait_ge(b.sem("lyb"), b.val("lyb"))
                for ke in range(24):
                    sl = ke % 2

                    @b.sec("gpsimd")
                    def _(gp, ke=ke, sl=sl):
                        gp.wait_ge(b.sem(f"swo{sl}"), b.val(f"swo{sl}"))
                        if b.val("pyq") > 0:
                            gp.wait_ge(b.sem("pyq"), b.val("pyq"))
                        b.dma(gp, "gpsimd", wos[sl][:], wobf_d[ke * 128:(ke + 1) * 128, :], f"lwo{sl}")

                    t.wait_ge(b.sem(f"lwo{sl}"), b.val(f"lwo{sl}"))
                    for dch in range(4):
                        ps_ = (pC, pD, pE2, pF)[dch]
                        inst = t.matmul(ps_[:, 0:512], ybuf[:, ke * 128:(ke + 1) * 128],
                                        wos[sl][:, dch * 512:(dch + 1) * 512], start=(ke == 0), stop=(ke == 23))
                        if dch == 3:
                            b.inc(inst, "pyq")

            @b.sec("vector")
            def _(v, tg=tg):
                v.wait_ge(b.sem("pyq"), b.val("pyq"))
                b.w(v, "ss1")
                for dch in range(4):
                    ps_ = (pC, pD, pE2, pF)[dch]
                    v.tensor_copy(out=s1t[:, dch * 512:(dch + 1) * 512], in_=ps_[:, 0:512])

            @b.sec("sync")
            def _(s, tg=tg, q0=q0):
                s.wait_ge(b.sem("ch_vector"), b.chv("vector"))
                b.dma(s, "sync", s1_d[q0 + tg * 128:q0 + (tg + 1) * 128, :], s1t[:], "ss1")
                s.wait_ge(b.sem("ss1"), b.val("ss1"))

    @b.sec("sync")
    def _(s):
        s.wait_ge(b.sem("ss1"), b.val("ss1"))


def dram_l5(nc):
    return {
        "mgw": nc.dram_tensor("mgw", [2048, 16], F32, kind="ExternalInput").ap(),
        "mebias": nc.dram_tensor("mebias", [128, 16], F32, kind="ExternalInput").ap(),
        "msgw": nc.dram_tensor("msgw", [2048, 1024], F32, kind="ExternalInput").ap(),
        "msuw": nc.dram_tensor("msuw", [2048, 1024], F32, kind="ExternalInput").ap(),
        "msdw": nc.dram_tensor("msdw", [1024, 2048], F32, kind="ExternalInput").ap(),
        "megu": nc.dram_tensor("megu", [16, 2048, 2048], F32, kind="ExternalInput").ap(),
        "medn": nc.dram_tensor("medn", [16, 1024, 2048], F32, kind="ExternalInput").ap(),
        "out": nc.dram_tensor("out", [T, D], F32, kind="ExternalOutput").ap(),
    }


def build_launch5():
    b = B()
    nc = b.nc
    Dd = dram_l5(nc)
    Dd["xnTi"] = nc.dram_tensor("xnTi", [128, ND * T], BF16, kind="ExternalInput").ap()
    Dd["x1i"] = nc.dram_tensor("x1i", [T, D], F32, kind="ExternalInput").ap()
    Dd["s1i"] = nc.dram_tensor("s1i", [T, D], BF16, kind="ExternalInput").ap()
    b.start()
    emit_phase5(b, Dd)
    b.close()
    return b


def emit_phase5(b, Dd):
    nc = b.nc
    xnT_d = Dd["xnTi"]; x1_d = Dd["x1i"]; s1_d = Dd["s1i"]
    gw_d = Dd["mgw"]; eb_d = Dd["mebias"]; sg_d = Dd["msgw"]; su_d = Dd["msuw"]
    sd_d = Dd["msdw"]; egu_d = Dd["megu"]; edn_d = Dd["medn"]; out_d = Dd["out"]

    xnTh = b.sb("xnTh", [128, ND * 1024], BF16)
    acc = b.sb("acc", [128, 8 * 2048], F32)
    wcol = [b.sb(f"wcol{i}", [128, ND * 128], BF16) for i in range(2)]
    dnb = b.sb("dnb", [128, 8 * 2048], BF16)
    itb = b.sb("itb", [128, 8 * 1024], BF16)
    uf_ = b.sb("uf_", [128, 1024], F32)
    gw_sb = b.sb("gw_sb", [128, ND * 16], BF16)
    ebb = b.sb("ebb", [128, 16], F32)
    scs = b.sb("scs_sb", [128, 8 * 16], F32)
    comb = b.sb("comb_sb", [128, 8 * 16], F32)
    rstat = b.sb("rstat", [128, 16], F32)
    xt = b.sb("xt", [128, D], F32)
    s1t = b.sb("s1t", [128, D], BF16)
    ot = b.sb("ot", [128, D], F32)

    pg = [b.ps(f"pg{i}", [128, 512], F32) for i in range(4)]
    pd = [b.ps(f"pd{i}", [128, 512], F32) for i in range(2)]
    pgmark = [0] * 4
    rtmark = [0] * 2
    pdmark = [0] * 2
    wcmark = [0, 0]
    dnmark = [0]

    @b.sec("gpsimd")
    def _(gp):
        b.dma(gp, "gpsimd", gw_sb[:].rearrange("p (k e) -> p k e", e=16),
              gw_d.rearrange("(k p) e -> p k e", p=128), "lgw")

    @b.sec("sync")
    def _(s):
        b.dma(s, "sync", ebb[:], eb_d, "leb")

    xnT3 = xnT_d.rearrange("p (k t) -> p k t", t=T)

    for half in range(2):
        th0 = half * 1024

        @b.sec("sync")
        def _(s, th0=th0):
            b.w(s, "pgu")
            b.w(s, "prt")
            b.dma(s, "sync", xnTh[:].rearrange("p (k t) -> p k t", t=1024),
                  xnT3[:, :, th0:th0 + 1024], "lxh")

        for tb in range(8):
            @b.sec("tensor")
            def _(t, tb=tb):
                t.wait_ge(b.sem("lxh"), b.val("lxh"))
                t.wait_ge(b.sem("lgw"), b.val("lgw"))
                if rtmark[tb % 2] > 0:
                    t.wait_ge(b.sem("ch_scalar"), rtmark[tb % 2])
                b.we(t, "vector") if tb < 2 else None
                last = None
                for kd in range(ND):
                    last = t.matmul(pg[tb % 2][:, 0:16], xnTh[:, kd * 1024 + tb * 128:kd * 1024 + (tb + 1) * 128],
                                    gw_sb[:, kd * 16:(kd + 1) * 16], start=(kd == 0), stop=(kd == ND - 1))
                b.inc(last, "prt")

            @b.sec("scalar")
            def _(sc, tb=tb):
                sc.wait_ge(b.sem("prt"), b.val("prt"))
                sc.wait_ge(b.sem("leb"), b.val("leb"))
                sc.activation(out=scs[:, tb * 16:(tb + 1) * 16], in_=pg[tb % 2][:, 0:16], func=AF.Sigmoid)
                rtmark[tb % 2] = b.chv("scalar")

            @b.sec("vector")
            def _(v, tb=tb):
                b.we(v, "scalar")
                sl = scs[:, tb * 16:(tb + 1) * 16]
                cb = comb[:, tb * 16:(tb + 1) * 16]
                v.tensor_tensor(out=cb, in0=sl, in1=ebb[:], op=ALU.add)
                v.tensor_reduce(out=rstat[:, 0:1], in_=cb, axis=AX, op=ALU.max)
                v.tensor_scalar(out=rstat[:, 4:5], in0=rstat[:, 0:1], scalar1=None, scalar2=None, op0=ALU.bypass) if False else None
                v.tensor_scalar(out=uf_[:, 0:16], in0=cb, scalar1=rstat[:, 0:1], scalar2=None, op0=ALU.is_ge)
                v.tensor_tensor(out=uf_[:, 16:32], in0=sl, in1=uf_[:, 0:16], op=ALU.mult)
                v.tensor_reduce(out=rstat[:, 1:2], in_=uf_[:, 16:32], axis=AX, op=ALU.add)
                v.scalar_tensor_tensor(out=cb, in0=uf_[:, 0:16], scalar=-1e30, in1=cb, op0=ALU.mult, op1=ALU.add)
                v.tensor_reduce(out=rstat[:, 2:3], in_=cb, axis=AX, op=ALU.max)
                v.tensor_scalar(out=uf_[:, 32:48], in0=cb, scalar1=rstat[:, 2:3], scalar2=None, op0=ALU.is_ge)
                v.tensor_tensor(out=uf_[:, 48:64], in0=sl, in1=uf_[:, 32:48], op=ALU.mult)
                v.tensor_reduce(out=rstat[:, 3:4], in_=uf_[:, 48:64], axis=AX, op=ALU.add)
                v.tensor_tensor(out=rstat[:, 4:5], in0=rstat[:, 1:2], in1=rstat[:, 3:4], op=ALU.add)
                v.tensor_scalar(out=rstat[:, 4:5], in0=rstat[:, 4:5], scalar1=1e-20, scalar2=None, op0=ALU.add)
                v.reciprocal(out=rstat[:, 5:6], in_=rstat[:, 4:5])
                v.tensor_scalar(out=rstat[:, 6:7], in0=rstat[:, 1:2], scalar1=rstat[:, 5:6], scalar2=None, op0=ALU.mult)
                v.tensor_scalar(out=rstat[:, 7:8], in0=rstat[:, 3:4], scalar1=rstat[:, 5:6], scalar2=None, op0=ALU.mult)
                v.tensor_scalar(out=uf_[:, 0:16], in0=uf_[:, 0:16], scalar1=rstat[:, 6:7], scalar2=None, op0=ALU.mult)
                v.scalar_tensor_tensor(out=cb, in0=uf_[:, 32:48], scalar=rstat[:, 7:8], in1=uf_[:, 0:16], op0=ALU.mult, op1=ALU.add)

        @b.sec("tensor")
        def _(t):
            b.we(t, "scalar")
        for e in range(17):
            shared = (e == 16)
            for nt in range(16):
                sl = nt % 2
                @b.sec("gpsimd")
                def _(gp, e=e, nt=nt, sl=sl, shared=shared):
                    if wcmark[sl] > 0:
                        gp.wait_ge(b.sem("pgu"), wcmark[sl])
                    if shared:
                        src = (sg_d if nt < 8 else su_d).rearrange("(k p) c -> p k c", p=128)[:, :, (nt % 8) * 128:(nt % 8 + 1) * 128]
                    else:
                        src = egu_d[e].rearrange("(k p) c -> p k c", p=128)[:, :, nt * 128:(nt + 1) * 128]
                    b.dma(gp, "gpsimd", wcol[sl][:].rearrange("p (k c) -> p k c", c=128), src, f"lwc{sl}")

                for tc in range(2):
                    @b.sec("tensor")
                    def _(t, nt=nt, tc=tc, sl=sl):
                        t.wait_ge(b.sem(f"lwc{sl}"), b.val(f"lwc{sl}"))
                        pidx = (nt * 2 + tc) % 4
                        if pgmark[pidx] > 0:
                            t.wait_ge(b.sem("ch_vector"), pgmark[pidx])
                        last = None
                        for kd in range(ND):
                            last = t.matmul(pg[pidx][:, 0:512], wcol[sl][:, kd * 128:(kd + 1) * 128],
                                            xnTh[:, kd * 1024 + tc * 512:kd * 1024 + (tc + 1) * 512],
                                            start=(kd == 0), stop=(kd == ND - 1))
                        b.inc(last, "pgu")

                    @b.sec("vector")
                    def _(v, nt=nt, tc=tc, shared=shared):
                        v.wait_ge(b.sem("pgu"), b.val("pgu"))
                        b.we(v, "scalar")
                        ps_ = pg[(nt * 2 + tc) % 4]
                        if nt < 8:
                            v.tensor_copy(out=uf_[:, 0:512], in_=ps_[:, 0:512])
                        else:
                            v.tensor_tensor(out=itb[:, (nt - 8) * 1024 + tc * 512:(nt - 8) * 1024 + (tc + 1) * 512],
                                            in0=itb[:, (nt - 8) * 1024 + tc * 512:(nt - 8) * 1024 + (tc + 1) * 512],
                                            in1=ps_[:, 0:512], op=ALU.mult)
                        pgmark[(nt * 2 + tc) % 4] = b.chv("vector")

                    if nt < 8:
                        @b.sec("scalar")
                        def _(sc, nt=nt, tc=tc):
                            b.we(sc, "vector")
                            sc.activation(out=itb[:, nt * 1024 + tc * 512:nt * 1024 + (tc + 1) * 512],
                                          in_=uf_[:, 0:512], func=AF.Silu)
                wcmark[sl] = b.val("pgu")

            @b.sec("gpsimd")
            def _(gp, e=e, shared=shared):
                if dnmark[0] > 0:
                    gp.wait_ge(b.sem("pdn"), dnmark[0])
                src = sd_d if shared else edn_d[e]
                b.dma(gp, "gpsimd", dnb[:].rearrange("p (k c) -> p k c", c=2048),
                      src.rearrange("(k p) c -> p k c", p=128), "ldn")

            for tb in range(8):
                for dch in range(4):
                    @b.sec("tensor")
                    def _(t, tb=tb, dch=dch):
                        t.wait_ge(b.sem("ldn"), b.val("ldn"))
                        if tb == 0 and dch == 0:
                            b.we(t, "scalar")
                            b.we(t, "vector")
                        pidx = (tb * 4 + dch) % 2
                        if pdmark[pidx] > 0:
                            t.wait_ge(b.sem("ch_vector"), pdmark[pidx])
                        last = None
                        for kf in range(8):
                            last = t.matmul(pd[(tb * 4 + dch) % 2][:, 0:512],
                                            itb[:, kf * 1024 + tb * 128:kf * 1024 + (tb + 1) * 128],
                                            dnb[:, kf * 2048 + dch * 512:kf * 2048 + (dch + 1) * 512],
                                            start=(kf == 0), stop=(kf == 7))
                        b.inc(last, "pdn")

                    @b.sec("vector")
                    def _(v, tb=tb, dch=dch, e=e, shared=shared):
                        v.wait_ge(b.sem("pdn"), b.val("pdn"))
                        ps_ = pd[(tb * 4 + dch) % 2]
                        pdmark[(tb * 4 + dch) % 2] = b.chv("vector") + 1
                        asl = acc[:, tb * 2048 + dch * 512:tb * 2048 + (dch + 1) * 512]
                        if e == 0:
                            v.tensor_scalar(out=asl, in0=ps_[:, 0:512], scalar1=comb[:, tb * 16:tb * 16 + 1], scalar2=None, op0=ALU.mult)
                        elif shared:
                            v.tensor_tensor(out=asl, in0=asl, in1=ps_[:, 0:512], op=ALU.add)
                        else:
                            v.scalar_tensor_tensor(out=asl, in0=ps_[:, 0:512], scalar=comb[:, tb * 16 + e:tb * 16 + e + 1],
                                                   in1=asl, op0=ALU.mult, op1=ALU.add)
            dnmark[0] = b.val("pdn")

        for tb in range(8):
            grow = th0 + tb * 128

            @b.sec("sync")
            def _(s, grow=grow):
                b.we(s, "vector")
                b.dma(s, "sync", xt[:], x1_d[grow:grow + 128, :], "lxt")
                b.dma(s, "sync", s1t[:], s1_d[grow:grow + 128, :], "ls1")

            @b.sec("scalar")
            def _(sc, tb=tb):
                b.we(sc, "vector")
                sc.activation(out=s1t[:].bitcast(BF16) if False else xt[:], in_=acc[:, tb * 2048:(tb + 1) * 2048], func=AF.Silu) if False else None
                sc.wait_ge(b.sem("lxt"), b.val("lxt"))

            @b.sec("vector")
            def _(v, tb=tb):
                v.wait_ge(b.sem("lxt"), b.val("lxt"))
                v.wait_ge(b.sem("ls1"), b.val("ls1"))
                b.w(v, "sout")

            @b.sec("scalar")
            def _(sc, tb=tb):
                sc.activation(out=acc[:, tb * 2048:(tb + 1) * 2048], in_=acc[:, tb * 2048:(tb + 1) * 2048], func=AF.Silu)
                b.inc(sc.nop(), "s2v") if False else None

            @b.sec("vector")
            def _(v, tb=tb):
                b.we(v, "scalar")
                v.tensor_tensor(out=ot[:], in0=s1t[:], in1=acc[:, tb * 2048:(tb + 1) * 2048], op=ALU.mult)
                v.tensor_tensor(out=ot[:], in0=ot[:], in1=xt[:], op=ALU.add)

            @b.sec("sync")
            def _(s, grow=grow):
                b.we(s, "vector")
                b.dma(s, "sync", out_d[grow:grow + 128, :], ot[:], "sout")
                s.wait_ge(b.sem("sout"), b.val("sout"))

    @b.sec("sync")
    def _(s):
        s.wait_ge(b.sem("sout"), b.val("sout"))


def build_fused():
    b = B()
    nc = b.nc
    D1 = dram_l1(nc, "Internal")
    D2 = dram_l2(nc, "Internal", with_ident=False)
    D2["ident"] = D1["ident"]
    D4 = dram_l4(nc, "Internal")
    D4["ident"] = D1["ident"]
    D5 = dram_l5(nc)
    x1_t = nc.dram_tensor("x1", [T, D], F32, kind="Internal").ap()
    xnT1_t = nc.dram_tensor("xnT1", [128, ND * T], BF16, kind="Internal").ap()
    P3 = {"x": D1["x"], "xnTi": D1["xnT"], "s0i": D1["s0"],
          "WgT": D2["WgT"], "WuT": D2["WuT"], "Wdd": D2["Wdd"],
          "ident": D1["ident"], "x1": x1_t, "xnT1": xnT1_t}
    P4 = dict(D4)
    P4["xnTi"] = xnT1_t
    P5 = dict(D5)
    P5.update({"xnTi": xnT1_t, "x1i": x1_t, "s1i": D4["s1"]})
    b.start()
    for emit, Dd in ((emit_phase1, D1), (emit_phase2, D2), (emit_phase3, P3),
                     (emit_phase4, P4), (emit_phase5, P5)):
        b.push_scope()
        emit(b, Dd)
        b.pop_scope()
    b.close()
    return b


def _fused_inmaps(inputs):
    consts = make_consts()
    cpk = np.zeros((128, CPK_W), dtype=np.float32)
    cpk[:, 0:2048] = consts["ropec"].reshape(16, 128, 128).transpose(1, 0, 2).reshape(128, 2048)
    cpk[:, 2048:4096] = consts["ropes"].reshape(16, 128, 128).transpose(1, 0, 2).reshape(128, 2048)
    cpk[:, 4096:4224] = consts["causal"]
    cpk[:, 4224:5376] = consts["para"]
    cpk[:, 5376:5504] = np.broadcast_to(inputs["L0_attn_gamma_q"], (128, 128)).astype(np.float32)
    cpk[:, 5504:5632] = np.broadcast_to(inputs["L0_attn_gamma_k"], (128, 128)).astype(np.float32)
    ebb = np.ascontiguousarray(np.broadcast_to(inputs["L1_ffn_expert_bias"], (128, 16)).astype(np.float32))
    im = []
    for c in range(NCORES):
        im.append({
            "x": np.ascontiguousarray(inputs["L0_attn_x"][c]),
            "wq": inputs["L0_attn_wq"], "wk": inputs["L0_attn_wk"],
            "wv": inputs["L0_attn_wv"], "wo": inputs["L0_attn_wo"],
            "cpk": cpk, "ident": consts["ident"], "identf": consts["identf"],
            "cp2": consts["cp2"],
            "gWp": inputs["L0_ffn_gate_W_packed"], "gSc": inputs["L0_ffn_gate_scales"],
            "uWp": inputs["L0_ffn_up_W_packed"], "uSc": inputs["L0_ffn_up_scales"],
            "dWp": inputs["L0_ffn_down_W_packed"], "dSc": inputs["L0_ffn_down_scales"],
            "l1wq": inputs["L1_attn_Wq"], "l1wk": inputs["L1_attn_Wk"], "l1wv": inputs["L1_attn_Wv"],
            "l1wa": inputs["L1_attn_Wa"], "l1wb": inputs["L1_attn_Wb"], "l1wg": inputs["L1_attn_Wg"],
            "l1wo": inputs["L1_attn_Wo"],
            "cwq": inputs["L1_attn_conv_wq"], "cwk": inputs["L1_attn_conv_wk"], "cwv": inputs["L1_attn_conv_wv"],
            "alog": np.ascontiguousarray(inputs["L1_attn_A_log"].reshape(6, 1)),
            "dtb": np.ascontiguousarray(inputs["L1_attn_dt_bias"].reshape(6, 1)),
            "nrmw": np.ascontiguousarray(inputs["L1_attn_norm_weight"].reshape(1, 512)),
            "mgw": inputs["L1_ffn_gate_w"], "mebias": ebb,
            "msgw": inputs["L1_ffn_shared_gate_w"], "msuw": inputs["L1_ffn_shared_up_w"],
            "msdw": inputs["L1_ffn_shared_down_w"],
            "megu": inputs["L1_ffn_experts_gate_up"], "medn": inputs["L1_ffn_experts_down"],
        })
    return im


def kernel(**inputs):
    b = build_fused()
    res = run_bass_kernel_spmd(b.nc, _fused_inmaps(inputs), core_ids=list(range(NCORES)))
    out = np.stack([np.asarray(res.results[c]["out"]) for c in range(NCORES)], axis=0)
    return out


def kernel_multilaunch(**inputs):
    consts = make_consts()
    ncores = NCORES
    cores = list(range(ncores))

    b1 = build_launch1()
    r1 = run_bass_kernel_spmd(b1.nc, _launch1_inmaps(inputs), core_ids=cores).results

    b2 = build_launch2()
    r2 = run_bass_kernel_spmd(b2.nc, _launch2_inmaps(inputs), core_ids=cores).results

    b3 = build_launch3()
    im3 = []
    for c in cores:
        im3.append({
            "x": np.ascontiguousarray(inputs["L0_attn_x"][c]),
            "xnTi": np.asarray(r1[c]["xnT"]),
            "s0i": np.asarray(r1[c]["s0"]),
            "WgT": np.asarray(r2[c]["WgT"]), "WuT": np.asarray(r2[c]["WuT"]),
            "Wdd": np.asarray(r2[c]["Wdd"]),
            "ident": consts["ident"],
        })
    r3 = run_bass_kernel_spmd(b3.nc, im3, core_ids=cores).results

    b4 = build_launch4()
    im4 = []
    for c in cores:
        im4.append({
            "xnTi": np.asarray(r3[c]["xnT1"]),
            "l1wq": inputs["L1_attn_Wq"], "l1wk": inputs["L1_attn_Wk"], "l1wv": inputs["L1_attn_Wv"],
            "l1wa": inputs["L1_attn_Wa"], "l1wb": inputs["L1_attn_Wb"], "l1wg": inputs["L1_attn_Wg"],
            "l1wo": inputs["L1_attn_Wo"],
            "cwq": inputs["L1_attn_conv_wq"], "cwk": inputs["L1_attn_conv_wk"], "cwv": inputs["L1_attn_conv_wv"],
            "alog": np.ascontiguousarray(inputs["L1_attn_A_log"].reshape(6, 1)),
            "dtb": np.ascontiguousarray(inputs["L1_attn_dt_bias"].reshape(6, 1)),
            "nrmw": np.ascontiguousarray(inputs["L1_attn_norm_weight"].reshape(1, 512)),
            "cp2": consts["cp2"], "ident": consts["ident"], "identf": consts["identf"],
        })
    r4 = run_bass_kernel_spmd(b4.nc, im4, core_ids=cores).results

    b5 = build_launch5()
    ebb = np.ascontiguousarray(np.broadcast_to(inputs["L1_ffn_expert_bias"], (128, 16)).astype(np.float32))
    im5 = []
    for c in cores:
        im5.append({
            "xnTi": np.asarray(r3[c]["xnT1"]),
            "x1i": np.asarray(r3[c]["x1"]),
            "s1i": np.asarray(r4[c]["s1"]),
            "mgw": inputs["L1_ffn_gate_w"], "mebias": ebb,
            "msgw": inputs["L1_ffn_shared_gate_w"], "msuw": inputs["L1_ffn_shared_up_w"],
            "msdw": inputs["L1_ffn_shared_down_w"],
            "megu": inputs["L1_ffn_experts_gate_up"], "medn": inputs["L1_ffn_experts_down"],
        })
    r5 = run_bass_kernel_spmd(b5.nc, im5, core_ids=cores).results

    out = np.stack([np.asarray(r5[c]["out"]) for c in cores], axis=0)
    return out
